# version 10
# baseline (speedup 1.0000x reference)
import numpy as np
import ml_dtypes
import concourse.bass as bass
import concourse.mybir as mybir
from concourse.bass_utils import run_bass_kernel_spmd

F32 = mybir.dt.float32
BF16 = mybir.dt.bfloat16
AF = mybir.ActivationFunctionType
ALU = mybir.AluOpType
AXX = mybir.AxisListType.X
MAGIC = float(np.float32(1.5 * 2.0 ** 23))
SQ128 = float(1.0 / np.sqrt(128.0))
NH = 16
D = 2048
DI = 8192
NT = 8
TOK = NT * 128
EPS_LN = 1e-5
EPS_RMS = 1e-6
EPS_Q = 1e-5
ISQ2048 = float(1.0 / np.sqrt(2048.0))
ISQ8192 = float(1.0 / np.sqrt(8192.0))
TAPS = False
DBG0 = False

QPRE_O, RTMP_O, RTMP2_O = 0, 2048, 4096
QS_O, VST0_O, VST1_O = 6144, 7168, 8192
XB_O = 16384
KH_O = (0, 8192)
VH_O = (4096, 12288)
QH_O = (20480, 20992)
OST_O = (21504, 22016)
FW_O = 0
Z_O = 4352
TMPF_O = 20992
AGB_O = 21504
MAXT_O = 22016
GT_O = 22272
SG_O = 22528
HFULL_O, HT_O, CB_O = 0, 4096, 8192
GATES_O, UJ_O, VJ_O = 16384, 18432, 20480
UGJ_O, VGJ_O, GO_O, GOTJ_O = 0, 2048, 4096, 6144
CVS_O = 2048
TOUT_O = 16384
G5S_O = 20480
XQ_O = 0
GOT2_O = (2048, 4096)


def _rope_perm():
    p = np.zeros(D, dtype=np.int64)
    for h in range(NH):
        base = h * 128
        for i in range(64):
            p[base + i] = base + 2 * i
            p[base + 64 + i] = base + 2 * i + 1
    return p


def _hadamard(n):
    h = np.array([[1.0]], dtype=np.float32)
    while h.shape[0] < n:
        h = np.block([[h, h], [h, -h]])
    return h.astype(np.float32)


def _host_prep(inputs):
    x = np.asarray(inputs["L0_attn_x"])
    perm = _rope_perm()
    wq = np.ascontiguousarray(np.asarray(inputs["L0_attn_Wq"])[:, perm])
    wk = np.ascontiguousarray(np.asarray(inputs["L0_attn_Wk"])[:, perm])
    bq = np.ascontiguousarray(np.asarray(inputs["L0_attn_bq"])[perm])
    bk = np.ascontiguousarray(np.asarray(inputs["L0_attn_bk"])[perm])
    bq_cols = np.ascontiguousarray(bq.reshape(16, 128).T)
    bk_cols = np.ascontiguousarray(bk.reshape(16, 128).T)
    wgT = np.ascontiguousarray(np.asarray(inputs["L0_ffn_w_gate"]).T)
    wupT = np.ascontiguousarray(np.asarray(inputs["L0_ffn_w_up"]).T)
    wdT = np.ascontiguousarray(np.asarray(inputs["L0_ffn_w_down"]).T)
    wu1T = np.ascontiguousarray(np.asarray(inputs["L1_attn_Wu"]).T)
    wv1T = np.ascontiguousarray(np.asarray(inputs["L1_attn_Wv"]).T)
    wg1T = np.ascontiguousarray(np.asarray(inputs["L1_attn_Wg1"]).T)
    wg2T = np.ascontiguousarray(np.asarray(inputs["L1_attn_Wg2"]).T)
    convw = np.asarray(inputs["L1_attn_conv_w"]).reshape(D, 3)
    conv_rows = np.ascontiguousarray(np.stack(
        [convw[:, 0], convw[:, 1], convw[:, 2],
         np.asarray(inputs["L1_attn_conv_b"])], 0)).astype(np.float32)
    w1T = np.ascontiguousarray(np.asarray(inputs["L1_ffn_w1"]).T)
    w2T = np.ascontiguousarray(np.asarray(inputs["L1_ffn_w2"]).T)
    b1_cols = np.ascontiguousarray(
        np.asarray(inputs["L1_ffn_b1"]).reshape(64, 128).T)
    bias_rows = np.zeros((8, D), dtype=np.float32)
    bias_rows[0] = np.asarray(inputs["L0_attn_bv"])
    bias_rows[1] = np.asarray(inputs["L0_attn_bo"])
    bias_rows[2] = np.asarray(inputs["L1_attn_bu"])
    bias_rows[3] = np.asarray(inputs["L1_attn_bv"])
    bias_rows[4] = np.asarray(inputs["L1_attn_bg2"])
    bias_rows[5] = np.asarray(inputs["L1_ffn_b2"])
    bias_rows[6, :512] = np.asarray(inputs["L1_attn_bg1"])
    bf = ml_dtypes.bfloat16
    h128 = _hadamard(128).astype(bf)
    ident = np.eye(128, dtype=np.float32).astype(bf)
    identf = np.eye(128, dtype=np.float32)
    tri = (np.arange(128)[:, None] <= np.arange(128)[None, :]).astype(np.float32)
    shd = np.eye(128, k=1, dtype=np.float32).astype(bf)
    shu = np.eye(128, k=-1, dtype=np.float32).astype(bf)
    swp = np.block([[np.zeros((64, 64)), np.eye(64)],
                    [np.eye(64), np.zeros((64, 64))]]).astype(np.float32).astype(bf)
    inv_freq = 1.0 / (10000.0 ** (np.arange(0, 128, 2, dtype=np.float64) / 128))

    in_maps, out_index = [], []
    for core in range(8):
        b, g = core // 4, core % 4
        tok = np.concatenate([np.arange(128) + (4 * j + g) * 128 for j in range(NT)])
        out_index.append((b, tok))
        ang = tok.astype(np.float64)[None, :] * inv_freq[:, None]
        ropec = np.concatenate([np.cos(ang)] * 2, 0).astype(np.float32)
        ropes = np.concatenate([-np.sin(ang), np.sin(ang)], 0).astype(np.float32)
        am = np.zeros((128, 512), dtype=np.float32)
        for m in range(4):
            if m < g:
                am[:, m * 128:(m + 1) * 128] = 1.0
            elif m == g:
                am[:, m * 128:(m + 1) * 128] = tri
        wsel = np.zeros((96, 8), dtype=np.float32)
        hselp = np.zeros((96, 1024), dtype=np.float32)
        hseln = np.zeros((96, 1024), dtype=np.float32)
        for r_ in range(4):
            for j_ in range(8):
                gt_row = 4 * j_ + r_
                for jl in range(8):
                    if gt_row < 4 * jl + g:
                        wsel[r_ * 24 + j_, jl] = 1.0
        wselr = np.repeat(wsel, 128, axis=1)
        def gr(gt, which):
            return (gt % 4) * 24 + 8 + (gt // 4) * 2 + which
        for jl in range(8):
            gt = 4 * jl + g
            if gt > 0:
                hselp[gr(gt - 1, 1), jl * 128 + 0] = 1.0
            if gt < 31:
                hseln[gr(gt + 1, 0), jl * 128 + 127] = 1.0
        sh = lambda w: np.ascontiguousarray(
            w[core * (w.shape[0] // 8):(core + 1) * (w.shape[0] // 8)])
        in_maps.append(dict(
            x=np.ascontiguousarray(x[b][tok]), wq=wq, wk=wk,
            wv=np.ascontiguousarray(np.asarray(inputs["L0_attn_Wv"])),
            wo=np.ascontiguousarray(np.asarray(inputs["L0_attn_Wo"])),
            bq_cols=bq_cols, bk_cols=bk_cols, b1_cols=b1_cols, bias_rows=bias_rows,
            wg_sh=sh(wgT), wup_sh=sh(wupT), wd_sh=sh(wdT),
            w1_sh=sh(w1T), w2_sh=sh(w2T),
            wu1T=wu1T, wv1T=wv1T, wg1T=wg1T, wg2T=wg2T, conv_rows=conv_rows,
            ropec=ropec, ropes=ropes, amask=am.astype(bf),
            wselr=wselr.astype(bf), hselp=hselp.astype(bf), hseln=hseln.astype(bf),
            h128=h128, ident=ident, identf=identf, tri=tri.astype(bf),
            shd=shd, shu=shu, swp=swp,
        ))
    return in_maps, out_index



class Ctr:
    __slots__ = ("sem", "val", "name")

    def __init__(self, nc, name):
        self.sem = nc.alloc_semaphore(name)
        self.val = 0
        self.name = name


class Plan:
    ENGINES = ("tensor", "vector", "scalar", "gpsimd", "sync")

    def __init__(self, nc):
        self.nc = nc
        self.ops = {e: [] for e in self.ENGINES}
        self._ctrs = {}
        self._wm = {e: {} for e in self.ENGINES}
        self.n = 0

    def ctr(self, name):
        if name not in self._ctrs:
            self._ctrs[name] = Ctr(self.nc, name)
        return self._ctrs[name]

    def _wl(self, eng, waits):
        wl = []
        for w in waits:
            c, v = (w if isinstance(w, tuple) else (w, None))
            v = c.val if v is None else v
            assert v <= c.val, f"future wait {c.name}: {v} > {c.val}"
            if v <= 0 or self._wm[eng].get(c.name, 0) >= v:
                continue
            self._wm[eng][c.name] = v
            wl.append((c.sem, v))
        return wl

    def raw(self, eng, emit, waits=(), incs=()):
        wl = self._wl(eng, waits)
        il = []
        first_val = None
        for c, n in incs:
            c.val += n
            if first_val is None:
                first_val = c.val
            il.append((c.sem, n))
        self.n += 1 + len(wl) + 2 * max(0, len(il) - 1)

        def fn(e, emit=emit, wl=wl, il=il, fv=first_val):
            for sem, v in wl:
                e.wait_ge(sem, v)
            inst = emit(e)
            if il:
                inst.then_inc(*il[0])
                if len(il) > 1:
                    e.wait_ge(il[0][0], fv)
                    for sem, n in il[1:]:
                        nop = (e.engine_nop() if hasattr(e, "engine_nop")
                               else e.nop())
                        nop.then_inc(sem, n)
        self.ops[eng].append(fn)

    def wait(self, eng, *waits):
        wl = self._wl(eng, waits)
        if wl:
            self.n += len(wl)

            def fn(e, wl=wl):
                for sem, v in wl:
                    e.wait_ge(sem, v)
            self.ops[eng].append(fn)

    def run(self):
        with self.nc.Block() as block:
            @block.tensor
            def _(t):
                for fn in self.ops["tensor"]:
                    fn(t)

            @block.vector
            def _(v):
                for fn in self.ops["vector"]:
                    fn(v)

            @block.scalar
            def _(sc):
                for fn in self.ops["scalar"]:
                    fn(sc)

            @block.gpsimd
            def _(g):
                for fn in self.ops["gpsimd"]:
                    fn(g)

            @block.sync
            def _(s):
                for fn in self.ops["sync"]:
                    fn(s)


def A_(handle, off, *dims):
    return bass.AP(handle, off, [list(d) for d in dims])


def _inc(ctr, n=1):
    return (ctr, n)


class MK:
    def __init__(self, nc):
        self.nc = nc
        self.P = Plan(nc)

    def c(self, name):
        return self.P.ctr(name)

    def dma(self, eng, dst, src, inc=None, waits=(), extra=()):
        incs = ([(inc, 16)] if inc is not None else []) + list(extra)
        self.P.raw(eng, lambda e, d=dst, s=src: e.dma_start(out=d, in_=s),
                   waits=waits, incs=incs)

    def mm(self, out, lhsT, rhs, start, stop, waits=(), incs=()):
        self.P.raw("tensor",
                   lambda t, o=out, l=lhsT, r=rhs, a=start, b=stop:
                   t.matmul(o, l, r, start=a, stop=b), waits=waits, incs=incs)

    def tr(self, out, in_, waits=(), incs=()):
        ident = self.CONST[:, 0:128]
        self.P.raw("tensor", lambda t, o=out, i=in_, d=ident:
                   t.transpose(o, i, d), waits=waits, incs=incs)

    def tt(self, eng, out, a, b, op, waits=(), incs=(), accum=None):
        if accum is None:
            self.P.raw(eng, lambda e, o=out, x=a, y=b, p=op:
                       e.tensor_tensor(o, x, y, p), waits=waits, incs=incs)
        else:
            self.P.raw(eng, lambda e, o=out, x=a, y=b, p=op, ac=accum:
                       e.scalar_tensor_tensor(o, x, 1.0, y, ALU.mult, p,
                                              accum_out=ac),
                       waits=waits, incs=incs)

    def ts(self, eng, out, a, s1, s2, op0, op1=None, waits=(), incs=()):
        if s2 is None and op1 is None:
            self.P.raw(eng, lambda e, o=out, x=a, u=s1, p0=op0:
                       e.tensor_single_scalar(o, x, u, p0), waits=waits, incs=incs)
        else:
            self.P.raw(eng, lambda e, o=out, x=a, u=s1, w=s2, p0=op0, p1=op1:
                       e.tensor_scalar(o, x, u, w, p0, p1), waits=waits, incs=incs)

    def stt(self, eng, out, a, scal, b, op0, op1, waits=(), incs=()):
        self.P.raw(eng, lambda e, o=out, x=a, u=scal, y=b, p0=op0, p1=op1:
                   e.scalar_tensor_tensor(o, x, u, y, p0, p1),
                   waits=waits, incs=incs)

    def cp(self, eng, out, in_, waits=(), incs=()):
        if eng == "scalar":
            self.P.raw(eng, lambda e, o=out, i=in_: e.copy(o, i),
                       waits=waits, incs=incs)
        else:
            self.P.raw(eng, lambda e, o=out, i=in_: e.tensor_copy(o, i),
                       waits=waits, incs=incs)

    def act(self, out, in_, f, bias=0.0, scale=1.0, accum=None, waits=(), incs=()):
        self.P.raw("scalar", lambda e, o=out, i=in_, ff=f, b=bias, s=scale,
                   ac=accum: e.activation(o, i, ff, bias=b, scale=s,
                                          accum_out=ac), waits=waits, incs=incs)

    def red(self, out, in_, op, absval=False, waits=(), incs=()):
        self.P.raw("vector", lambda e, o=out, i=in_, p=op, av=absval:
                   e.tensor_reduce(o, i, AXX, p, apply_absolute_value=av),
                   waits=waits, incs=incs)

    def recip(self, out, in_, waits=(), incs=()):
        self.P.raw("vector", lambda e, o=out, i=in_: e.reciprocal(o, i),
                   waits=waits, incs=incs)

    def memset(self, t, v, waits=(), incs=()):
        self.P.raw("vector", lambda e, o=t, x=v: e.memset(o, x),
                   waits=waits, incs=incs)

    def coll(self, ins, outs, groups, waits=(), incs=()):
        self.P.raw("gpsimd", lambda e, i=ins, o=outs, g=groups:
                   e.collective_compute("AllGather", ALU.bypass,
                                        replica_groups=g, ins=[i], outs=[o]),
                   waits=waits, incs=incs)

    def declare(self):
        nc = self.nc
        DT = nc.dram_tensor

        def EI(name, shape, dt=F32):
            return DT(name, shape, dt, kind="ExternalInput")
        self.i_x = EI("x", [TOK, D])
        self.i_wq = EI("wq", [D, D]); self.i_wk = EI("wk", [D, D])
        self.i_wv = EI("wv", [D, D]); self.i_wo = EI("wo", [D, D])
        self.i_bq_cols = EI("bq_cols", [128, 16])
        self.i_bk_cols = EI("bk_cols", [128, 16])
        self.i_b1_cols = EI("b1_cols", [128, 64])
        self.i_bias_rows = EI("bias_rows", [8, D])
        self.i_wg_sh = EI("wg_sh", [256, DI]); self.i_wup_sh = EI("wup_sh", [256, DI])
        self.i_wd_sh = EI("wd_sh", [1024, D])
        self.i_w1_sh = EI("w1_sh", [256, DI]); self.i_w2_sh = EI("w2_sh", [1024, D])
        self.i_wu1T = EI("wu1T", [D, D]); self.i_wv1T = EI("wv1T", [D, D])
        self.i_wg1T = EI("wg1T", [D, 512]); self.i_wg2T = EI("wg2T", [512, D])
        self.i_conv_rows = EI("conv_rows", [4, D])
        self.i_ropec = EI("ropec", [128, TOK]); self.i_ropes = EI("ropes", [128, TOK])
        self.i_amask = EI("amask", [128, 512], BF16)
        self.i_wselr = EI("wselr", [96, 1024], BF16)
        self.i_hselp = EI("hselp", [96, 1024], BF16)
        self.i_hseln = EI("hseln", [96, 1024], BF16)
        self.i_h128 = EI("h128", [128, 128], BF16)
        self.i_ident = EI("ident", [128, 128], BF16)
        self.i_identf = EI("identf", [128, 128])
        self.i_tri = EI("tri", [128, 128], BF16)
        self.i_shd = EI("shd", [128, 128], BF16)
        self.i_shu = EI("shu", [128, 128], BF16)
        self.i_swp = EI("swp", [128, 128], BF16)
        self.o_out = DT("out", [TOK, D], F32, kind="ExternalOutput")
        if TAPS:
            self.o_taps = {i: DT(f"tap{i}", [TOK, D], F32, kind="ExternalOutput")
                           for i in (1, 2, 3)}
        self.dQ = DT("dQ", [D, TOK], BF16)
        self.dO = DT("dO", [D, TOK], BF16)
        self.dU = DT("dU", [TOK, D], BF16)
        self.dV = DT("dV", [TOK, D], BF16)
        self.dUG = DT("dUG", [TOK, D], BF16)
        self.dVG = DT("dVG", [TOK, D], BF16)
        self.dGOT = DT("dGOT", [D, TOK], BF16)
        self.g1i = DT("g1i", [1, 5], F32); self.g1o = DT("g1o", [8, 5], F32)
        self.g2gi = DT("g2gi", [256, DI], BF16)
        self.g2go = DT("g2go", [D, DI], BF16, addr_space="Shared")
        self.g2ui = DT("g2ui", [256, DI], BF16)
        self.g2uo = DT("g2uo", [D, DI], BF16, addr_space="Shared")
        self.g2di = DT("g2di", [1024, D], BF16)
        self.g2do = DT("g2do", [DI, D], BF16, addr_space="Shared")
        self.g3ai = DT("g3ai", [256, DI], BF16)
        self.g3ao = DT("g3ao", [D, DI], BF16, addr_space="Shared")
        self.g3bi = DT("g3bi", [1024, D], BF16)
        self.g3bo = DT("g3bo", [DI, D], BF16, addr_space="Shared")
        self.g4i = DT("g4i", [2, D * TOK], BF16)
        self.g4oK = DT("g4oK", [4, D * TOK], BF16)
        self.g4oV = DT("g4oV", [4, D * TOK], BF16)
        self.g5i = DT("g5i", [24, D], F32); self.g5o = DT("g5o", [96, D], F32)
        if DBG0:
            self.dbg_fw = DT("dbg_fw", [128, 4352], BF16)
            self.dbg_z = DT("dbg_z", [128, 16640], BF16)
            self.dbg_z2 = DT("dbg_z2", [128, 16640], BF16)
            self.dbg_arow = DT("dbg_arow", [1, 1792], F32)

        A = nc.alloc_sbuf_tensor
        self.X = A("X", [128, NT * D], F32)
        self.xT = A("xT", [128, 16 * TOK], BF16)
        self.AR1 = self.xT
        self.AR2 = A("AR2", [128, 24576], BF16)
        self.WB = [A(f"WB{i}", [128, 4096], BF16) for i in range(2)]
        self.WPST = A("WPST", [128, 4096], F32)
        self.WPQ = A("WPQ", [128, 2048], BF16)
        self.CONST = A("CONST", [128, 1280], BF16)
        self.CONSTF = A("CONSTF", [128, 104], F32)
        self.IDENTF = A("IDENTF", [128, 128], F32)
        self.BROW = A("BROW", [1, D], BF16)
        self.WSELR = A("WSELR", [96, 1024], BF16)
        self.HSELP = A("HSELP", [96, 1024], BF16)
        self.HSELN = A("HSELN", [96, 1024], BF16)
        self.ONES1 = A("ONES1", [1, 128], BF16)
        self.ONESK = A("ONESK", [128, 1], BF16)
        self.ONES1F = A("ONES1F", [1, 128], F32)
        self.ONESKF = A("ONESKF", [128, 1], F32)
        self.STAT = A("STAT", [128, 64], F32)
        self.AROW = A("AROW", [1, 1792], F32)
        self.G1S = A("G1S", [8, 24], F32)
        self.CONSTC = A("CONSTC", [128, 2], F32)
        self.pb = [nc.alloc_psum_tensor(f"pb{i}", [128, 512], F32) for i in range(5)]
        self.psb = nc.alloc_psum_tensor("ps", [128, 512], F32)
        self.pt = [nc.alloc_psum_tensor(f"pt{i}", [128, 1024], BF16)
                   for i in range(2)]
        assert nc.sbuf_bytes_remaining >= 0, nc.sbuf_bytes_remaining
        self.bk_rd = [self.c(f"bk{i}_rd") for i in range(6)]
        self.bk_fr = [self.c(f"bk{i}_fr") for i in range(6)]
        self.pt_rd = [self.c(f"pt{i}_rd") for i in range(2)]
        self.pt_fr = [self.c(f"pt{i}_fr") for i in range(2)]

    def bank(self, i):
        return (self.psb if i == 5 else self.pb[i]), self.bk_rd[i], self.bk_fr[i]

    def ar2b(self, off, n):
        return self.AR2[:, off:off + n]

    def ar2f(self, off, n_f32):
        return self.AR2[:, off:off + 2 * n_f32].bitcast(F32)

    def phase_load(self):
        ld = self.c("ld_init")
        for j in range(NT):
            self.dma("sync", self.X[:, j * D:(j + 1) * D],
                     self.i_x[j * 128:(j + 1) * 128, :], ld)
        self.dma("sync", self.CONST[:, 0:128], self.i_ident[:, :], ld)
        self.dma("sync", self.CONST[:, 128:256], self.i_h128[:, :], ld)
        self.dma("sync", self.CONST[:, 256:384], self.i_tri[:, :], ld)
        self.dma("sync", self.CONST[:, 384:896], self.i_amask[:, :], ld)
        self.dma("sync", self.CONST[:, 896:1024], self.i_shd[:, :], ld)
        self.dma("sync", self.CONST[:, 1024:1152], self.i_shu[:, :], ld)
        self.dma("sync", self.CONST[:, 1152:1280], self.i_swp[:, :], ld)
        self.dma("sync", self.IDENTF[:, :], self.i_identf[:, :], ld)
        ldg = self.c("ld_init_g")
        self.dma("gpsimd", self.ar2b(9216, 1024), self.i_ropec[:, :], ldg)
        self.dma("gpsimd", self.ar2b(10240, 1024), self.i_ropes[:, :], ldg)
        self.dma("sync", self.CONSTF[:, 0:16], self.i_bq_cols[:, :], ld)
        self.dma("sync", self.CONSTF[:, 16:32], self.i_bk_cols[:, :], ld)
        self.dma("sync", self.CONSTF[:, 32:96], self.i_b1_cols[:, :], ld)
        self.dma("sync", self.WSELR[:, :], self.i_wselr[:, :], ld)
        self.dma("sync", self.HSELP[:, :], self.i_hselp[:, :], ld)
        self.dma("sync", self.HSELN[:, :], self.i_hseln[:, :], ld)
        on = self.c("ones_ready")
        self.memset(self.ONES1[:, :], 1.0, incs=[_inc(on)])
        self.memset(self.ONESK[:, :], 1.0, incs=[_inc(on)])
        self.memset(self.ONES1F[:, :], 1.0, incs=[_inc(on)])
        self.memset(self.ONESKF[:, :], 1.0, incs=[_inc(on)])
        self.memset(self.CONSTC[:, 0:1], MAGIC, incs=[_inc(on)])
        self.memset(self.CONSTC[:, 1:2], SQ128, incs=[_inc(on)])
        self.ld, self.ones = ld, on
        self.ldg = ldg

    def transpose_tile(self, dst, src_tile, j, done, waits=()):
        for half in range(2):
            pt, prd, pfr = self.pt[half], self.pt_rd[half], self.pt_fr[half]
            for dci in range(8):
                dc = half * 8 + dci
                self.tr(pt[:, dci * 128:(dci + 1) * 128],
                        src_tile[:, dc * 128:(dc + 1) * 128],
                        waits=list(waits) + [(pfr, pfr.val), self.ld],
                        incs=([_inc(prd)] if dci == 7 else []))
            dv = A_(dst, (half * 8) * TOK + j * 128,
                    (16 * TOK, 128), (TOK, 8), (1, 128))
            sv = A_(pt, 0, (1024, 128), (128, 8), (1, 128))
            self.cp("vector", dv, sv, waits=[(prd, prd.val)],
                    incs=[_inc(done), _inc(pfr)])

    def stream_transpose(self, dst, tag, gate_per_tile=None):
        cast = self.c("st_cast")
        done = self.c("st_tev")
        for j in range(NT):
            w = [self.ld, self.ones]
            if gate_per_tile is not None:
                w.append((gate_per_tile[0], gate_per_tile[1] + j + 1))
            if j >= 2:
                w.append((done, (j - 1) * 2))
            xb = self.ar2b(XB_O + (j % 2) * 2048, 2048)
            self.cp("vector", xb, self.X[:, j * D:(j + 1) * D], waits=w,
                    incs=[_inc(cast)])
            self.transpose_tile(dst, xb, j, done, waits=[(cast, cast.val)])
        return done

    def phase_wprep1(self):
        self.shards = [
            ("g", self.i_wg_sh, self.g2gi, 2, 4),
            ("u", self.i_wup_sh, self.g2ui, 2, 4),
            ("d", self.i_wd_sh, self.g2di, 8, 1),
            ("1", self.i_w1_sh, self.g3ai, 2, 4),
            ("2", self.i_w2_sh, self.g3bi, 8, 1),
        ]
        ldc2 = [self.c("wp1_ld0"), self.c("wp1_ld1")]
        acc = self.c("wp1_acc")
        tsum = self.c("wp1_tsum")
        self.memset(self.STAT[:, 56:61], 0.0, incs=[_inc(tsum)])
        gci = 0
        for mi, (nm, src, _d, nrb, ncb) in enumerate(self.shards):
            cw = src.shape[1] // ncb
            for rb in range(nrb):
                for cb in range(ncb):
                    half = gci % 2
                    st = self.WPST[:, half * 2048:(half + 1) * 2048]
                    w = [(acc, acc.val - 1)] if gci >= 2 else []
                    gci += 1
                    self.dma("scalar", st,
                             src[rb * 128:(rb + 1) * 128, cb * cw:(cb + 1) * cw],
                             ldc2[half], waits=w)
                    self.act(st, st,
                             AF.Abs, accum=self.STAT[:, 32 + half:33 + half],
                             waits=[(ldc2[half], ldc2[half].val),
                                    (tsum, max(1, gci - 1))],
                             incs=[_inc(acc)])
                    self.tt("vector", self.STAT[:, 56 + mi:57 + mi],
                            self.STAT[:, 56 + mi:57 + mi],
                            self.STAT[:, 32 + half:33 + half], ALU.add,
                            waits=[(acc, acc.val), (tsum, tsum.val)],
                            incs=[_inc(tsum)])
        self.wp1_tsum = tsum

    def phase_wprep1_tail(self):
        red = self.c("wp1_red")
        g1dma = self.c("g1in_dma")
        tsum = self.wp1_tsum
        base = 1
        for mi, (nm, src, _d, nrb, ncb) in enumerate(self.shards):
            base += nrb * ncb
            pbk, prd, pfr = self.bank(5)
            self.mm(pbk[0:1, 0:128], self.STAT[:, 56 + mi:57 + mi],
                    self.IDENTF[:, :],
                    True, True, waits=[(tsum, base), self.ld, (pfr, pfr.val)],
                    incs=[_inc(prd)])
            self.red(self.G1S[0:1, mi:mi + 1], pbk[0:1, 0:128], ALU.add,
                     waits=[(prd, prd.val)], incs=[_inc(red), _inc(pfr)])
            self.dma("scalar", self.g1i[0:1, mi:mi + 1],
                     self.G1S[0:1, mi:mi + 1], g1dma, waits=[(red, red.val)])
        g1done = self.c("g1_done")
        self.coll(self.g1i[:, :], self.g1o[:, :], [list(range(8))],
                  waits=[g1dma], incs=[_inc(g1done)])
        g1ld = self.c("ld_init")
        self.dma("sync", self.G1S[0:8, 0:5], self.g1o[:, :], g1ld, waits=[g1done])
        sc = self.c("wp1_scales")
        pbk, prd, pfr = self.bank(5)
        self.mm(pbk[0:1, 0:5], self.ONESKF[0:8, 0:1], self.G1S[0:8, 0:5],
                True, True, waits=[(g1ld, g1ld.val), self.ones, (pfr, pfr.val)],
                incs=[_inc(prd)])
        numel = [D * DI, D * DI, DI * D, D * DI, DI * D]
        for mi in range(5):
            self.ts("vector", self.G1S[0:1, 8 + mi:9 + mi], pbk[0:1, mi:mi + 1],
                    1.0 / numel[mi], EPS_Q, ALU.mult, ALU.max,
                    waits=[(prd, prd.val)], incs=[_inc(sc)])
        self.recip(self.G1S[0:1, 16:21], self.G1S[0:1, 8:13],
                   waits=[(sc, sc.val)], incs=[_inc(sc), _inc(pfr)])
        for mi in range(3):
            pb4, prd4, pfr4 = self.bank(4)
            self.mm(pb4[:, 0:1], self.ONES1F[0:1, 0:128],
                    self.G1S[0:1, 16 + mi:17 + mi], True, True,
                    waits=[(sc, sc.val), (pfr4, pfr4.val)], incs=[_inc(prd4)])
            self.cp("vector", self.CONSTF[:, 96 + mi:97 + mi], pb4[:, 0:1],
                    waits=[(prd4, prd4.val)], incs=[_inc(sc), _inc(pfr4)])
        for mi in range(5):
            pb4, prd4, pfr4 = self.bank(4)
            self.mm(pb4[:, 0:1], self.ONES1F[0:1, 0:128],
                    self.G1S[0:1, 8 + mi:9 + mi], True, True,
                    waits=[(sc, sc.val), (pfr4, pfr4.val)], incs=[_inc(prd4)])
            self.cp("vector", self.CONSTF[:, 99 + mi:100 + mi], pb4[:, 0:1],
                    waits=[(prd4, prd4.val)], incs=[_inc(sc), _inc(pfr4)])
        self.wp1_scales = sc

    def phase_wprep2(self):
        ldc2 = [self.c("wp1_ld0"), self.c("wp1_ld1")]
        up = self.c("wp2_u")
        qd = self.c("wp2_q")
        outd = self.c("wp2_out")
        self.wp2_trig = []
        gci = 0
        for mi, (nm, src, dstb, nrb, ncb) in enumerate(self.shards):
            tern = mi < 3
            cw = src.shape[1] // ncb
            ci = 0
            for rb in range(nrb):
                for cb in range(ncb):
                    half = gci % 2
                    st = self.WPST[:, half * 2048:(half + 1) * 2048]
                    w = [(outd, max(0, outd.val - 16))] if gci >= 2 else []
                    gci += 1
                    self.dma("scalar", st,
                             src[rb * 128:(rb + 1) * 128, cb * cw:(cb + 1) * cw],
                             ldc2[half], waits=w)
                    qtile = self.WPQ[:, 0:2048]
                    w2 = [(outd, outd.val)]
                    if tern:
                        ups = self.c("wp2_us")
                        self.ts("vector", st, st,
                                self.CONSTF[:, 96 + mi:97 + mi], MAGIC,
                                ALU.mult, ALU.add,
                                waits=[(ldc2[half], ldc2[half].val),
                                       (self.wp1_scales, self.wp1_scales.val)],
                                incs=[_inc(ups)])
                        self.ts("vector", st, st, MAGIC, 1.0, ALU.subtract,
                                ALU.min, waits=[(ups, ups.val)], incs=[_inc(up)])
                        self.ts("vector", qtile, st, -1.0, None, ALU.max,
                                waits=[(up, up.val)] + w2, incs=[_inc(qd)])
                        qthis = qd
                    else:
                        qs = self.c("wp2_us")
                        self.ts("vector", st, st, 1e38, 1.0, ALU.mult,
                                ALU.min,
                                waits=[(ldc2[half], ldc2[half].val)],
                                incs=[_inc(qs)])
                        self.ts("vector", qtile, st, -1.0, None, ALU.max,
                                waits=[(qs, qs.val)] + w2, incs=[_inc(qs)])
                        qthis = qs
                    self.dma("scalar",
                             dstb[rb * 128:(rb + 1) * 128, cb * cw:(cb + 1) * cw],
                             qtile, outd, waits=[(qthis, qthis.val)])
                    ci += 1
            self.wp2_trig.append(outd.val)
        self.wp2_outd = outd

    def emit_g2g3_triggers(self):
        pairs = [(self.g2gi, self.g2go), (self.g2ui, self.g2uo),
                 (self.g2di, self.g2do), (self.g3ai, self.g3ao),
                 (self.g3bi, self.g3bo)]
        g23 = self.c("g23")
        for mi, (i_, o_) in enumerate(pairs):
            self.coll(i_[:, :], o_[:, :], [list(range(8))],
                      waits=[(self.wp2_outd, self.wp2_trig[mi])], incs=[_inc(g23)])
        self.g23 = g23

    def emit_qkv(self):
        self.x0t = self.stream_transpose(self.xT, "x0T")
        x0t = self.x0t
        wdma = [self.c("wb_d0"), self.c("wb_d1")]
        wfree = [self.c("wb_f0"), self.c("wb_f1")]
        stg = self.c("g4_stage")
        qspill = self.c("q_spill")
        ev = self.c("qkv_ev")
        QPRE = self.ar2f(QPRE_O, 1024)
        QB = self.ar2b(QS_O, 1024)
        RT = self.ar2f(RTMP_O, 1024)
        RT2 = self.ar2f(RTMP2_O, 1024)

        for proj, wsrc, bc0, isq in (("q", self.i_wq, 0, True),
                                     ("k", self.i_wk, 16, False)):
            for ogh in range(8):
                slot = ogh % 2
                dstw = A_(self.WB[slot], 0, (4096, 128), (256, 16), (128, 2), (1, 128))
                srcw = A_(wsrc, ogh * 256, (D, 128), (128 * D, 16), (128, 2), (1, 128))
                self.dma("gpsimd", dstw, srcw, wdma[slot],
                         waits=[(wfree[slot], wfree[slot].val)])
                for oc in range(2):
                    och = ogh * 2 + oc
                    pair = (0, 1) if och % 2 == 0 else (2, 3)
                    for hh in range(2):
                        bi = pair[hh]
                        pbk, prd, pfr = self.bank(bi)
                        for k in range(16):
                            self.mm(pbk[:, :],
                                    self.WB[slot][:, (k * 2 + oc) * 128:
                                                  (k * 2 + oc + 1) * 128],
                                    self.xT[:, k * TOK + hh * 512:
                                            k * TOK + hh * 512 + 512],
                                    k == 0, k == 15,
                                    waits=[(wdma[slot], wdma[slot].val),
                                           (x0t, x0t.val), (pfr, pfr.val)],
                                    incs=([_inc(prd)] if k == 15 else []))
                    for hh in range(2):
                        bi = pair[hh]
                        pbk, prd, pfr = self.bank(bi)
                        self.ts("vector", QPRE[:, hh * 512:hh * 512 + 512],
                                pbk[:, :], self.CONSTF[:, bc0 + och:bc0 + och + 1],
                                None, ALU.add,
                                waits=[(prd, prd.val), self.ld],
                                incs=[_inc(ev), _inc(pfr)])
                    self.cp("vector", QB, QPRE,
                            waits=[(ev, ev.val), (qspill, qspill.val),
                                   (stg, stg.val)], incs=[_inc(ev)])
                    for hh, bi in ((0, 4), (1, 5)):
                        pbk, prd, pfr = self.bank(bi)
                        self.mm(pbk[:, :], self.CONST[:, 1152:1280],
                                QB[:, hh * 512:hh * 512 + 512], True, True,
                                waits=[(ev, ev.val), (pfr, pfr.val)],
                                incs=[_inc(prd)])
                    self.tt("vector", RT, QPRE, self.ar2b(9216, 1024), ALU.mult,
                            waits=[(ev, ev.val), (self.ldg, self.ldg.val)],
                            incs=[_inc(ev)])
                    for hh, bi in ((0, 4), (1, 5)):
                        pbk, prd, pfr = self.bank(bi)
                        self.tt("vector", RT2[:, hh * 512:hh * 512 + 512],
                                pbk[:, :],
                                self.ar2b(10240 + hh * 512, 512), ALU.mult,
                                waits=[(prd, prd.val)], incs=[_inc(ev), _inc(pfr)])
                    self.tt("vector", QB, RT, RT2, ALU.add,
                            waits=[(ev, ev.val)], incs=[_inc(ev)])
                    if isq:
                        self.dma("sync", self.dQ[och * 128:(och + 1) * 128, :],
                                 QB, qspill, waits=[(ev, ev.val)])
                    else:
                        dstap = A_(self.g4i, och * 128 * TOK, (TOK, 128), (1, TOK))
                        self.dma("sync", dstap, QB, stg, waits=[(ev, ev.val)])
                self.P.wait("sync", (ev, ev.val))
                self.P.raw("sync", lambda e: e.nop(), incs=[_inc(wfree[slot])])
            if proj == "q":
                self.phase_wprep1_tail()
            if proj == "k":
                self.kstg = stg.val
        vev = self.c("qkv_ev")
        browc = self.c("brow")
        self.dma("gpsimd", self.BROW[:, :], self.i_bias_rows[0:1, :], browc)
        for oh in range(8):
            slot = oh % 2
            dstw = A_(self.WB[slot], 0, (4096, 128), (256, 16), (1, 256))
            srcw = A_(self.i_wv, oh * 256, (D, 128), (128 * D, 16), (1, 256))
            self.dma("gpsimd", dstw, srcw, wdma[slot],
                     waits=[(wfree[slot], wfree[slot].val)])
            for j in range(NT):
                vst = self.ar2b((VST0_O if j % 2 == 0 else VST1_O), 256)
                bi = j % 2
                pbk, prd, pfr = self.bank(bi)
                for k in range(16):
                    self.mm(pbk[:, 0:256],
                            self.xT[:, k * TOK + j * 128:k * TOK + (j + 1) * 128],
                            self.WB[slot][:, k * 256:(k + 1) * 256],
                            k == 0, False,
                            waits=[(wdma[slot], wdma[slot].val),
                                   (x0t, x0t.val), (pfr, pfr.val)])
                self.mm(pbk[:, 0:256], self.ONES1[0:1, 0:128],
                        self.BROW[0:1, oh * 256:oh * 256 + 256], False, True,
                        waits=[self.ones, (browc, browc.val)], incs=[_inc(prd)])
                self.cp("vector", vst, pbk[:, 0:256],
                        waits=[(prd, prd.val), (stg, stg.val)],
                        incs=[_inc(vev), _inc(pfr)])
                dstap = A_(self.g4i, D * TOK + j * 128 * D + oh * 256,
                           (D, 128), (1, 256))
                self.dma("sync", dstap, vst, stg, waits=[(vev, vev.val)])
            self.P.raw("sync", lambda e: e.nop(), waits=[(stg, stg.val)],
                       incs=[_inc(wfree[slot])])
        self.g4stage = stg
        self.qspill = qspill

    def emit_g4(self):
        g4k = self.c("g4k_done")
        g4v = self.c("g4v_done")
        self.coll(self.g4i[0:1, :], self.g4oK[:, :],
                  [[0, 1, 2, 3], [4, 5, 6, 7]],
                  waits=[(self.g4stage, self.kstg)], incs=[_inc(g4k)])
        self.coll(self.g4i[1:2, :], self.g4oV[:, :],
                  [[0, 1, 2, 3], [4, 5, 6, 7]],
                  waits=[(self.g4stage, self.g4stage.val)], incs=[_inc(g4v)])
        self.g4kdone, self.g4vdone = g4k, g4v

    def emit_attention(self):
        kld = [self.c("k_ld0"), self.c("k_ld1")]
        vld = [self.c("v_ld0"), self.c("v_ld1")]
        kvfr = [self.c("kv_fr0"), self.c("kv_fr1")]
        qld = [self.c("q_ld0"), self.c("q_ld1")]
        qfr = [self.c("q_fr0"), self.c("q_fr1")]
        aready = self.c("at_exp")
        atfree = self.c("at_free")
        amsk = self.c("at_msk")
        dnm = self.c("at_dnm")
        oev = self.c("at_oev")
        odma = self.c("at_odma")
        nquad = 0
        for h in range(NH):
            hb = h % 2
            for r in range(4):
                src = A_(self.g4oK, r * D * TOK + h * 128 * TOK,
                         (TOK, 128), (1, TOK))
                self.dma("sync", self.ar2b(KH_O[hb] + r * 1024, 1024), src,
                         kld[hb],
                         waits=[(self.g4kdone, 1), (kvfr[hb], kvfr[hb].val)])
            for r in range(4):
                src = A_(self.g4oV, r * D * TOK + h * 128,
                         (D, 128), (128 * D, 8), (1, 128))
                dst = A_(self.AR2, VH_O[hb] + r * 128, (24576, 128), (512, 8),
                         (1, 128))
                self.dma("gpsimd", dst, src, vld[hb],
                         waits=[(self.g4vdone, 1), (kvfr[hb], kvfr[hb].val)])
            KH = self.ar2b(KH_O[hb], 4096)
            VH = self.ar2b(VH_O[hb], 4096)
            for quad in range(2):
                j0 = 4 * quad
                qb = nquad % 2
                QH = self.ar2b(QH_O[qb], 512)
                self.dma("sync", QH,
                         self.dQ[h * 128:(h + 1) * 128, quad * 512:quad * 512 + 512],
                         qld[qb], waits=[(self.qspill, self.qspill.val),
                                         (qfr[qb], qfr[qb].val)])
                a0, d0, m0 = aready.val, dnm.val, amsk.val
                for t in range(32):
                    koff = (t % 4) * 1024 + (t // 4) * 128
                    bi = t % 4
                    pbk, prd, pfr = self.bank(bi)
                    gidx = 32 * nquad + t
                    w = [(kld[hb], kld[hb].val), (qld[qb], qld[qb].val),
                         (atfree, max(0, nquad - 1))]
                    if gidx >= 4:
                        w.append((aready, gidx - 3))
                    self.mm(pbk[:, :], KH[:, koff:koff + 128], QH, True, True,
                            waits=w, incs=[_inc(prd)])
                    self.act(self.AR1[:, t * 512:(t + 1) * 512], pbk[:, :], AF.Exp,
                             scale=self.CONSTC[:, 1:2],
                             waits=[(prd, prd.val), (atfree, max(0, nquad - 1))],
                             incs=[_inc(aready)])
                pb4, prd4, pfr4 = self.bank(4)
                for t in range(32):
                    self.mm(pb4[0:1, :], self.ONESK[:, :],
                            self.AR1[:, t * 512:(t + 1) * 512], t == 0, t == 31,
                            waits=[(aready, a0 + t + 1), self.ones,
                                   (pfr4, pfr4.val)],
                            incs=[_inc(dnm)] + ([_inc(prd4)] if t == 31 else []))
                for jj in range(4):
                    for m in range(4):
                        t = 4 * (j0 + jj) + m
                        sl = self.AR1[:, t * 512 + jj * 128:t * 512 + (jj + 1) * 128]
                        self.tt("vector", sl, sl,
                                self.CONST[:, 384 + m * 128:384 + (m + 1) * 128],
                                ALU.mult,
                                waits=[(dnm, d0 + t + 1), self.ld],
                                incs=[_inc(amsk)])
                psk, prd5, pfr5 = self.bank(5)
                nav = 4 * j0 + 16
                for t in range(nav):
                    c0 = 128 * max(0, min(4, t // 4 - j0))
                    w = [(aready, a0 + t + 1), (pfr5, pfr5.val),
                         (vld[hb], vld[hb].val)]
                    if j0 <= t // 4 <= j0 + 3:
                        w.append((amsk, m0 + (t // 4 - j0) * 4 + (t % 4) + 1))
                    self.mm(psk[:, c0:512], VH[:, t * 128:(t + 1) * 128],
                            self.AR1[:, t * 512 + c0:(t + 1) * 512],
                            t == 0, t == nav - 1, waits=w,
                            incs=(([_inc(prd5), _inc(atfree)])
                                  if t == nav - 1 else []))
                self.recip(self.AROW[0:1, 0:512], pb4[0:1, :],
                           waits=[(prd4, prd4.val)], incs=[_inc(oev), _inc(pfr4)])
                self.mm(pb4[:, :], self.ONES1F[0:1, 0:128], self.AROW[0:1, 0:512],
                        True, True, waits=[(oev, oev.val), (pfr4, pfr4.val)],
                        incs=[_inc(prd4)])
                OST = self.ar2b(OST_O[qb], 512)
                RBF = self.ar2b(18432 + (qb % 2) * 1024, 1024).bitcast(F32)
                self.cp("vector", RBF, pb4[:, :],
                        waits=[(prd4, prd4.val), (oev, oev.val)],
                        incs=[_inc(oev), _inc(pfr4)])
                self.tt("vector", OST, psk[:, :], RBF, ALU.mult,
                        waits=[(oev, oev.val), (prd5, prd5.val),
                               (odma, odma.val)],
                        incs=[_inc(oev), _inc(pfr5)])
                self.dma("sync", self.dO[h * 128:(h + 1) * 128,
                                         quad * 512:quad * 512 + 512], OST,
                         odma, waits=[(oev, oev.val)],
                         extra=[_inc(qfr[qb])])
                nquad += 1
            self.P.raw("sync", lambda e: e.nop(), waits=[(odma, odma.val)],
                       incs=[_inc(kvfr[hb])])
        self.odma = odma
        self.atfree = atfree

    def ln(self, j, ready, tag):
        eS = self.c("ln_s")
        e = self.c("ln_v")
        src = self.X[:, j * D:(j + 1) * D]
        wq = self.c("wp2_out")
        for q in range(4):
            sl = src[:, q * 512:(q + 1) * 512]
            self.tt("vector", self.WPST[:, (q % 2) * 512:(q % 2) * 512 + 512],
                    sl, sl, ALU.mult,
                    waits=[(ready, ready.val), (wq, wq.val), (eS, eS.val),
                           (e, e.val)],
                    incs=[_inc(e)], accum=self.STAT[:, 2 + q:3 + q])
        self.act(src[:, 0:1024], src[:, 0:1024], AF.Identity,
                 accum=self.STAT[:, 0:1],
                 waits=[(ready, ready.val), (e, e.val)], incs=[_inc(eS)])
        self.act(src[:, 1024:2048], src[:, 1024:2048], AF.Identity,
                 accum=self.STAT[:, 1:2],
                 waits=[(ready, ready.val), (eS, eS.val)], incs=[_inc(eS)])
        self.tt("vector", self.STAT[:, 6:7], self.STAT[:, 0:1], self.STAT[:, 1:2],
                ALU.add, waits=[(e, e.val), (eS, eS.val)], incs=[_inc(e)])
        self.ts("vector", self.STAT[:, 6:7], self.STAT[:, 6:7], -1.0 / D, None,
                ALU.mult, waits=[(e, e.val)], incs=[_inc(e)])
        self.tt("vector", self.STAT[:, 7:8], self.STAT[:, 2:3], self.STAT[:, 3:4],
                ALU.add, waits=[(e, e.val)], incs=[_inc(e)])
        self.tt("vector", self.STAT[:, 8:9], self.STAT[:, 4:5], self.STAT[:, 5:6],
                ALU.add, waits=[(e, e.val)], incs=[_inc(e)])
        self.tt("vector", self.STAT[:, 7:8], self.STAT[:, 7:8], self.STAT[:, 8:9],
                ALU.add, waits=[(e, e.val)], incs=[_inc(e)])
        self.ts("vector", self.STAT[:, 7:8], self.STAT[:, 7:8], 1.0 / D, EPS_LN,
                ALU.mult, ALU.add, waits=[(e, e.val)], incs=[_inc(e)])
        self.tt("vector", self.STAT[:, 8:9], self.STAT[:, 6:7], self.STAT[:, 6:7],
                ALU.mult, waits=[(e, e.val)], incs=[_inc(e)])
        self.tt("vector", self.STAT[:, 7:8], self.STAT[:, 7:8], self.STAT[:, 8:9],
                ALU.subtract, waits=[(e, e.val)], incs=[_inc(e)])
        self.act(self.STAT[:, 9:10], self.STAT[:, 7:8], AF.Sqrt,
                 waits=[(e, e.val)], incs=[_inc(eS)])
        self.recip(self.STAT[:, 10:11], self.STAT[:, 9:10],
                   waits=[(eS, eS.val)], incs=[_inc(e)])
        self.ts("vector", src, src, self.STAT[:, 6:7], self.STAT[:, 10:11],
                ALU.add, ALU.mult, waits=[(e, e.val)],
                incs=[_inc(self.stream_ready)])

    def emit_wo(self):
        old = self.c("wo_old")
        src = A_(self.dO, 0, (TOK, 128), (128 * TOK, 16), (1, TOK))
        dst = A_(self.AR1, 0, (16 * TOK, 128), (TOK, 16), (1, TOK))
        self.dma("sync", dst, src, old,
                 waits=[(self.odma, self.odma.val), (self.atfree, self.atfree.val)])
        wdma = [self.c("wb_d0"), self.c("wb_d1")]
        wfree = [self.c("wb_f0"), self.c("wb_f1")]
        acc = self.c("wo_acc")
        browc = self.c("brow")
        self.dma("gpsimd", self.BROW[:, :], self.i_bias_rows[1:2, :], browc,
                 waits=[(self.g4stage, self.g4stage.val)])
        for og in range(4):
            for half in range(2):
                dstw = A_(self.WB[half], 0, (4096, 128), (256, 16), (1, 256))
                srcw = A_(self.i_wo, og * 512 + half * 256,
                          (D, 128), (128 * D, 16), (1, 256))
                self.dma("gpsimd", dstw, srcw, wdma[half],
                         waits=[(wfree[half], wfree[half].val)])
            for j in range(NT):
                bi = j % 2
                pbk, prd, pfr = self.bank(bi)
                for k in range(16):
                    lhs = self.AR1[:, k * TOK + j * 128:k * TOK + (j + 1) * 128]
                    for half in range(2):
                        self.mm(pbk[:, half * 256:half * 256 + 256], lhs,
                                self.WB[half][:, k * 256:(k + 1) * 256],
                                k == 0 and half == 0, False,
                                waits=[(old, old.val),
                                       (wdma[half], wdma[half].val),
                                       (pfr, pfr.val)])
                self.mm(pbk[:, :], self.ONES1[0:1, 0:128],
                        self.BROW[0:1, og * 512:og * 512 + 512], False, True,
                        waits=[self.ones, (browc, browc.val)], incs=[_inc(prd)])
                xs = self.X[:, j * D + og * 512:j * D + og * 512 + 512]
                self.stt("vector", xs, pbk[:, :], 1.0, xs, ALU.mult, ALU.add,
                         waits=[(prd, prd.val)], incs=[_inc(acc), _inc(pfr)])
            for half in range(2):
                self.P.raw("vector", lambda e: e.nop(),
                           waits=[(acc, acc.val)], incs=[_inc(wfree[half])])
        self.stream_base = self.c("stream").val
        self.stream_ready = self.c("stream")
        for j in range(NT):
            self.ln(j, acc, f"b1_{j}")

    def emit_tap(self, i):
        tc = self.c("tapdma")
        for j in range(NT):
            self.dma("scalar", self.o_taps[i][j * 128:(j + 1) * 128, :],
                     self.X[:, j * D:(j + 1) * D], tc,
                     waits=[(self.stream_ready, self.stream_ready.val)])

    def tap_gate(self):
        if not TAPS:
            return []
        tc = self.c("tapdma")
        return [(tc, tc.val)]

    def fwht_stages(self, n, base, width, first_waits, done):
        phys = list(range(n))
        spare = n
        hc = 1
        first = True
        while hc < n:
            for g0 in range(0, n, 2 * hc):
                for i in range(hc):
                    a, b = g0 + i, g0 + hc + i
                    pa, pb_ = phys[a], phys[b]
                    ta = self.ar2b(base + pa * width, width)
                    tb = self.ar2b(base + pb_ * width, width)
                    ts_ = self.ar2b(base + spare * width, width)
                    w = (list(first_waits) if first else []) + [(done, done.val)]
                    self.tt("vector", ts_, ta, tb, ALU.add, waits=w,
                            incs=[_inc(done)])
                    self.tt("vector", ta, ta, tb, ALU.subtract,
                            waits=[(done, done.val)], incs=[_inc(done)])
                    phys[a], phys[b], spare = spare, pa, pb_
            first = False
            hc *= 2
        return phys

    def quant_fm(self, slotmap, base, width, nslots, tag, sw_cell,
                 ready, c1_const=None, c1_from_rms=False):
        assert width == 256
        e = self.c(tag + "_e")
        qd = self.c(tag + "_qd")
        MT = self.ar2b(MAXT_O, 256)
        MN = self.ar2b(GT_O, 256)
        s0 = self.ar2b(base + slotmap[0] * 256, 256)
        self.cp("vector", MT, s0, waits=[ready, (e, e.val)], incs=[_inc(e)])
        self.cp("vector", MN, s0, waits=[(e, e.val)], incs=[_inc(e)])
        for i in range(1, nslots):
            si = self.ar2b(base + slotmap[i] * 256, 256)
            self.tt("vector", MT, MT, si, ALU.max,
                    waits=[(e, e.val)], incs=[_inc(e)])
            self.tt("vector", MN, MN, si, ALU.min,
                    waits=[(e, e.val)], incs=[_inc(e)])
        self.ts("vector", MN, MN, -1.0, None, ALU.mult,
                waits=[(e, e.val)], incs=[_inc(e)])
        self.tt("vector", MT, MT, MN, ALU.max,
                waits=[(e, e.val)], incs=[_inc(e)])
        if c1_from_rms:
            pb4, prd4, pfr4 = self.bank(4)
            TMPB = self.ar2b(TMPF_O, 256)
            sq = self.c(tag + "_sq")
            sqm = self.c(tag + "_sqm")
            for i in range(nslots):
                si = self.ar2b(base + slotmap[i] * 256, 256)
                self.tt("vector", TMPB, si, si, ALU.mult,
                        waits=[ready, (pfr4, pfr4.val), (sqm, sqm.val)],
                        incs=[_inc(sq)])
                self.mm(pb4[0:1, 0:256], self.ONESK[:, 0:1], TMPB, i == 0,
                        i == nslots - 1,
                        waits=[(sq, sq.val), self.ones, (pfr4, pfr4.val)],
                        incs=[_inc(sqm)] + ([_inc(prd4)] if i == nslots - 1 else []))
            self.ts("vector", self.AROW[0:1, 768:1024], pb4[0:1, 0:256],
                    1.0 / DI, EPS_RMS, ALU.mult, ALU.add,
                    waits=[(prd4, prd4.val)], incs=[_inc(e), _inc(pfr4)])
            qfs = self.c("f1_ss")
            self.P.raw("scalar", lambda sc, o=self.AROW[0:1, 768:1024]:
                       sc.activation(o, o, AF.Sqrt, bias=0.0, scale=1.0),
                       waits=[(e, e.val)], incs=[_inc(qfs)])
            self.recip(self.AROW[0:1, 768:1024], self.AROW[0:1, 768:1024],
                       waits=[(qfs, qfs.val)], incs=[_inc(e)])
            self.ts("vector", self.AROW[0:1, 768:1024], self.AROW[0:1, 768:1024],
                    ISQ8192, None, ALU.mult, waits=[(e, e.val)], incs=[_inc(e)])
        for half in range(2):
            pt, prd, pfr = self.pt[0], self.pt_rd[0], self.pt_fr[0]
            self.tr(pt[:, 0:128], MT[:, half * 128:(half + 1) * 128],
                    waits=[(e, e.val), (pfr, pfr.val)], incs=[_inc(prd)])
            self.red(self.STAT[:, 10:11], pt[:, 0:128], ALU.max,
                     waits=[(prd, prd.val)], incs=[_inc(e), _inc(pfr)])
            pb4, prd4, pfr4 = self.bank(4)
            self.mm(pb4[0:1, half * 128:(half + 1) * 128], self.STAT[:, 10:11],
                    self.IDENTF[:, :], True, True,
                    waits=[(e, e.val), (pfr4, pfr4.val), self.ld],
                    incs=[_inc(prd4)])
            self.cp("vector", self.AROW[0:1, half * 128:(half + 1) * 128],
                    pb4[0:1, half * 128:(half + 1) * 128],
                    waits=[(prd4, prd4.val)], incs=[_inc(e), _inc(pfr4)])
        if c1_from_rms:
            self.tt("vector", self.AROW[0:1, 0:256], self.AROW[0:1, 0:256],
                    self.AROW[0:1, 768:1024], ALU.mult, waits=[(e, e.val)],
                    incs=[_inc(e)])
            self.ts("vector", self.AROW[0:1, 0:256], self.AROW[0:1, 0:256],
                    EPS_Q, None, ALU.max, waits=[(e, e.val)], incs=[_inc(e)])
        else:
            self.ts("vector", self.AROW[0:1, 0:256], self.AROW[0:1, 0:256],
                    c1_const, EPS_Q, ALU.mult, ALU.max, waits=[(e, e.val)],
                    incs=[_inc(e)])
        self.P.raw("vector", lambda v, o=self.AROW[0:1, 512:768],
                   i=self.AROW[0:1, 0:256], s=sw_cell:
                   v.tensor_scalar(o, i, s, 1.0 / 127.0, ALU.mult, ALU.mult),
                   waits=[(e, e.val), (self.wp1_scales, self.wp1_scales.val)],
                   incs=[_inc(e)])
        self.recip(self.AROW[0:1, 1280:1536], self.AROW[0:1, 0:256],
                   waits=[(e, e.val)], incs=[_inc(e)])
        if c1_from_rms:
            self.tt("vector", self.AROW[0:1, 1280:1536], self.AROW[0:1, 1280:1536],
                    self.AROW[0:1, 768:1024], ALU.mult, waits=[(e, e.val)],
                    incs=[_inc(e)])
            self.ts("vector", self.AROW[0:1, 1280:1536], self.AROW[0:1, 1280:1536],
                    127.0, None, ALU.mult, waits=[(e, e.val)], incs=[_inc(e)])
        else:
            self.ts("vector", self.AROW[0:1, 1280:1536], self.AROW[0:1, 1280:1536],
                    127.0 * c1_const, None, ALU.mult, waits=[(e, e.val)],
                    incs=[_inc(e)])
        pb4, prd4, pfr4 = self.bank(4)
        self.mm(pb4[:, 0:256], self.ONES1F[0:1, 0:128], self.AROW[0:1, 1280:1536],
                True, True, waits=[(e, e.val), (pfr4, pfr4.val), self.ones],
                incs=[_inc(prd4)])
        TMPF = self.ar2f(TMPF_O, 256)
        for i in range(nslots):
            si = self.ar2b(base + slotmap[i] * 256, 256)
            self.tt("vector", TMPF, si, pb4[:, 0:256], ALU.mult,
                    waits=[(prd4, prd4.val), (e, e.val), (qd, qd.val)],
                    incs=[_inc(e)])
            self.ts("vector", si, TMPF, MAGIC, MAGIC, ALU.add, ALU.subtract,
                    waits=[(e, e.val)], incs=[_inc(qd)])
        self.P.raw("vector", lambda e_: e_.engine_nop(), waits=[(qd, qd.val)],
                   incs=[_inc(pfr4)])
        return qd

    def emit_l0ffn(self):
        x1t = self.stream_transpose(self.xT, "x1T",
                                    gate_per_tile=(self.stream_ready,
                                                   self.stream_base))
        wdma = [self.c("ws_d0"), self.c("ws_d1")]
        wfree = [self.c("wb_f0"), self.c("wb_f1")]
        ns2 = self.c("stream")
        for tt in range(4):
            hmm = self.c("f0_h")
            for dc in range(16):
                bi = dc % 2
                pbk, prd, pfr = self.bank(bi)
                self.mm(pbk[:, 0:256], self.CONST[:, 128:256],
                        self.xT[:, dc * TOK + tt * 256:dc * TOK + tt * 256 + 256],
                        True, True,
                        waits=[(x1t, x1t.val), self.ld, (pfr, pfr.val)],
                        incs=[_inc(prd)])
                self.cp("vector", self.ar2b(FW_O + dc * 256, 256), pbk[:, 0:256],
                        waits=[(prd, prd.val)], incs=[_inc(hmm), _inc(pfr)])
            bd = self.c("f0_bf")
            smap = self.fwht_stages(16, FW_O, 256, [(hmm, hmm.val)], bd)
            qd = self.quant_fm(smap, FW_O, 256, 16, "qf",
                               self.G1S[0:1, 8:9], (bd, bd.val),
                               c1_const=ISQ2048)
            self.P.raw("vector", lambda v, o=self.AROW[0:1, 1536:1792],
                       i=self.AROW[0:1, 0:256], s=self.G1S[0:1, 9:10]:
                       v.tensor_scalar(o, i, s, 1.0 / 127.0, ALU.mult, ALU.mult),
                       waits=[(qd, qd.val),
                              (self.wp1_scales, self.wp1_scales.val)],
                       incs=[_inc(qd)])
            if DBG0 and tt == 0:
                self.smap_dbg = list(smap)
                self.dma("sync", self.dbg_fw[:, :], self.ar2b(FW_O, 4352),
                         self.c("dbgd"), waits=[(qd, qd.val)])
                self.dma("sync", self.dbg_arow[0:1, 0:256],
                         self.AROW[0:1, 0:256], self.c("dbgd"),
                         waits=[(qd, qd.val)])
                self.dma("sync", self.dbg_arow[0:1, 512:768],
                         self.AROW[0:1, 512:768], self.c("dbgd"),
                         waits=[(qd, qd.val)])
                self.dma("sync", self.dbg_arow[0:1, 1280:1536],
                         self.AROW[0:1, 1280:1536], self.c("dbgd"),
                         waits=[(qd, qd.val)])
            agb = self.c("f0_agb")
            pb4, prd4, pfr4 = self.bank(4)
            self.mm(pb4[:, 0:256], self.ONES1F[0:1, 0:128], self.AROW[0:1, 512:768],
                    True, True, waits=[(qd, qd.val), (pfr4, pfr4.val)],
                    incs=[_inc(prd4)])
            AGB = self.ar2f(AGB_O, 256)
            self.cp("vector", AGB, pb4[:, 0:256], waits=[(prd4, prd4.val)],
                    incs=[_inc(agb), _inc(pfr4)])
            zready = self.c("f0_z")
            GTt = self.ar2b(GT_O, 256)
            SGt = self.ar2b(SG_O, 256)
            for og in range(32):
                wv = [0, 0]
                for mat in range(2):
                    gsrc = self.g2go if mat == 0 else self.g2uo
                    dstw = A_(self.WB[mat], 0, (4096, 128), (256, 16), (128, 2),
                              (1, 128))
                    srcw = A_(gsrc, og * 256, (DI, 128), (128 * DI, 16), (128, 2),
                              (1, 128))
                    self.dma("sync", dstw, srcw, wdma[mat],
                             waits=[(wfree[mat], wfree[mat].val),
                                    (self.g23, mat + 1)])
                    wv[mat] = wdma[mat].val
                for oc in range(2):
                    och = og * 2 + oc
                    gb, ub = (0, 1) if och % 2 == 0 else (2, 3)
                    for mat, bi in ((0, gb), (1, ub)):
                        pbk, prd, pfr = self.bank(bi)
                        for k in range(16):
                            self.mm(pbk[:, 0:256],
                                    self.WB[mat][:, (k * 2 + oc) * 128:
                                                 (k * 2 + oc + 1) * 128],
                                    self.ar2b(FW_O + smap[k] * 256, 256),
                                    k == 0, k == 15,
                                    waits=[(wdma[mat], wv[mat]), (qd, qd.val),
                                           (pfr, pfr.val)],
                                    incs=([_inc(prd)] if k == 15 else []))
                    gbk, gprd, gpfr = self.bank(gb)
                    ubk, uprd, upfr = self.bank(ub)
                    zs = self.c("f0_zs")
                    self.tt("vector", GTt, gbk[:, 0:256], AGB, ALU.mult,
                            waits=[(gprd, gprd.val), (agb, agb.val),
                                   (zs, zs.val)],
                            incs=[_inc(zready), _inc(gpfr)])
                    self.act(SGt, GTt, AF.Silu, waits=[(zready, zready.val)],
                             incs=[_inc(zs)])
                    self.tt("vector", self.ar2b(Z_O + och * 256, 256), SGt,
                            ubk[:, 0:256], ALU.mult,
                            waits=[(zs, zs.val), (uprd, uprd.val)]
                            + getattr(self, "_dbgz_gate", []),
                            incs=[_inc(zready), _inc(upfr)])
                for mat in range(2):
                    self.P.raw("vector", lambda e: e.nop(),
                               waits=[(zready, zready.val)],
                               incs=[_inc(wfree[mat])])
            dbg_gate = []
            if DBG0 and tt == 0:
                dd = self.c("dbgd")
                self.dma("sync", self.dbg_z[:, :], self.ar2b(Z_O, 16640),
                         dd, waits=[(zready, zready.val)])
                dbg_gate = [(dd, dd.val)]
                self._dbgz_gate = dbg_gate
            h2 = self.c("f0_h2")
            for sl in range(64):
                bi = sl % 2
                pbk, prd, pfr = self.bank(bi)
                zt = self.ar2b(Z_O + sl * 256, 256)
                self.mm(pbk[:, 0:256], self.CONST[:, 128:256], zt, True, True,
                        waits=[(zready, zready.val), (pfr, pfr.val)] + dbg_gate,
                        incs=[_inc(prd)])
                self.cp("vector", zt, pbk[:, 0:256], waits=[(prd, prd.val)],
                        incs=[_inc(h2), _inc(pfr)])
            bd2 = self.c("f0_bf2")
            smap2 = self.fwht_stages(64, Z_O, 256, [(h2, h2.val)], bd2)
            qd2 = self.quant_fm(smap2, Z_O, 256, 64, "qf",
                                self.G1S[0:1, 10:11], (bd2, bd2.val),
                                c1_const=ISQ8192)
            if DBG0 and tt == 0:
                self.smap2_dbg = list(smap2)
                dd = self.c("dbgd")
                self.dma("sync", self.dbg_z2[:, :], self.ar2b(Z_O, 16640),
                         dd, waits=[(qd2, qd2.val)])
                self._dbgz_gate = [(dd, dd.val)]
            self.tt("vector", self.AROW[0:1, 512:768], self.AROW[0:1, 512:768],
                    self.AROW[0:1, 1536:1792], ALU.mult,
                    waits=[(qd2, qd2.val)], incs=[_inc(qd2)])
            dacc = self.c("f0_dacc")
            bcol = self.c("f0_bcol")
            for mh in range(2):
                pb4, prd4, pfr4 = self.bank(4)
                self.mm(pb4[:, mh:mh + 1],
                        self.AROW[0:1, 512 + mh * 128:512 + (mh + 1) * 128],
                        self.ONES1F[0:1, 0:1], True, True,
                        waits=[(qd2, qd2.val), (pfr4, pfr4.val), self.ones],
                        incs=[_inc(prd4)])
                self.cp("vector", self.STAT[:, 12 + mh:13 + mh], pb4[:, mh:mh + 1],
                        waits=[(prd4, prd4.val)], incs=[_inc(bcol), _inc(pfr4)])
            for og in range(4):
                for kl in range(32):
                    slot = kl % 2
                    dstw = A_(self.WB[slot], 0, (4096, 128), (512, 2), (1, 512))
                    srcw = A_(self.g2do, kl * 256 * D + og * 512,
                              (D, 128), (128 * D, 2), (1, 512))
                    self.dma("sync", dstw, srcw, wdma[slot],
                             waits=[(wfree[slot], wfree[slot].val),
                                    (self.g23, 3)])
                    wv_ = wdma[slot].val
                    for kc2 in range(2):
                        kc = kl * 2 + kc2
                        for mh in range(2):
                            pbk, prd, pfr = self.bank(mh)
                            last_of_slot = (kc2 == 1 and mh == 1)
                            ii = ([_inc(prd)] if kc == 63 else [])
                            if last_of_slot:
                                ii = ii + [_inc(wfree[slot])]
                            self.mm(pbk[:, :],
                                    self.ar2b(Z_O + smap2[kc] * 256 + mh * 128, 128),
                                    self.WB[slot][:, kc2 * 512:kc2 * 512 + 512],
                                    kc == 0, kc == 63,
                                    waits=[(wdma[slot], wv_), (qd2, qd2.val),
                                           (pfr, pfr.val)],
                                    incs=ii)
                for mh in range(2):
                    j = tt * 2 + mh
                    pbk, prd, pfr = self.bank(mh)
                    xs = self.X[:, j * D + og * 512:j * D + og * 512 + 512]
                    self.stt("vector", xs, pbk[:, :], self.STAT[:, 12 + mh:13 + mh],
                             xs, ALU.mult, ALU.add,
                             waits=[(prd, prd.val), (bcol, bcol.val)]
                             + self.tap_gate(),
                             incs=[_inc(dacc), _inc(pfr)])
            if tt == 0:
                self.stream_base = ns2.val
            self.stream_ready = ns2
            for mh in range(2):
                self.ln(tt * 2 + mh, dacc, f"b2_{tt}_{mh}")
        self.stream_ready = ns2

    def emit_l1attn(self):
        x2t = self.stream_transpose(self.xT, "x2T",
                                    gate_per_tile=(self.stream_ready,
                                                   self.stream_base))
        wdma = [self.c("wb_d0"), self.c("wb_d1")]
        wfree = [self.c("wb_f0"), self.c("wb_f1")]
        spill = self.c("a1_spill")
        aev = self.c("qkv_ev")
        browc = self.c("brow")
        for mat, wsrc, ddst, brow in ((0, self.i_wu1T, self.dU, 2),
                                      (1, self.i_wv1T, self.dV, 3)):
            self.dma("gpsimd", self.BROW[:, :], self.i_bias_rows[brow:brow + 1, :],
                     browc, waits=[(aev, aev.val),
                                   (self.c("wo_acc"), self.c("wo_acc").val)])
            for oh in range(8):
                slot = oh % 2
                dstw = A_(self.WB[slot], 0, (4096, 128), (256, 16), (1, 256))
                srcw = A_(wsrc, oh * 256, (D, 128), (128 * D, 16), (1, 256))
                self.dma("gpsimd", dstw, srcw, wdma[slot],
                         waits=[(wfree[slot], wfree[slot].val)])
                for j in range(NT):
                    bi = j % 2
                    pbk, prd, pfr = self.bank(bi)
                    for k in range(16):
                        self.mm(pbk[:, 0:256],
                                self.xT[:, k * TOK + j * 128:k * TOK + (j + 1) * 128],
                                self.WB[slot][:, k * 256:(k + 1) * 256],
                                k == 0, False,
                                waits=[(wdma[slot], wdma[slot].val), (x2t, x2t.val),
                                       (pfr, pfr.val)])
                    self.mm(pbk[:, 0:256], self.ONES1[0:1, 0:128],
                            self.BROW[0:1, oh * 256:oh * 256 + 256],
                            False, True, waits=[self.ones, (browc, browc.val)],
                            incs=[_inc(prd)])
                    ust = self.ar2b(UJ_O + (j % 2) * 2048, 256)
                    self.cp("vector", ust, pbk[:, 0:256],
                            waits=[(prd, prd.val), (spill, spill.val)],
                            incs=[_inc(aev), _inc(pfr)])
                    self.dma("sync",
                             ddst[j * 128:(j + 1) * 128, oh * 256:oh * 256 + 256],
                             ust, spill, waits=[(aev, aev.val)])
                self.P.raw("sync", lambda e: e.nop(), waits=[(spill, spill.val)],
                           incs=[_inc(wfree[slot])])
        hred = self.c("a1_h")
        self.dma("gpsimd", self.BROW[:, :], self.i_bias_rows[6:7, :], browc,
                 waits=[(aev, aev.val)])
        for half in range(2):
            dstw = A_(self.WB[half], 0, (4096, 128), (256, 16), (1, 256))
            srcw = A_(self.i_wg1T, half * 256, (512, 128), (128 * 512, 16), (1, 256))
            self.dma("gpsimd", dstw, srcw, wdma[half],
                     waits=[(wfree[half], wfree[half].val)])
        HF = self.ar2b(HFULL_O, 4096)
        for j in range(NT):
            bi = j % 2
            pbk, prd, pfr = self.bank(bi)
            for half in range(2):
                for k in range(16):
                    self.mm(pbk[:, half * 256:half * 256 + 256],
                            self.xT[:, k * TOK + j * 128:k * TOK + (j + 1) * 128],
                            self.WB[half][:, k * 256:(k + 1) * 256],
                            k == 0 and half == 0, False,
                            waits=[(wdma[half], wdma[half].val), (x2t, x2t.val),
                                   (pfr, pfr.val)])
            self.mm(pbk[:, :], self.ONES1[0:1, 0:128], self.BROW[0:1, 0:512],
                    False, True, waits=[self.ones, (browc, browc.val)],
                    incs=[_inc(prd)])
            self.ts("vector", HF[:, j * 512:(j + 1) * 512], pbk[:, :], 0.0, None,
                    ALU.max, waits=[(prd, prd.val)], incs=[_inc(hred), _inc(pfr)])
        for half in range(2):
            self.P.raw("vector", lambda e: e.nop(), waits=[(hred, hred.val)],
                       incs=[_inc(wfree[half])])
        ht = self.c("a1_ht")
        for j in range(NT):
            pt, prd, pfr = self.pt[j % 2], self.pt_rd[j % 2], self.pt_fr[j % 2]
            for hc in range(4):
                self.tr(pt[:, hc * 128:(hc + 1) * 128],
                        HF[:, j * 512 + hc * 128:j * 512 + (hc + 1) * 128],
                        waits=[(hred, hred.val), (pfr, pfr.val)],
                        incs=([_inc(prd)] if hc == 3 else []))
            dv = A_(self.AR2, HT_O + j * 128, (24576, 128), (TOK, 4), (1, 128))
            sv = A_(pt, 0, (1024, 128), (128, 4), (1, 128))
            self.cp("vector", dv, sv, waits=[(prd, prd.val)],
                    incs=[_inc(ht), _inc(pfr)])
        for half in range(2):
            dstw = A_(self.WB[half], 0, (4096, 128), (2048, 2), (1, 2048))
            srcw = A_(self.i_wg2T, half * 2 * 128 * D, (D, 128), (128 * D, 2),
                      (1, 2048))
            self.dma("gpsimd", dstw, srcw, wdma[half],
                     waits=[(wfree[half], wfree[half].val)])
        uvld = self.c("a1_uvld")
        self.dma("gpsimd", self.BROW[:, :], self.i_bias_rows[4:5, :], browc,
                 waits=[(hred, hred.val)])
        gdone = self.c("a1_g")
        ugd = self.c("a1_ug")
        tdma = self.c("g5_stage")
        totd = self.c("a1_ug")
        GA = self.ar2b(GATES_O, 2048)
        UJ = self.ar2b(UJ_O, 2048)
        VJ = self.ar2b(VJ_O, 2048)
        gd0 = gdone.val
        for j in range(NT):
            for og in range(4):
                pbk, prd, pfr = self.bank(og)
                for kc in range(4):
                    ws = kc // 2
                    lhs = A_(self.AR2, HT_O + kc * 1024 + j * 128, (24576, 128),
                             (1, 128))
                    w = [(wdma[ws], wdma[ws].val), (ht, ht.val),
                         (pfr, pfr.val)]
                    if j > 0 and kc == 0:
                        w.append((gdone, gd0 + 4 * (j - 1) + og + 1))
                    self.mm(pbk[:, :], lhs,
                            self.WB[ws][:, (kc % 2) * 2048 + og * 512:
                                        (kc % 2) * 2048 + og * 512 + 512],
                            kc == 0, False, waits=w)
                self.mm(pbk[:, :], self.ONES1[0:1, 0:128],
                        self.BROW[0:1, og * 512:og * 512 + 512], False, True,
                        waits=[self.ones, (browc, browc.val)], incs=[_inc(prd)])
                self.act(GA[:, og * 512:og * 512 + 512], pbk[:, :], AF.Sigmoid,
                         waits=[(prd, prd.val), (ugd, ugd.val)],
                         incs=[_inc(gdone)])
            self.dma("sync", UJ, self.dU[j * 128:(j + 1) * 128, :], uvld,
                     waits=[(spill, spill.val), (ugd, ugd.val),
                            (totd, totd.val)])
            self.dma("sync", VJ, self.dV[j * 128:(j + 1) * 128, :], uvld,
                     waits=[(spill, spill.val), (ugd, ugd.val),
                            (totd, totd.val)])
            self.tt("vector", UJ, UJ, GA, ALU.mult,
                    waits=[(gdone, gdone.val), (uvld, uvld.val)], incs=[_inc(ugd)])
            self.tt("vector", VJ, VJ, GA, ALU.mult, incs=[_inc(ugd)])
            self.dma("sync", self.dUG[j * 128:(j + 1) * 128, :], UJ, spill,
                     waits=[(ugd, ugd.val)])
            self.dma("sync", self.dVG[j * 128:(j + 1) * 128, :], VJ, spill,
                     waits=[(ugd, ugd.val)])
            for og in range(4):
                pb4, prd4, pfr4 = self.bank(4)
                self.mm(pb4[0:1, :], self.ONESK[:, :],
                        UJ[:, og * 512:og * 512 + 512], True, True,
                        waits=[(ugd, ugd.val), self.ones, (pfr4, pfr4.val)],
                        incs=[_inc(prd4)])
                self.cp("vector", self.AROW[0:1, 1280:1792], pb4[0:1, :],
                        waits=[(prd4, prd4.val), (tdma, tdma.val)],
                        incs=[_inc(totd), _inc(pfr4)])
                self.dma("scalar", self.g5i[j:j + 1, og * 512:og * 512 + 512],
                         self.AROW[0:1, 1280:1792], tdma, waits=[(totd, totd.val)])
        for j in range(NT):
            self.dma("scalar", self.g5i[8 + 2 * j:9 + 2 * j, :],
                     self.X[0:1, j * D:(j + 1) * D], tdma,
                     waits=[(self.stream_ready, self.stream_ready.val)])
            self.dma("scalar", self.g5i[9 + 2 * j:10 + 2 * j, :],
                     self.X[127:128, j * D:(j + 1) * D], tdma,
                     waits=[(self.stream_ready, self.stream_ready.val)])
        g5done = self.c("g5_done")
        self.coll(self.g5i[:, :], self.g5o[:, :], [[0, 1, 2, 3], [4, 5, 6, 7]],
                  waits=[(tdma, tdma.val)], incs=[_inc(g5done)])
        g5ld = self.c("ld_init_g")
        G5Sb = self.AR2[0:96, G5S_O:G5S_O + 2048]
        self.dma("gpsimd", G5Sb, self.g5o[:, :], g5ld,
                 waits=[(g5done, 1), (spill, spill.val), (ugd, ugd.val)])
        seld = g5ld
        cbd = self.c("a1_cb")
        for r in range(4):
            self.dma("gpsimd", self.BROW[:, :], self.i_conv_rows[r:r + 1, :],
                     browc, waits=[(cbd, cbd.val)])
            for og in range(4):
                pb4, prd4, pfr4 = self.bank(4)
                self.mm(pb4[:, :], self.ONES1[0:1, 0:128],
                        self.BROW[0:1, og * 512:og * 512 + 512], True, True,
                        waits=[(browc, browc.val), self.ones, (pfr4, pfr4.val)],
                        incs=[_inc(prd4)])
                self.cp("vector", self.ar2b(CB_O + r * 2048 + og * 512, 512),
                        pb4[:, :], waits=[(prd4, prd4.val)],
                        incs=[_inc(cbd), _inc(pfr4)])
        p2 = self.c("a1_p2")
        gotd = self.c("a1_got")
        ld2 = self.c("a1_uvld")
        UG = self.ar2b(UGJ_O, 2048)
        VG = self.ar2b(VGJ_O, 2048)
        GO = self.ar2b(GO_O, 2048)
        GOT = self.ar2b(GOTJ_O, 2048)
        TOUT = self.ar2f(TOUT_O, 2048)
        CVS = self.ar2f(CVS_O, 512)
        for j in range(NT):
            self.dma("sync", UG, self.dUG[j * 128:(j + 1) * 128, :], ld2,
                     waits=[(spill, spill.val), (p2, p2.val)])
            self.dma("sync", VG, self.dVG[j * 128:(j + 1) * 128, :], ld2,
                     waits=[(spill, spill.val), (p2, p2.val)])
            for og in range(4):
                pbk, prd, pfr = self.bank(og)
                self.mm(pbk[:, :], self.CONST[:, 256:384],
                        UG[:, og * 512:og * 512 + 512], True, False,
                        waits=[(ld2, ld2.val), self.ld, (pfr, pfr.val)])
                self.mm(pbk[:, :], self.WSELR[:, j * 128:(j + 1) * 128],
                        G5Sb[:, og * 512:og * 512 + 512], False, True,
                        waits=[(seld, seld.val), self.ld], incs=[_inc(prd)])
                self.tt("vector", GO[:, og * 512:og * 512 + 512], pbk[:, :],
                        VG[:, og * 512:og * 512 + 512], ALU.mult,
                        waits=[(prd, prd.val), (ld2, ld2.val)],
                        incs=[_inc(p2), _inc(pfr)])
            for half in range(2):
                pt, prd, pfr = self.pt[half], self.pt_rd[half], self.pt_fr[half]
                for ci in range(8):
                    kc = half * 8 + ci
                    self.tr(pt[:, ci * 128:(ci + 1) * 128],
                            GO[:, kc * 128:(kc + 1) * 128],
                            waits=[(p2, p2.val), (pfr, pfr.val)],
                            incs=([_inc(prd)] if ci == 7 else []))
                self.cp("vector", GOT[:, half * 1024:(half + 1) * 1024], pt[:, :],
                        waits=[(prd, prd.val), (gotd, gotd.val)],
                        incs=[_inc(p2), _inc(pfr)])
            self.dma("sync", A_(self.dGOT, j * 128, (TOK, 128), (128 * TOK, 16),
                                (1, 128)), GOT, gotd, waits=[(p2, p2.val)])
            xb = self.ar2b(UGJ_O, 2048)
            self.cp("vector", xb, self.X[:, j * D:(j + 1) * D],
                    waits=[(self.stream_ready, self.stream_ready.val),
                           (p2, p2.val)], incs=[_inc(p2)])
            for og in range(4):
                pb4, prd4, pfr4 = self.bank(4)
                psk, prd5, pfr5 = self.bank(5)
                self.mm(pb4[:, :], self.CONST[:, 896:1024],
                        xb[:, og * 512:og * 512 + 512], True, False,
                        waits=[(p2, p2.val), (pfr4, pfr4.val)])
                self.mm(pb4[:, :], self.HSELP[:, j * 128:(j + 1) * 128],
                        G5Sb[:, og * 512:og * 512 + 512],
                        False, True, waits=[(seld, seld.val)], incs=[_inc(prd4)])
                self.mm(psk[:, :], self.CONST[:, 1024:1152],
                        xb[:, og * 512:og * 512 + 512], True, False,
                        waits=[(p2, p2.val), (pfr5, pfr5.val)])
                self.mm(psk[:, :], self.HSELN[:, j * 128:(j + 1) * 128],
                        G5Sb[:, og * 512:og * 512 + 512],
                        False, True, waits=[(seld, seld.val)], incs=[_inc(prd5)])
                to = TOUT[:, og * 512:og * 512 + 512]
                self.tt("vector", to, pb4[:, :],
                        self.ar2b(CB_O + og * 512, 512), ALU.mult,
                        waits=[(prd4, prd4.val), (cbd, cbd.val)],
                        incs=[_inc(p2), _inc(pfr4)])
                self.tt("vector", CVS, self.X[:, j * D + og * 512:
                                              j * D + og * 512 + 512],
                        self.ar2b(CB_O + 2048 + og * 512, 512), ALU.mult,
                        waits=[(p2, p2.val)], incs=[_inc(p2)])
                self.tt("vector", to, to, CVS, ALU.add,
                        waits=[(p2, p2.val)], incs=[_inc(p2)])
                self.tt("vector", CVS, psk[:, :],
                        self.ar2b(CB_O + 4096 + og * 512, 512), ALU.mult,
                        waits=[(prd5, prd5.val), (p2, p2.val)],
                        incs=[_inc(p2), _inc(pfr5)])
                self.tt("vector", to, to, CVS, ALU.add,
                        waits=[(p2, p2.val)], incs=[_inc(p2)])
                self.tt("vector", to, to, self.ar2b(CB_O + 6144 + og * 512, 512),
                        ALU.add, waits=[(cbd, cbd.val), (p2, p2.val)],
                        incs=[_inc(p2)])
                xs = self.X[:, j * D + og * 512:j * D + og * 512 + 512]
                self.tt("vector", xs, xs, to, ALU.add,
                        waits=self.tap_gate() + [(p2, p2.val)], incs=[_inc(p2)])
        uvacc = self.c("a1_uv")
        g2ld2 = [self.c("wp1_ld0"), self.c("wp1_ld1")]
        for oh in range(8):
            slot = oh % 2
            dstw = A_(self.WB[slot], 0, (4096, 128), (256, 16), (1, 256))
            srcw = A_(self.i_wu1T, oh * 256, (D, 128), (128 * D, 16), (1, 256))
            self.dma("gpsimd", dstw, srcw, wdma[slot],
                     waits=[(wfree[slot], wfree[slot].val)])
            for j in range(NT):
                gsl = self.ar2b(GOT2_O[j % 2], 2048)
                g2ld = g2ld2[j % 2]
                self.dma("sync", gsl,
                         A_(self.dGOT, j * 128, (TOK, 128), (128 * TOK, 16),
                            (1, 128)), g2ld,
                         waits=[(gotd, gotd.val), (p2, p2.val),
                                (uvacc, max(0, uvacc.val - 1))])
                bi = j % 2
                pbk, prd, pfr = self.bank(bi)
                for k in range(16):
                    self.mm(pbk[:, 0:256], gsl[:, k * 128:(k + 1) * 128],
                            self.WB[slot][:, k * 256:(k + 1) * 256],
                            k == 0, k == 15,
                            waits=[(wdma[slot], wdma[slot].val),
                                   (g2ld, g2ld.val), (pfr, pfr.val)],
                            incs=([_inc(prd)] if k == 15 else []))
                xs = self.X[:, j * D + oh * 256:j * D + oh * 256 + 256]
                self.stt("vector", xs, pbk[:, 0:256], 1.0, xs, ALU.mult, ALU.add,
                         waits=[(prd, prd.val), (p2, p2.val)],
                         incs=[_inc(uvacc), _inc(pfr)])
            self.P.raw("vector", lambda e: e.nop(),
                       waits=[(uvacc, uvacc.val)], incs=[_inc(wfree[slot])])
        self.stream_base = self.c("stream").val
        self.stream_ready = self.c("stream")
        for j in range(NT):
            self.ln(j, uvacc, f"b3_{j}")

    def emit_l1ffn(self):
        wdma = [self.c("ws_d0"), self.c("ws_d1")]
        wfree = [self.c("wb_f0"), self.c("wb_f1")]
        browc = self.c("brow")
        self.dma("gpsimd", self.BROW[:, :], self.i_bias_rows[5:6, :], browc,
                 waits=[(self.c("a1_g"), self.c("a1_g").val)])
        qdone = self.c("f1_qT")
        XQ = self.ar2b(XQ_O, 2048)
        RND = self.ar2f(TOUT_O, 2048)
        for j in range(NT):
            e = self.c("f1_s")
            xs = self.X[:, j * D:(j + 1) * D]
            dumpb = self.WPQ[:, 0:1024]
            self.tt("vector", dumpb, xs[:, 0:1024], xs[:, 0:1024], ALU.mult,
                    waits=[(self.stream_ready, self.stream_ready.val),
                           (e, e.val)],
                    incs=[_inc(e)], accum=self.STAT[:, 16:17])
            self.tt("vector", dumpb, xs[:, 1024:2048], xs[:, 1024:2048], ALU.mult,
                    waits=[(e, e.val)], incs=[_inc(e)],
                    accum=self.STAT[:, 17:18])
            self.red(self.STAT[:, 18:19], xs, ALU.max, absval=True,
                     waits=[(e, e.val)], incs=[_inc(e)])
            self.tt("vector", self.STAT[:, 19:20], self.STAT[:, 16:17],
                    self.STAT[:, 17:18], ALU.add, waits=[(e, e.val)],
                    incs=[_inc(e)])
            self.ts("vector", self.STAT[:, 19:20], self.STAT[:, 19:20], 1.0 / D,
                    EPS_RMS, ALU.mult, ALU.add, waits=[(e, e.val)], incs=[_inc(e)])
            es = self.c("f1_ss")
            self.act(self.STAT[:, 20:21], self.STAT[:, 19:20], AF.Sqrt,
                     waits=[(e, e.val)], incs=[_inc(es)])
            self.recip(self.STAT[:, 21:22], self.STAT[:, 20:21],
                       waits=[(es, es.val)], incs=[_inc(e)])
            self.ts("vector", self.STAT[:, 22:23], self.STAT[:, 21:22], ISQ2048,
                    None, ALU.mult, waits=[(e, e.val)], incs=[_inc(e)])
            self.tt("vector", self.STAT[:, 23:24], self.STAT[:, 18:19],
                    self.STAT[:, 22:23], ALU.mult, waits=[(e, e.val)],
                    incs=[_inc(e)])
            self.ts("vector", self.STAT[:, 23:24], self.STAT[:, 23:24], EPS_Q,
                    None, ALU.max, waits=[(e, e.val)], incs=[_inc(e)])
            self.recip(self.STAT[:, 24:25], self.STAT[:, 23:24],
                       waits=[(e, e.val)], incs=[_inc(e)])
            self.ts("vector", self.STAT[:, 25:26], self.STAT[:, 24:25], 127.0,
                    None, ALU.mult, waits=[(e, e.val)], incs=[_inc(e)])
            self.tt("vector", self.STAT[:, 26:27], self.STAT[:, 25:26],
                    self.STAT[:, 22:23], ALU.mult, waits=[(e, e.val)],
                    incs=[_inc(e)])
            self.ts("vector", self.STAT[:, 27:28], self.STAT[:, 23:24],
                    1.0 / 127.0, None, ALU.mult, waits=[(e, e.val)],
                    incs=[_inc(e)])
            self.tt("vector", self.STAT[:, 27:28], self.STAT[:, 27:28],
                    self.CONSTF[:, 102:103], ALU.mult,
                    waits=[(e, e.val),
                           (self.wp1_scales, self.wp1_scales.val)],
                    incs=[_inc(e)])
            pb4, prd4, pfr4 = self.bank(4)
            self.mm(pb4[0:1, 0:128], self.STAT[:, 27:28], self.IDENTF[:, :],
                    True, True, waits=[(e, e.val), (pfr4, pfr4.val), self.ld],
                    incs=[_inc(prd4)])
            self.cp("vector", self.AROW[0:1, 768 + j * 128:768 + (j + 1) * 128],
                    pb4[0:1, 0:128], waits=[(prd4, prd4.val)],
                    incs=[_inc(e), _inc(pfr4)])
            self.ts("vector", RND, xs, self.STAT[:, 26:27], MAGIC, ALU.mult,
                    ALU.add, waits=[(e, e.val)], incs=[_inc(e)])
            self.ts("vector", XQ, RND, MAGIC, None, ALU.subtract,
                    waits=[(e, e.val)], incs=[_inc(e)])
            self.transpose_tile(self.xT, XQ, j, qdone, waits=[(e, e.val)])
        ns4 = self.c("stream")
        for tt in range(4):
            a1b = self.c("f1_a1b")
            pb4, prd4, pfr4 = self.bank(4)
            self.mm(pb4[:, 0:256], self.ONES1F[0:1, 0:128],
                    self.AROW[0:1, 768 + tt * 256:768 + tt * 256 + 256], True, True,
                    waits=[(qdone, qdone.val), (pfr4, pfr4.val), self.ones],
                    incs=[_inc(prd4)])
            A1B = self.ar2f(AGB_O, 256)
            self.cp("vector", A1B, pb4[:, 0:256], waits=[(prd4, prd4.val)],
                    incs=[_inc(a1b), _inc(pfr4)])
            hdone = self.c("f1_h")
            TMPF = self.ar2f(TMPF_O, 256)
            for og in range(32):
                slot = og % 2
                dstw = A_(self.WB[slot], 0, (4096, 128), (256, 16), (128, 2),
                          (1, 128))
                srcw = A_(self.g3ao, og * 256, (DI, 128), (128 * DI, 16), (128, 2),
                          (1, 128))
                self.dma("sync", dstw, srcw, wdma[slot],
                         waits=[(wfree[slot], wfree[slot].val),
                                (self.g23, 4)])
                wv_ = wdma[slot].val
                for oc in range(2):
                    och = og * 2 + oc
                    bi = och % 2
                    pbk, prd, pfr = self.bank(bi)
                    for k in range(16):
                        self.mm(pbk[:, 0:256],
                                self.WB[slot][:, (k * 2 + oc) * 128:
                                              (k * 2 + oc + 1) * 128],
                                self.xT[:, k * TOK + tt * 256:
                                        k * TOK + tt * 256 + 256],
                                k == 0, k == 15,
                                waits=[(wdma[slot], wv_), (qdone, qdone.val),
                                       (pfr, pfr.val)],
                                incs=([_inc(prd)] if k == 15 else []))
                    hv = self.c("f1_hv")
                    self.tt("vector", TMPF, pbk[:, 0:256], A1B, ALU.mult,
                            waits=[(prd, prd.val), (a1b, a1b.val),
                                   (hdone, hdone.val)],
                            incs=[_inc(hv), _inc(pfr)])
                    self.act(self.ar2b(Z_O + och * 256, 256), TMPF, AF.Gelu,
                             bias=self.CONSTF[:, 32 + och:33 + och],
                             waits=[(hv, hv.val), self.ld],
                             incs=[_inc(hdone)])
                self.P.raw("scalar", lambda e: e.nop(),
                           waits=[(hdone, hdone.val)], incs=[_inc(wfree[slot])])
            qd2 = self.quant_fm(list(range(64)), Z_O, 256, 64, "qf",
                                self.G1S[0:1, 12:13], (hdone, hdone.val),
                                c1_from_rms=True)
            dacc = self.c("f1_dacc")
            bcol = self.c("f1_bcol")
            for mh in range(2):
                pb4, prd4, pfr4 = self.bank(4)
                self.mm(pb4[:, mh:mh + 1],
                        self.AROW[0:1, 512 + mh * 128:512 + (mh + 1) * 128],
                        self.ONES1F[0:1, 0:1], True, True,
                        waits=[(qd2, qd2.val), (pfr4, pfr4.val), self.ones],
                        incs=[_inc(prd4)])
                self.cp("vector", self.STAT[:, 14 + mh:15 + mh], pb4[:, mh:mh + 1],
                        waits=[(prd4, prd4.val)], incs=[_inc(bcol), _inc(pfr4)])
            for og in range(4):
                for kl in range(32):
                    slot = kl % 2
                    dstw = A_(self.WB[slot], 0, (4096, 128), (512, 2), (1, 512))
                    srcw = A_(self.g3bo, kl * 256 * D + og * 512,
                              (D, 128), (128 * D, 2), (1, 512))
                    self.dma("sync", dstw, srcw, wdma[slot],
                             waits=[(wfree[slot], wfree[slot].val),
                                    (self.g23, 5)])
                    wv_ = wdma[slot].val
                    for kc2 in range(2):
                        kc = kl * 2 + kc2
                        for mh in range(2):
                            pbk, prd, pfr = self.bank(mh)
                            ii = ([_inc(wfree[slot])]
                                  if (kc2 == 1 and mh == 1) else [])
                            self.mm(pbk[:, :],
                                    self.ar2b(Z_O + kc * 256 + mh * 128, 128),
                                    self.WB[slot][:, kc2 * 512:kc2 * 512 + 512],
                                    kc == 0, False,
                                    waits=[(wdma[slot], wv_), (qd2, qd2.val),
                                           (pfr, pfr.val)],
                                    incs=ii)
                for mh in range(2):
                    j = tt * 2 + mh
                    pbk, prd, pfr = self.bank(mh)
                    self.mm(pbk[:, :], self.ONES1[0:1, 0:128],
                            self.BROW[0:1, og * 512:og * 512 + 512], False, True,
                            waits=[self.ones, (browc, browc.val)],
                            incs=[_inc(prd)])
                    xs = self.X[:, j * D + og * 512:j * D + og * 512 + 512]
                    self.stt("vector", xs, pbk[:, :], self.STAT[:, 14 + mh:15 + mh],
                             xs, ALU.mult, ALU.add,
                             waits=[(prd, prd.val), (bcol, bcol.val)]
                             + self.tap_gate(),
                             incs=[_inc(dacc), _inc(pfr)])
            if tt == 0:
                self.stream_base = ns4.val
            self.stream_ready = ns4
            for mh in range(2):
                self.ln(tt * 2 + mh, dacc, f"b4_{tt}_{mh}")
        self.stream_ready = ns4

    def emit_store(self):
        st = self.c("outdma")
        for j in range(NT):
            self.dma("sync", self.o_out[j * 128:(j + 1) * 128, :],
                     self.X[:, j * D:(j + 1) * D], st,
                     waits=[(self.stream_ready, self.stream_ready.val)])
        self.P.wait("sync", st)

    def build(self, upto=7):
        self.declare()
        self.phase_load()
        self.stream_ready = self.ld
        self.stream_base = 0
        if upto >= 2:
            self.phase_wprep1()
            self.emit_qkv()
        if upto >= 3:
            self.emit_g4()
        if upto >= 5:
            self.phase_wprep2()
        if upto >= 5:
            self.emit_g2g3_triggers()
        if upto >= 3:
            self.emit_attention()
        if upto >= 4:
            self.emit_wo()
            if TAPS:
                self.emit_tap(1)
        if upto >= 5:
            self.emit_l0ffn()
            if TAPS:
                self.emit_tap(2)
        if upto >= 6:
            self.emit_l1attn()
            if TAPS:
                self.emit_tap(3)
        if upto >= 7:
            self.emit_l1ffn()
        self.emit_store()
        self.P.run()


def build_program(upto=7):
    nc = bass.Bass("TRN2", target_bir_lowering=False, debug=False,
                   dynamic_dma_scratch_size=20480)
    mk = MK(nc)
    mk.build(upto=upto)
    return nc


def kernel(**inputs):
    in_maps, out_index = _host_prep(inputs)
    nc = build_program()
    res = run_bass_kernel_spmd(nc, in_maps, core_ids=list(range(8)))
    out = np.zeros((2, 4096, D), dtype=np.float32)
    for core in range(8):
        b, tok = out_index[core]
        out[b][tok] = res.results[core]["out"]
    return out


# version 11 (final)
# speedup vs baseline: 1.0020x; 1.0020x over previous
import numpy as np
import ml_dtypes
import concourse.bass as bass
import concourse.mybir as mybir
from concourse.bass_utils import run_bass_kernel_spmd

F32 = mybir.dt.float32
BF16 = mybir.dt.bfloat16
AF = mybir.ActivationFunctionType
ALU = mybir.AluOpType
AXX = mybir.AxisListType.X
MAGIC = float(np.float32(1.5 * 2.0 ** 23))
SQ128 = float(1.0 / np.sqrt(128.0))
NH = 16
D = 2048
DI = 8192
NT = 8
TOK = NT * 128
EPS_LN = 1e-5
EPS_RMS = 1e-6
EPS_Q = 1e-5
ISQ2048 = float(1.0 / np.sqrt(2048.0))
ISQ8192 = float(1.0 / np.sqrt(8192.0))
TAPS = False
DBG0 = False

QPRE_O, RTMP_O, RTMP2_O = 0, 2048, 4096
QS_O, VST0_O, VST1_O = 6144, 7168, 8192
XB_O = 16384
KH_O = (0, 8192)
VH_O = (4096, 12288)
QH_O = (20480, 20992)
OST_O = (21504, 22016)
FW_O = 0
Z_O = 4352
TMPF_O = 20992
AGB_O = 21504
MAXT_O = 22016
GT_O = 22272
SG_O = 22528
HFULL_O, HT_O, CB_O = 0, 4096, 8192
GATES_O, UJ_O, VJ_O = 16384, 18432, 20480
UGJ_O, VGJ_O, GO_O, GOTJ_O = 0, 2048, 4096, 6144
CVS_O = 2048
TOUT_O = 16384
G5S_O = 20480
XQ_O = 0
GOT2_O = (2048, 4096)


def _rope_perm():
    p = np.zeros(D, dtype=np.int64)
    for h in range(NH):
        base = h * 128
        for i in range(64):
            p[base + i] = base + 2 * i
            p[base + 64 + i] = base + 2 * i + 1
    return p


def _hadamard(n):
    h = np.array([[1.0]], dtype=np.float32)
    while h.shape[0] < n:
        h = np.block([[h, h], [h, -h]])
    return h.astype(np.float32)


def _host_prep(inputs):
    x = np.asarray(inputs["L0_attn_x"])
    perm = _rope_perm()
    wq = np.ascontiguousarray(np.asarray(inputs["L0_attn_Wq"])[:, perm])
    wk = np.ascontiguousarray(np.asarray(inputs["L0_attn_Wk"])[:, perm])
    bq = np.ascontiguousarray(np.asarray(inputs["L0_attn_bq"])[perm])
    bk = np.ascontiguousarray(np.asarray(inputs["L0_attn_bk"])[perm])
    bq_cols = np.ascontiguousarray(bq.reshape(16, 128).T)
    bk_cols = np.ascontiguousarray(bk.reshape(16, 128).T)
    wgT = np.ascontiguousarray(np.asarray(inputs["L0_ffn_w_gate"]).T)
    wupT = np.ascontiguousarray(np.asarray(inputs["L0_ffn_w_up"]).T)
    wdT = np.ascontiguousarray(np.asarray(inputs["L0_ffn_w_down"]).T)
    wu1T = np.ascontiguousarray(np.asarray(inputs["L1_attn_Wu"]).T)
    wv1T = np.ascontiguousarray(np.asarray(inputs["L1_attn_Wv"]).T)
    wg1T = np.ascontiguousarray(np.asarray(inputs["L1_attn_Wg1"]).T)
    wg2T = np.ascontiguousarray(np.asarray(inputs["L1_attn_Wg2"]).T)
    convw = np.asarray(inputs["L1_attn_conv_w"]).reshape(D, 3)
    conv_rows = np.ascontiguousarray(np.stack(
        [convw[:, 0], convw[:, 1], convw[:, 2],
         np.asarray(inputs["L1_attn_conv_b"])], 0)).astype(np.float32)
    w1T = np.ascontiguousarray(np.asarray(inputs["L1_ffn_w1"]).T)
    w2T = np.ascontiguousarray(np.asarray(inputs["L1_ffn_w2"]).T)
    b1_cols = np.ascontiguousarray(
        np.asarray(inputs["L1_ffn_b1"]).reshape(64, 128).T)
    bias_rows = np.zeros((8, D), dtype=np.float32)
    bias_rows[0] = np.asarray(inputs["L0_attn_bv"])
    bias_rows[1] = np.asarray(inputs["L0_attn_bo"])
    bias_rows[2] = np.asarray(inputs["L1_attn_bu"])
    bias_rows[3] = np.asarray(inputs["L1_attn_bv"])
    bias_rows[4] = np.asarray(inputs["L1_attn_bg2"])
    bias_rows[5] = np.asarray(inputs["L1_ffn_b2"])
    bias_rows[6, :512] = np.asarray(inputs["L1_attn_bg1"])
    bf = ml_dtypes.bfloat16
    h128 = _hadamard(128).astype(bf)
    ident = np.eye(128, dtype=np.float32).astype(bf)
    identf = np.eye(128, dtype=np.float32)
    tri = (np.arange(128)[:, None] <= np.arange(128)[None, :]).astype(np.float32)
    shd = np.eye(128, k=1, dtype=np.float32).astype(bf)
    shu = np.eye(128, k=-1, dtype=np.float32).astype(bf)
    swp = np.block([[np.zeros((64, 64)), np.eye(64)],
                    [np.eye(64), np.zeros((64, 64))]]).astype(np.float32).astype(bf)
    inv_freq = 1.0 / (10000.0 ** (np.arange(0, 128, 2, dtype=np.float64) / 128))

    in_maps, out_index = [], []
    for core in range(8):
        b, g = core // 4, core % 4
        tok = np.concatenate([np.arange(128) + (4 * j + g) * 128 for j in range(NT)])
        out_index.append((b, tok))
        ang = tok.astype(np.float64)[None, :] * inv_freq[:, None]
        ropec = np.concatenate([np.cos(ang)] * 2, 0).astype(np.float32)
        ropes = np.concatenate([-np.sin(ang), np.sin(ang)], 0).astype(np.float32)
        am = np.zeros((128, 512), dtype=np.float32)
        for m in range(4):
            if m < g:
                am[:, m * 128:(m + 1) * 128] = 1.0
            elif m == g:
                am[:, m * 128:(m + 1) * 128] = tri
        wsel = np.zeros((96, 8), dtype=np.float32)
        hselp = np.zeros((96, 1024), dtype=np.float32)
        hseln = np.zeros((96, 1024), dtype=np.float32)
        for r_ in range(4):
            for j_ in range(8):
                gt_row = 4 * j_ + r_
                for jl in range(8):
                    if gt_row < 4 * jl + g:
                        wsel[r_ * 24 + j_, jl] = 1.0
        wselr = np.repeat(wsel, 128, axis=1)
        def gr(gt, which):
            return (gt % 4) * 24 + 8 + (gt // 4) * 2 + which
        for jl in range(8):
            gt = 4 * jl + g
            if gt > 0:
                hselp[gr(gt - 1, 1), jl * 128 + 0] = 1.0
            if gt < 31:
                hseln[gr(gt + 1, 0), jl * 128 + 127] = 1.0
        sh = lambda w: np.ascontiguousarray(
            w[core * (w.shape[0] // 8):(core + 1) * (w.shape[0] // 8)])
        in_maps.append(dict(
            x=np.ascontiguousarray(x[b][tok]), wq=wq, wk=wk,
            wv=np.ascontiguousarray(np.asarray(inputs["L0_attn_Wv"])),
            wo=np.ascontiguousarray(np.asarray(inputs["L0_attn_Wo"])),
            bq_cols=bq_cols, bk_cols=bk_cols, b1_cols=b1_cols, bias_rows=bias_rows,
            wg_sh=sh(wgT), wup_sh=sh(wupT), wd_sh=sh(wdT),
            w1_sh=sh(w1T), w2_sh=sh(w2T),
            wu1T=wu1T, wv1T=wv1T, wg1T=wg1T, wg2T=wg2T, conv_rows=conv_rows,
            ropec=ropec, ropes=ropes, amask=am.astype(bf),
            wselr=wselr.astype(bf), hselp=hselp.astype(bf), hseln=hseln.astype(bf),
            h128=h128, ident=ident, identf=identf, tri=tri.astype(bf),
            shd=shd, shu=shu, swp=swp,
        ))
    return in_maps, out_index



class Ctr:
    __slots__ = ("sem", "val", "name")

    def __init__(self, nc, name):
        self.sem = nc.alloc_semaphore(name)
        self.val = 0
        self.name = name


class Plan:
    ENGINES = ("tensor", "vector", "scalar", "gpsimd", "sync")

    def __init__(self, nc):
        self.nc = nc
        self.ops = {e: [] for e in self.ENGINES}
        self._ctrs = {}
        self._wm = {e: {} for e in self.ENGINES}
        self.n = 0

    def ctr(self, name):
        if name not in self._ctrs:
            self._ctrs[name] = Ctr(self.nc, name)
        return self._ctrs[name]

    def _wl(self, eng, waits):
        wl = []
        for w in waits:
            c, v = (w if isinstance(w, tuple) else (w, None))
            v = c.val if v is None else v
            assert v <= c.val, f"future wait {c.name}: {v} > {c.val}"
            if v <= 0 or self._wm[eng].get(c.name, 0) >= v:
                continue
            self._wm[eng][c.name] = v
            wl.append((c.sem, v))
        return wl

    def raw(self, eng, emit, waits=(), incs=()):
        wl = self._wl(eng, waits)
        il = []
        first_val = None
        for c, n in incs:
            c.val += n
            if first_val is None:
                first_val = c.val
            il.append((c.sem, n))
        self.n += 1 + len(wl) + 2 * max(0, len(il) - 1)

        def fn(e, emit=emit, wl=wl, il=il, fv=first_val):
            for sem, v in wl:
                e.wait_ge(sem, v)
            inst = emit(e)
            if il:
                inst.then_inc(*il[0])
                if len(il) > 1:
                    e.wait_ge(il[0][0], fv)
                    for sem, n in il[1:]:
                        nop = (e.engine_nop() if hasattr(e, "engine_nop")
                               else e.nop())
                        nop.then_inc(sem, n)
        self.ops[eng].append(fn)

    def wait(self, eng, *waits):
        wl = self._wl(eng, waits)
        if wl:
            self.n += len(wl)

            def fn(e, wl=wl):
                for sem, v in wl:
                    e.wait_ge(sem, v)
            self.ops[eng].append(fn)

    def run(self):
        with self.nc.Block() as block:
            @block.tensor
            def _(t):
                for fn in self.ops["tensor"]:
                    fn(t)

            @block.vector
            def _(v):
                for fn in self.ops["vector"]:
                    fn(v)

            @block.scalar
            def _(sc):
                for fn in self.ops["scalar"]:
                    fn(sc)

            @block.gpsimd
            def _(g):
                for fn in self.ops["gpsimd"]:
                    fn(g)

            @block.sync
            def _(s):
                for fn in self.ops["sync"]:
                    fn(s)


def A_(handle, off, *dims):
    return bass.AP(handle, off, [list(d) for d in dims])


def _inc(ctr, n=1):
    return (ctr, n)


class MK:
    def __init__(self, nc):
        self.nc = nc
        self.P = Plan(nc)

    def c(self, name):
        return self.P.ctr(name)

    def dma(self, eng, dst, src, inc=None, waits=(), extra=()):
        incs = ([(inc, 16)] if inc is not None else []) + list(extra)
        self.P.raw(eng, lambda e, d=dst, s=src: e.dma_start(out=d, in_=s),
                   waits=waits, incs=incs)

    def mm(self, out, lhsT, rhs, start, stop, waits=(), incs=()):
        self.P.raw("tensor",
                   lambda t, o=out, l=lhsT, r=rhs, a=start, b=stop:
                   t.matmul(o, l, r, start=a, stop=b), waits=waits, incs=incs)

    def tr(self, out, in_, waits=(), incs=()):
        ident = self.CONST[:, 0:128]
        self.P.raw("tensor", lambda t, o=out, i=in_, d=ident:
                   t.transpose(o, i, d), waits=waits, incs=incs)

    def tt(self, eng, out, a, b, op, waits=(), incs=(), accum=None):
        if accum is None:
            self.P.raw(eng, lambda e, o=out, x=a, y=b, p=op:
                       e.tensor_tensor(o, x, y, p), waits=waits, incs=incs)
        else:
            self.P.raw(eng, lambda e, o=out, x=a, y=b, p=op, ac=accum:
                       e.scalar_tensor_tensor(o, x, 1.0, y, ALU.mult, p,
                                              accum_out=ac),
                       waits=waits, incs=incs)

    def ts(self, eng, out, a, s1, s2, op0, op1=None, waits=(), incs=()):
        if s2 is None and op1 is None:
            self.P.raw(eng, lambda e, o=out, x=a, u=s1, p0=op0:
                       e.tensor_single_scalar(o, x, u, p0), waits=waits, incs=incs)
        else:
            self.P.raw(eng, lambda e, o=out, x=a, u=s1, w=s2, p0=op0, p1=op1:
                       e.tensor_scalar(o, x, u, w, p0, p1), waits=waits, incs=incs)

    def stt(self, eng, out, a, scal, b, op0, op1, waits=(), incs=()):
        self.P.raw(eng, lambda e, o=out, x=a, u=scal, y=b, p0=op0, p1=op1:
                   e.scalar_tensor_tensor(o, x, u, y, p0, p1),
                   waits=waits, incs=incs)

    def cp(self, eng, out, in_, waits=(), incs=()):
        if eng == "scalar":
            self.P.raw(eng, lambda e, o=out, i=in_: e.copy(o, i),
                       waits=waits, incs=incs)
        else:
            self.P.raw(eng, lambda e, o=out, i=in_: e.tensor_copy(o, i),
                       waits=waits, incs=incs)

    def act(self, out, in_, f, bias=0.0, scale=1.0, accum=None, waits=(), incs=()):
        self.P.raw("scalar", lambda e, o=out, i=in_, ff=f, b=bias, s=scale,
                   ac=accum: e.activation(o, i, ff, bias=b, scale=s,
                                          accum_out=ac), waits=waits, incs=incs)

    def red(self, out, in_, op, absval=False, waits=(), incs=()):
        self.P.raw("vector", lambda e, o=out, i=in_, p=op, av=absval:
                   e.tensor_reduce(o, i, AXX, p, apply_absolute_value=av),
                   waits=waits, incs=incs)

    def recip(self, out, in_, waits=(), incs=()):
        self.P.raw("vector", lambda e, o=out, i=in_: e.reciprocal(o, i),
                   waits=waits, incs=incs)

    def memset(self, t, v, waits=(), incs=()):
        self.P.raw("vector", lambda e, o=t, x=v: e.memset(o, x),
                   waits=waits, incs=incs)

    def coll(self, ins, outs, groups, waits=(), incs=()):
        self.P.raw("gpsimd", lambda e, i=ins, o=outs, g=groups:
                   e.collective_compute("AllGather", ALU.bypass,
                                        replica_groups=g, ins=[i], outs=[o]),
                   waits=waits, incs=incs)

    def declare(self):
        nc = self.nc
        DT = nc.dram_tensor

        def EI(name, shape, dt=F32):
            return DT(name, shape, dt, kind="ExternalInput")
        self.i_x = EI("x", [TOK, D])
        self.i_wq = EI("wq", [D, D]); self.i_wk = EI("wk", [D, D])
        self.i_wv = EI("wv", [D, D]); self.i_wo = EI("wo", [D, D])
        self.i_bq_cols = EI("bq_cols", [128, 16])
        self.i_bk_cols = EI("bk_cols", [128, 16])
        self.i_b1_cols = EI("b1_cols", [128, 64])
        self.i_bias_rows = EI("bias_rows", [8, D])
        self.i_wg_sh = EI("wg_sh", [256, DI]); self.i_wup_sh = EI("wup_sh", [256, DI])
        self.i_wd_sh = EI("wd_sh", [1024, D])
        self.i_w1_sh = EI("w1_sh", [256, DI]); self.i_w2_sh = EI("w2_sh", [1024, D])
        self.i_wu1T = EI("wu1T", [D, D]); self.i_wv1T = EI("wv1T", [D, D])
        self.i_wg1T = EI("wg1T", [D, 512]); self.i_wg2T = EI("wg2T", [512, D])
        self.i_conv_rows = EI("conv_rows", [4, D])
        self.i_ropec = EI("ropec", [128, TOK]); self.i_ropes = EI("ropes", [128, TOK])
        self.i_amask = EI("amask", [128, 512], BF16)
        self.i_wselr = EI("wselr", [96, 1024], BF16)
        self.i_hselp = EI("hselp", [96, 1024], BF16)
        self.i_hseln = EI("hseln", [96, 1024], BF16)
        self.i_h128 = EI("h128", [128, 128], BF16)
        self.i_ident = EI("ident", [128, 128], BF16)
        self.i_identf = EI("identf", [128, 128])
        self.i_tri = EI("tri", [128, 128], BF16)
        self.i_shd = EI("shd", [128, 128], BF16)
        self.i_shu = EI("shu", [128, 128], BF16)
        self.i_swp = EI("swp", [128, 128], BF16)
        self.o_out = DT("out", [TOK, D], F32, kind="ExternalOutput")
        if TAPS:
            self.o_taps = {i: DT(f"tap{i}", [TOK, D], F32, kind="ExternalOutput")
                           for i in (1, 2, 3)}
        self.dQ = DT("dQ", [D, TOK], BF16)
        self.dO = DT("dO", [D, TOK], BF16)
        self.dU = DT("dU", [TOK, D], BF16)
        self.dV = DT("dV", [TOK, D], BF16)
        self.dUG = DT("dUG", [TOK, D], BF16)
        self.dVG = DT("dVG", [TOK, D], BF16)
        self.dGOT = DT("dGOT", [D, TOK], BF16)
        self.g1i = DT("g1i", [1, 5], F32); self.g1o = DT("g1o", [8, 5], F32)
        self.g2gi = DT("g2gi", [256, DI], BF16)
        self.g2go = DT("g2go", [D, DI], BF16, addr_space="Shared")
        self.g2ui = DT("g2ui", [256, DI], BF16)
        self.g2uo = DT("g2uo", [D, DI], BF16, addr_space="Shared")
        self.g2di = DT("g2di", [1024, D], BF16)
        self.g2do = DT("g2do", [DI, D], BF16, addr_space="Shared")
        self.g3ai = DT("g3ai", [256, DI], BF16)
        self.g3ao = DT("g3ao", [D, DI], BF16, addr_space="Shared")
        self.g3bi = DT("g3bi", [1024, D], BF16)
        self.g3bo = DT("g3bo", [DI, D], BF16, addr_space="Shared")
        self.g4i = DT("g4i", [2, D * TOK], BF16)
        self.g4oK = DT("g4oK", [4, D * TOK], BF16)
        self.g4oV = DT("g4oV", [4, D * TOK], BF16)
        self.g5i = DT("g5i", [24, D], F32); self.g5o = DT("g5o", [96, D], F32)
        if DBG0:
            self.dbg_fw = DT("dbg_fw", [128, 4352], BF16)
            self.dbg_z = DT("dbg_z", [128, 16640], BF16)
            self.dbg_z2 = DT("dbg_z2", [128, 16640], BF16)
            self.dbg_arow = DT("dbg_arow", [1, 1792], F32)

        A = nc.alloc_sbuf_tensor
        self.X = A("X", [128, NT * D], F32)
        self.xT = A("xT", [128, 16 * TOK], BF16)
        self.AR1 = self.xT
        self.AR2 = A("AR2", [128, 24576], BF16)
        self.WB = [A(f"WB{i}", [128, 4096], BF16) for i in range(2)]
        self.WPST = A("WPST", [128, 4096], F32)
        self.WPQ = A("WPQ", [128, 2048], BF16)
        self.CONST = A("CONST", [128, 1280], BF16)
        self.CONSTF = A("CONSTF", [128, 104], F32)
        self.IDENTF = A("IDENTF", [128, 128], F32)
        self.BROW = A("BROW", [1, D], BF16)
        self.WSELR = A("WSELR", [96, 1024], BF16)
        self.HSELP = A("HSELP", [96, 1024], BF16)
        self.HSELN = A("HSELN", [96, 1024], BF16)
        self.ONES1 = A("ONES1", [1, 128], BF16)
        self.ONESK = A("ONESK", [128, 1], BF16)
        self.ONES1F = A("ONES1F", [1, 128], F32)
        self.ONESKF = A("ONESKF", [128, 1], F32)
        self.STAT = A("STAT", [128, 64], F32)
        self.AROW = A("AROW", [1, 1792], F32)
        self.G1S = A("G1S", [8, 24], F32)
        self.CONSTC = A("CONSTC", [128, 2], F32)
        self.pb = [nc.alloc_psum_tensor(f"pb{i}", [128, 512], F32) for i in range(5)]
        self.psb = nc.alloc_psum_tensor("ps", [128, 512], F32)
        self.pt = [nc.alloc_psum_tensor(f"pt{i}", [128, 1024], BF16)
                   for i in range(2)]
        assert nc.sbuf_bytes_remaining >= 0, nc.sbuf_bytes_remaining
        self.bk_rd = [self.c(f"bk{i}_rd") for i in range(6)]
        self.bk_fr = [self.c(f"bk{i}_fr") for i in range(6)]
        self.pt_rd = [self.c(f"pt{i}_rd") for i in range(2)]
        self.pt_fr = [self.c(f"pt{i}_fr") for i in range(2)]

    def bank(self, i):
        return (self.psb if i == 5 else self.pb[i]), self.bk_rd[i], self.bk_fr[i]

    def ar2b(self, off, n):
        return self.AR2[:, off:off + n]

    def ar2f(self, off, n_f32):
        return self.AR2[:, off:off + 2 * n_f32].bitcast(F32)

    def phase_load(self):
        ld = self.c("ld_init")
        for j in range(NT):
            self.dma("sync", self.X[:, j * D:(j + 1) * D],
                     self.i_x[j * 128:(j + 1) * 128, :], ld)
        self.dma("sync", self.CONST[:, 0:128], self.i_ident[:, :], ld)
        self.dma("sync", self.CONST[:, 128:256], self.i_h128[:, :], ld)
        self.dma("sync", self.CONST[:, 256:384], self.i_tri[:, :], ld)
        self.dma("sync", self.CONST[:, 384:896], self.i_amask[:, :], ld)
        self.dma("sync", self.CONST[:, 896:1024], self.i_shd[:, :], ld)
        self.dma("sync", self.CONST[:, 1024:1152], self.i_shu[:, :], ld)
        self.dma("sync", self.CONST[:, 1152:1280], self.i_swp[:, :], ld)
        self.dma("sync", self.IDENTF[:, :], self.i_identf[:, :], ld)
        ldg = self.c("ld_init_g")
        self.dma("gpsimd", self.ar2b(9216, 1024), self.i_ropec[:, :], ldg)
        self.dma("gpsimd", self.ar2b(10240, 1024), self.i_ropes[:, :], ldg)
        self.dma("sync", self.CONSTF[:, 0:16], self.i_bq_cols[:, :], ld)
        self.dma("sync", self.CONSTF[:, 16:32], self.i_bk_cols[:, :], ld)
        self.dma("sync", self.CONSTF[:, 32:96], self.i_b1_cols[:, :], ld)
        self.dma("sync", self.WSELR[:, :], self.i_wselr[:, :], ld)
        self.dma("sync", self.HSELP[:, :], self.i_hselp[:, :], ld)
        self.dma("sync", self.HSELN[:, :], self.i_hseln[:, :], ld)
        on = self.c("ones_ready")
        self.memset(self.ONES1[:, :], 1.0, incs=[_inc(on)])
        self.memset(self.ONESK[:, :], 1.0, incs=[_inc(on)])
        self.memset(self.ONES1F[:, :], 1.0, incs=[_inc(on)])
        self.memset(self.ONESKF[:, :], 1.0, incs=[_inc(on)])
        self.memset(self.CONSTC[:, 0:1], MAGIC, incs=[_inc(on)])
        self.memset(self.CONSTC[:, 1:2], SQ128, incs=[_inc(on)])
        self.ld, self.ones = ld, on
        self.ldg = ldg

    def transpose_tile(self, dst, src_tile, j, done, waits=()):
        for half in range(2):
            pt, prd, pfr = self.pt[half], self.pt_rd[half], self.pt_fr[half]
            for dci in range(8):
                dc = half * 8 + dci
                self.tr(pt[:, dci * 128:(dci + 1) * 128],
                        src_tile[:, dc * 128:(dc + 1) * 128],
                        waits=list(waits) + [(pfr, pfr.val), self.ld],
                        incs=([_inc(prd)] if dci == 7 else []))
            dv = A_(dst, (half * 8) * TOK + j * 128,
                    (16 * TOK, 128), (TOK, 8), (1, 128))
            sv = A_(pt, 0, (1024, 128), (128, 8), (1, 128))
            self.cp("vector", dv, sv, waits=[(prd, prd.val)],
                    incs=[_inc(done), _inc(pfr)])

    def stream_transpose(self, dst, tag, gate_per_tile=None):
        cast = self.c("st_cast")
        done = self.c("st_tev")
        for j in range(NT):
            w = [self.ld, self.ones]
            if gate_per_tile is not None:
                w.append((gate_per_tile[0], gate_per_tile[1] + j + 1))
            if j >= 2:
                w.append((done, (j - 1) * 2))
            xb = self.ar2b(XB_O + (j % 2) * 2048, 2048)
            self.cp("vector", xb, self.X[:, j * D:(j + 1) * D], waits=w,
                    incs=[_inc(cast)])
            self.transpose_tile(dst, xb, j, done, waits=[(cast, cast.val)])
        return done

    def phase_wprep1(self):
        self.shards = [
            ("g", self.i_wg_sh, self.g2gi, 2, 4),
            ("u", self.i_wup_sh, self.g2ui, 2, 4),
            ("d", self.i_wd_sh, self.g2di, 8, 1),
            ("1", self.i_w1_sh, self.g3ai, 2, 4),
            ("2", self.i_w2_sh, self.g3bi, 8, 1),
        ]
        ldc2 = [self.c("wp1_ld0"), self.c("wp1_ld1")]
        acc = self.c("wp1_acc")
        tsum = self.c("wp1_tsum")
        self.memset(self.STAT[:, 56:61], 0.0, incs=[_inc(tsum)])
        gci = 0
        for mi, (nm, src, _d, nrb, ncb) in enumerate(self.shards):
            cw = src.shape[1] // ncb
            for rb in range(nrb):
                for cb in range(ncb):
                    half = gci % 2
                    st = self.WPST[:, half * 2048:(half + 1) * 2048]
                    w = [(acc, acc.val - 1)] if gci >= 2 else []
                    gci += 1
                    self.dma("scalar", st,
                             src[rb * 128:(rb + 1) * 128, cb * cw:(cb + 1) * cw],
                             ldc2[half], waits=w)
                    self.act(st, st,
                             AF.Abs, accum=self.STAT[:, 32 + half:33 + half],
                             waits=[(ldc2[half], ldc2[half].val),
                                    (tsum, max(1, gci - 1))],
                             incs=[_inc(acc)])
                    self.tt("vector", self.STAT[:, 56 + mi:57 + mi],
                            self.STAT[:, 56 + mi:57 + mi],
                            self.STAT[:, 32 + half:33 + half], ALU.add,
                            waits=[(acc, acc.val), (tsum, tsum.val)],
                            incs=[_inc(tsum)])
        self.wp1_tsum = tsum

    def phase_wprep1_tail(self):
        red = self.c("wp1_red")
        g1dma = self.c("g1in_dma")
        tsum = self.wp1_tsum
        base = 1
        for mi, (nm, src, _d, nrb, ncb) in enumerate(self.shards):
            base += nrb * ncb
            pbk, prd, pfr = self.bank(5)
            self.mm(pbk[0:1, 0:128], self.STAT[:, 56 + mi:57 + mi],
                    self.IDENTF[:, :],
                    True, True, waits=[(tsum, base), self.ld, (pfr, pfr.val)],
                    incs=[_inc(prd)])
            self.red(self.G1S[0:1, mi:mi + 1], pbk[0:1, 0:128], ALU.add,
                     waits=[(prd, prd.val)], incs=[_inc(red), _inc(pfr)])
            self.dma("scalar", self.g1i[0:1, mi:mi + 1],
                     self.G1S[0:1, mi:mi + 1], g1dma, waits=[(red, red.val)])
        g1done = self.c("g1_done")
        self.coll(self.g1i[:, :], self.g1o[:, :], [list(range(8))],
                  waits=[g1dma], incs=[_inc(g1done)])
        g1ld = self.c("ld_init")
        self.dma("sync", self.G1S[0:8, 0:5], self.g1o[:, :], g1ld, waits=[g1done])
        sc = self.c("wp1_scales")
        pbk, prd, pfr = self.bank(5)
        self.mm(pbk[0:1, 0:5], self.ONESKF[0:8, 0:1], self.G1S[0:8, 0:5],
                True, True, waits=[(g1ld, g1ld.val), self.ones, (pfr, pfr.val)],
                incs=[_inc(prd)])
        numel = [D * DI, D * DI, DI * D, D * DI, DI * D]
        for mi in range(5):
            self.ts("vector", self.G1S[0:1, 8 + mi:9 + mi], pbk[0:1, mi:mi + 1],
                    1.0 / numel[mi], EPS_Q, ALU.mult, ALU.max,
                    waits=[(prd, prd.val)], incs=[_inc(sc)])
        self.recip(self.G1S[0:1, 16:21], self.G1S[0:1, 8:13],
                   waits=[(sc, sc.val)], incs=[_inc(sc), _inc(pfr)])
        for mi in range(3):
            pb4, prd4, pfr4 = self.bank(4)
            self.mm(pb4[:, 0:1], self.ONES1F[0:1, 0:128],
                    self.G1S[0:1, 16 + mi:17 + mi], True, True,
                    waits=[(sc, sc.val), (pfr4, pfr4.val)], incs=[_inc(prd4)])
            self.cp("vector", self.CONSTF[:, 96 + mi:97 + mi], pb4[:, 0:1],
                    waits=[(prd4, prd4.val)], incs=[_inc(sc), _inc(pfr4)])
        for mi in range(5):
            pb4, prd4, pfr4 = self.bank(4)
            self.mm(pb4[:, 0:1], self.ONES1F[0:1, 0:128],
                    self.G1S[0:1, 8 + mi:9 + mi], True, True,
                    waits=[(sc, sc.val), (pfr4, pfr4.val)], incs=[_inc(prd4)])
            self.cp("vector", self.CONSTF[:, 99 + mi:100 + mi], pb4[:, 0:1],
                    waits=[(prd4, prd4.val)], incs=[_inc(sc), _inc(pfr4)])
        self.wp1_scales = sc

    def phase_wprep2(self):
        ld2 = self.c("wp2_ld")
        up = self.c("wp2_u")
        qd = self.c("wp2_q")
        outd = self.c("wp2_out")
        self.wp2_trig = []
        gci = 0
        for mi, (nm, src, dstb, nrb, ncb) in enumerate(self.shards):
            tern = mi < 3
            cw = src.shape[1] // ncb
            ci = 0
            for rb in range(nrb):
                for cb in range(ncb):
                    half = gci % 2
                    st = self.WPST[:, half * 2048:(half + 1) * 2048]
                    w = [(outd, max(0, outd.val - 16))] if gci >= 2 else []
                    gci += 1
                    self.dma("gpsimd", st,
                             src[rb * 128:(rb + 1) * 128, cb * cw:(cb + 1) * cw],
                             ld2, waits=w)
                    qtile = self.WPQ[:, 0:2048]
                    w2 = [(outd, outd.val)]
                    if tern:
                        ups = self.c("wp2_us")
                        self.ts("vector", st, st,
                                self.CONSTF[:, 96 + mi:97 + mi], MAGIC,
                                ALU.mult, ALU.add,
                                waits=[(ld2, ld2.val),
                                       (self.wp1_scales, self.wp1_scales.val)],
                                incs=[_inc(ups)])
                        self.ts("vector", st, st, MAGIC, 1.0, ALU.subtract,
                                ALU.min, waits=[(ups, ups.val)], incs=[_inc(up)])
                        self.ts("vector", qtile, st, -1.0, None, ALU.max,
                                waits=[(up, up.val)] + w2, incs=[_inc(qd)])
                        qthis = qd
                    else:
                        qs = self.c("wp2_us")
                        self.ts("vector", st, st, 1e38, 1.0, ALU.mult,
                                ALU.min,
                                waits=[(ld2, ld2.val)],
                                incs=[_inc(qs)])
                        self.ts("vector", qtile, st, -1.0, None, ALU.max,
                                waits=[(qs, qs.val)] + w2, incs=[_inc(qs)])
                        qthis = qs
                    self.dma("gpsimd",
                             dstb[rb * 128:(rb + 1) * 128, cb * cw:(cb + 1) * cw],
                             qtile, outd, waits=[(qthis, qthis.val)])
                    ci += 1
            self.wp2_trig.append(outd.val)
        self.wp2_outd = outd

    def emit_g2g3_triggers(self):
        pairs = [(self.g2gi, self.g2go), (self.g2ui, self.g2uo),
                 (self.g2di, self.g2do), (self.g3ai, self.g3ao),
                 (self.g3bi, self.g3bo)]
        g23 = self.c("g23")
        for mi, (i_, o_) in enumerate(pairs):
            self.coll(i_[:, :], o_[:, :], [list(range(8))],
                      waits=[(self.wp2_outd, self.wp2_trig[mi])], incs=[_inc(g23)])
        self.g23 = g23

    def emit_qkv(self):
        self.x0t = self.stream_transpose(self.xT, "x0T")
        x0t = self.x0t
        wdma = [self.c("wb_d0"), self.c("wb_d1")]
        wfree = [self.c("wb_f0"), self.c("wb_f1")]
        stg = self.c("g4_stage")
        qspill = self.c("q_spill")
        ev = self.c("qkv_ev")
        QPRE = self.ar2f(QPRE_O, 1024)
        QB = self.ar2b(QS_O, 1024)
        RT = self.ar2f(RTMP_O, 1024)
        RT2 = self.ar2f(RTMP2_O, 1024)

        for proj, wsrc, bc0, isq in (("q", self.i_wq, 0, True),
                                     ("k", self.i_wk, 16, False)):
            for ogh in range(8):
                slot = ogh % 2
                dstw = A_(self.WB[slot], 0, (4096, 128), (256, 16), (128, 2), (1, 128))
                srcw = A_(wsrc, ogh * 256, (D, 128), (128 * D, 16), (128, 2), (1, 128))
                self.dma("gpsimd", dstw, srcw, wdma[slot],
                         waits=[(wfree[slot], wfree[slot].val)])
                for oc in range(2):
                    och = ogh * 2 + oc
                    pair = (0, 1) if och % 2 == 0 else (2, 3)
                    for hh in range(2):
                        bi = pair[hh]
                        pbk, prd, pfr = self.bank(bi)
                        for k in range(16):
                            self.mm(pbk[:, :],
                                    self.WB[slot][:, (k * 2 + oc) * 128:
                                                  (k * 2 + oc + 1) * 128],
                                    self.xT[:, k * TOK + hh * 512:
                                            k * TOK + hh * 512 + 512],
                                    k == 0, k == 15,
                                    waits=[(wdma[slot], wdma[slot].val),
                                           (x0t, x0t.val), (pfr, pfr.val)],
                                    incs=([_inc(prd)] if k == 15 else []))
                    for hh in range(2):
                        bi = pair[hh]
                        pbk, prd, pfr = self.bank(bi)
                        self.ts("vector", QPRE[:, hh * 512:hh * 512 + 512],
                                pbk[:, :], self.CONSTF[:, bc0 + och:bc0 + och + 1],
                                None, ALU.add,
                                waits=[(prd, prd.val), self.ld],
                                incs=[_inc(ev), _inc(pfr)])
                    self.cp("vector", QB, QPRE,
                            waits=[(ev, ev.val), (qspill, qspill.val),
                                   (stg, stg.val)], incs=[_inc(ev)])
                    for hh, bi in ((0, 4), (1, 5)):
                        pbk, prd, pfr = self.bank(bi)
                        self.mm(pbk[:, :], self.CONST[:, 1152:1280],
                                QB[:, hh * 512:hh * 512 + 512], True, True,
                                waits=[(ev, ev.val), (pfr, pfr.val)],
                                incs=[_inc(prd)])
                    self.tt("vector", RT, QPRE, self.ar2b(9216, 1024), ALU.mult,
                            waits=[(ev, ev.val), (self.ldg, self.ldg.val)],
                            incs=[_inc(ev)])
                    for hh, bi in ((0, 4), (1, 5)):
                        pbk, prd, pfr = self.bank(bi)
                        self.tt("vector", RT2[:, hh * 512:hh * 512 + 512],
                                pbk[:, :],
                                self.ar2b(10240 + hh * 512, 512), ALU.mult,
                                waits=[(prd, prd.val)], incs=[_inc(ev), _inc(pfr)])
                    self.tt("vector", QB, RT, RT2, ALU.add,
                            waits=[(ev, ev.val)], incs=[_inc(ev)])
                    if isq:
                        self.dma("sync", self.dQ[och * 128:(och + 1) * 128, :],
                                 QB, qspill, waits=[(ev, ev.val)])
                    else:
                        dstap = A_(self.g4i, och * 128 * TOK, (TOK, 128), (1, TOK))
                        self.dma("sync", dstap, QB, stg, waits=[(ev, ev.val)])
                self.P.wait("sync", (ev, ev.val))
                self.P.raw("sync", lambda e: e.nop(), incs=[_inc(wfree[slot])])
            if proj == "q":
                self.phase_wprep1_tail()
            if proj == "k":
                self.kstg = stg.val
        vev = self.c("qkv_ev")
        browc = self.c("ld_init_g")
        self.dma("gpsimd", self.BROW[:, :], self.i_bias_rows[0:1, :], browc)
        for oh in range(8):
            slot = oh % 2
            dstw = A_(self.WB[slot], 0, (4096, 128), (256, 16), (1, 256))
            srcw = A_(self.i_wv, oh * 256, (D, 128), (128 * D, 16), (1, 256))
            self.dma("gpsimd", dstw, srcw, wdma[slot],
                     waits=[(wfree[slot], wfree[slot].val)])
            for j in range(NT):
                vst = self.ar2b((VST0_O if j % 2 == 0 else VST1_O), 256)
                bi = j % 2
                pbk, prd, pfr = self.bank(bi)
                for k in range(16):
                    self.mm(pbk[:, 0:256],
                            self.xT[:, k * TOK + j * 128:k * TOK + (j + 1) * 128],
                            self.WB[slot][:, k * 256:(k + 1) * 256],
                            k == 0, False,
                            waits=[(wdma[slot], wdma[slot].val),
                                   (x0t, x0t.val), (pfr, pfr.val)])
                self.mm(pbk[:, 0:256], self.ONES1[0:1, 0:128],
                        self.BROW[0:1, oh * 256:oh * 256 + 256], False, True,
                        waits=[self.ones, (browc, browc.val)], incs=[_inc(prd)])
                self.cp("vector", vst, pbk[:, 0:256],
                        waits=[(prd, prd.val), (stg, stg.val)],
                        incs=[_inc(vev), _inc(pfr)])
                dstap = A_(self.g4i, D * TOK + j * 128 * D + oh * 256,
                           (D, 128), (1, 256))
                self.dma("sync", dstap, vst, stg, waits=[(vev, vev.val)])
            self.P.raw("sync", lambda e: e.nop(), waits=[(stg, stg.val)],
                       incs=[_inc(wfree[slot])])
        self.g4stage = stg
        self.qspill = qspill

    def emit_g4(self):
        g4k = self.c("g4k_done")
        g4v = self.c("g4v_done")
        self.coll(self.g4i[0:1, :], self.g4oK[:, :],
                  [[0, 1, 2, 3], [4, 5, 6, 7]],
                  waits=[(self.g4stage, self.kstg)], incs=[_inc(g4k)])
        self.coll(self.g4i[1:2, :], self.g4oV[:, :],
                  [[0, 1, 2, 3], [4, 5, 6, 7]],
                  waits=[(self.g4stage, self.g4stage.val)], incs=[_inc(g4v)])
        self.g4kdone, self.g4vdone = g4k, g4v

    def emit_attention(self):
        kld = [self.c("k_ld0"), self.c("k_ld1")]
        vld = [self.c("v_ld0"), self.c("v_ld1")]
        kvfr = [self.c("kv_fr0"), self.c("kv_fr1")]
        qld = [self.c("q_ld0"), self.c("q_ld1")]
        qfr = [self.c("q_fr0"), self.c("q_fr1")]
        aready = self.c("at_exp")
        atfree = self.c("at_free")
        amsk = self.c("at_msk")
        dnm = self.c("at_dnm")
        oev = self.c("at_oev")
        odma = self.c("at_odma")
        nquad = 0
        for h in range(NH):
            hb = h % 2
            for r in range(4):
                src = A_(self.g4oK, r * D * TOK + h * 128 * TOK,
                         (TOK, 128), (1, TOK))
                self.dma("sync", self.ar2b(KH_O[hb] + r * 1024, 1024), src,
                         kld[hb],
                         waits=[(self.g4kdone, 1), (kvfr[hb], kvfr[hb].val)])
            for r in range(4):
                src = A_(self.g4oV, r * D * TOK + h * 128,
                         (D, 128), (128 * D, 8), (1, 128))
                dst = A_(self.AR2, VH_O[hb] + r * 128, (24576, 128), (512, 8),
                         (1, 128))
                self.dma("sync", dst, src, vld[hb],
                         waits=[(self.g4vdone, 1), (kvfr[hb], kvfr[hb].val)])
            KH = self.ar2b(KH_O[hb], 4096)
            VH = self.ar2b(VH_O[hb], 4096)
            for quad in range(2):
                j0 = 4 * quad
                qb = nquad % 2
                QH = self.ar2b(QH_O[qb], 512)
                self.dma("sync", QH,
                         self.dQ[h * 128:(h + 1) * 128, quad * 512:quad * 512 + 512],
                         qld[qb], waits=[(self.qspill, self.qspill.val),
                                         (qfr[qb], qfr[qb].val)])
                a0, d0, m0 = aready.val, dnm.val, amsk.val
                for t in range(32):
                    koff = (t % 4) * 1024 + (t // 4) * 128
                    bi = t % 4
                    pbk, prd, pfr = self.bank(bi)
                    gidx = 32 * nquad + t
                    w = [(kld[hb], kld[hb].val), (qld[qb], qld[qb].val),
                         (atfree, max(0, nquad - 1))]
                    if gidx >= 4:
                        w.append((aready, gidx - 3))
                    self.mm(pbk[:, :], KH[:, koff:koff + 128], QH, True, True,
                            waits=w, incs=[_inc(prd)])
                    self.act(self.AR1[:, t * 512:(t + 1) * 512], pbk[:, :], AF.Exp,
                             scale=self.CONSTC[:, 1:2],
                             waits=[(prd, prd.val), (atfree, max(0, nquad - 1))],
                             incs=[_inc(aready)])
                pb4, prd4, pfr4 = self.bank(4)
                for t in range(32):
                    self.mm(pb4[0:1, :], self.ONESK[:, :],
                            self.AR1[:, t * 512:(t + 1) * 512], t == 0, t == 31,
                            waits=[(aready, a0 + t + 1), self.ones,
                                   (pfr4, pfr4.val)],
                            incs=[_inc(dnm)] + ([_inc(prd4)] if t == 31 else []))
                for jj in range(4):
                    for m in range(4):
                        t = 4 * (j0 + jj) + m
                        sl = self.AR1[:, t * 512 + jj * 128:t * 512 + (jj + 1) * 128]
                        self.tt("vector", sl, sl,
                                self.CONST[:, 384 + m * 128:384 + (m + 1) * 128],
                                ALU.mult,
                                waits=[(dnm, d0 + t + 1), self.ld],
                                incs=[_inc(amsk)])
                psk, prd5, pfr5 = self.bank(5)
                nav = 4 * j0 + 16
                for t in range(nav):
                    c0 = 128 * max(0, min(4, t // 4 - j0))
                    w = [(aready, a0 + t + 1), (pfr5, pfr5.val),
                         (vld[hb], vld[hb].val)]
                    if j0 <= t // 4 <= j0 + 3:
                        w.append((amsk, m0 + (t // 4 - j0) * 4 + (t % 4) + 1))
                    self.mm(psk[:, c0:512], VH[:, t * 128:(t + 1) * 128],
                            self.AR1[:, t * 512 + c0:(t + 1) * 512],
                            t == 0, t == nav - 1, waits=w,
                            incs=(([_inc(prd5), _inc(atfree)])
                                  if t == nav - 1 else []))
                self.recip(self.AROW[0:1, 0:512], pb4[0:1, :],
                           waits=[(prd4, prd4.val)], incs=[_inc(oev), _inc(pfr4)])
                self.mm(pb4[:, :], self.ONES1F[0:1, 0:128], self.AROW[0:1, 0:512],
                        True, True, waits=[(oev, oev.val), (pfr4, pfr4.val)],
                        incs=[_inc(prd4)])
                OST = self.ar2b(OST_O[qb], 512)
                RBF = self.ar2b(18432 + (qb % 2) * 1024, 1024).bitcast(F32)
                self.cp("vector", RBF, pb4[:, :],
                        waits=[(prd4, prd4.val), (oev, oev.val)],
                        incs=[_inc(oev), _inc(pfr4)])
                self.tt("vector", OST, psk[:, :], RBF, ALU.mult,
                        waits=[(oev, oev.val), (prd5, prd5.val),
                               (odma, odma.val)],
                        incs=[_inc(oev), _inc(pfr5)])
                self.dma("sync", self.dO[h * 128:(h + 1) * 128,
                                         quad * 512:quad * 512 + 512], OST,
                         odma, waits=[(oev, oev.val)],
                         extra=[_inc(qfr[qb])])
                nquad += 1
            self.P.raw("sync", lambda e: e.nop(), waits=[(odma, odma.val)],
                       incs=[_inc(kvfr[hb])])
        self.odma = odma
        self.atfree = atfree

    def ln(self, j, ready, tag):
        eS = self.c("ln_s")
        e = self.c("ln_v")
        src = self.X[:, j * D:(j + 1) * D]
        wq = self.c("wp2_out")
        for q in range(4):
            sl = src[:, q * 512:(q + 1) * 512]
            self.tt("vector", self.WPST[:, (q % 2) * 512:(q % 2) * 512 + 512],
                    sl, sl, ALU.mult,
                    waits=[(ready, ready.val), (wq, wq.val), (eS, eS.val),
                           (e, e.val)],
                    incs=[_inc(e)], accum=self.STAT[:, 2 + q:3 + q])
        self.act(src[:, 0:1024], src[:, 0:1024], AF.Identity,
                 accum=self.STAT[:, 0:1],
                 waits=[(ready, ready.val), (e, e.val)], incs=[_inc(eS)])
        self.act(src[:, 1024:2048], src[:, 1024:2048], AF.Identity,
                 accum=self.STAT[:, 1:2],
                 waits=[(ready, ready.val), (eS, eS.val)], incs=[_inc(eS)])
        self.tt("vector", self.STAT[:, 6:7], self.STAT[:, 0:1], self.STAT[:, 1:2],
                ALU.add, waits=[(e, e.val), (eS, eS.val)], incs=[_inc(e)])
        self.ts("vector", self.STAT[:, 6:7], self.STAT[:, 6:7], -1.0 / D, None,
                ALU.mult, waits=[(e, e.val)], incs=[_inc(e)])
        self.tt("vector", self.STAT[:, 7:8], self.STAT[:, 2:3], self.STAT[:, 3:4],
                ALU.add, waits=[(e, e.val)], incs=[_inc(e)])
        self.tt("vector", self.STAT[:, 8:9], self.STAT[:, 4:5], self.STAT[:, 5:6],
                ALU.add, waits=[(e, e.val)], incs=[_inc(e)])
        self.tt("vector", self.STAT[:, 7:8], self.STAT[:, 7:8], self.STAT[:, 8:9],
                ALU.add, waits=[(e, e.val)], incs=[_inc(e)])
        self.ts("vector", self.STAT[:, 7:8], self.STAT[:, 7:8], 1.0 / D, EPS_LN,
                ALU.mult, ALU.add, waits=[(e, e.val)], incs=[_inc(e)])
        self.tt("vector", self.STAT[:, 8:9], self.STAT[:, 6:7], self.STAT[:, 6:7],
                ALU.mult, waits=[(e, e.val)], incs=[_inc(e)])
        self.tt("vector", self.STAT[:, 7:8], self.STAT[:, 7:8], self.STAT[:, 8:9],
                ALU.subtract, waits=[(e, e.val)], incs=[_inc(e)])
        self.act(self.STAT[:, 9:10], self.STAT[:, 7:8], AF.Sqrt,
                 waits=[(e, e.val)], incs=[_inc(eS)])
        self.recip(self.STAT[:, 10:11], self.STAT[:, 9:10],
                   waits=[(eS, eS.val)], incs=[_inc(e)])
        self.ts("vector", src, src, self.STAT[:, 6:7], self.STAT[:, 10:11],
                ALU.add, ALU.mult, waits=[(e, e.val)],
                incs=[_inc(self.stream_ready)])

    def emit_wo(self):
        old = self.c("wo_old")
        src = A_(self.dO, 0, (TOK, 128), (128 * TOK, 16), (1, TOK))
        dst = A_(self.AR1, 0, (16 * TOK, 128), (TOK, 16), (1, TOK))
        self.dma("sync", dst, src, old,
                 waits=[(self.odma, self.odma.val), (self.atfree, self.atfree.val)])
        wdma = [self.c("wb_d0"), self.c("wb_d1")]
        wfree = [self.c("wb_f0"), self.c("wb_f1")]
        acc = self.c("wo_acc")
        browc = self.c("ld_init_g")
        self.dma("gpsimd", self.BROW[:, :], self.i_bias_rows[1:2, :], browc,
                 waits=[(self.g4stage, self.g4stage.val)])
        for og in range(4):
            for half in range(2):
                dstw = A_(self.WB[half], 0, (4096, 128), (256, 16), (1, 256))
                srcw = A_(self.i_wo, og * 512 + half * 256,
                          (D, 128), (128 * D, 16), (1, 256))
                self.dma("gpsimd", dstw, srcw, wdma[half],
                         waits=[(wfree[half], wfree[half].val)])
            for j in range(NT):
                bi = j % 2
                pbk, prd, pfr = self.bank(bi)
                for k in range(16):
                    lhs = self.AR1[:, k * TOK + j * 128:k * TOK + (j + 1) * 128]
                    for half in range(2):
                        self.mm(pbk[:, half * 256:half * 256 + 256], lhs,
                                self.WB[half][:, k * 256:(k + 1) * 256],
                                k == 0 and half == 0, False,
                                waits=[(old, old.val),
                                       (wdma[half], wdma[half].val),
                                       (pfr, pfr.val)])
                self.mm(pbk[:, :], self.ONES1[0:1, 0:128],
                        self.BROW[0:1, og * 512:og * 512 + 512], False, True,
                        waits=[self.ones, (browc, browc.val)], incs=[_inc(prd)])
                xs = self.X[:, j * D + og * 512:j * D + og * 512 + 512]
                self.stt("vector", xs, pbk[:, :], 1.0, xs, ALU.mult, ALU.add,
                         waits=[(prd, prd.val)], incs=[_inc(acc), _inc(pfr)])
            for half in range(2):
                self.P.raw("vector", lambda e: e.nop(),
                           waits=[(acc, acc.val)], incs=[_inc(wfree[half])])
        self.stream_base = self.c("stream").val
        self.stream_ready = self.c("stream")
        for j in range(NT):
            self.ln(j, acc, f"b1_{j}")

    def emit_tap(self, i):
        tc = self.c("tapdma")
        for j in range(NT):
            self.dma("scalar", self.o_taps[i][j * 128:(j + 1) * 128, :],
                     self.X[:, j * D:(j + 1) * D], tc,
                     waits=[(self.stream_ready, self.stream_ready.val)])

    def tap_gate(self):
        if not TAPS:
            return []
        tc = self.c("tapdma")
        return [(tc, tc.val)]

    def fwht_stages(self, n, base, width, first_waits, done):
        phys = list(range(n))
        spare = n
        hc = 1
        first = True
        while hc < n:
            for g0 in range(0, n, 2 * hc):
                for i in range(hc):
                    a, b = g0 + i, g0 + hc + i
                    pa, pb_ = phys[a], phys[b]
                    ta = self.ar2b(base + pa * width, width)
                    tb = self.ar2b(base + pb_ * width, width)
                    ts_ = self.ar2b(base + spare * width, width)
                    w = (list(first_waits) if first else []) + [(done, done.val)]
                    self.tt("vector", ts_, ta, tb, ALU.add, waits=w,
                            incs=[_inc(done)])
                    self.tt("vector", ta, ta, tb, ALU.subtract,
                            waits=[(done, done.val)], incs=[_inc(done)])
                    phys[a], phys[b], spare = spare, pa, pb_
            first = False
            hc *= 2
        return phys

    def quant_fm(self, slotmap, base, width, nslots, tag, sw_cell,
                 ready, c1_const=None, c1_from_rms=False):
        assert width == 256
        e = self.c(tag + "_e")
        qd = self.c(tag + "_qd")
        MT = self.ar2b(MAXT_O, 256)
        MN = self.ar2b(GT_O, 256)
        s0 = self.ar2b(base + slotmap[0] * 256, 256)
        self.cp("vector", MT, s0, waits=[ready, (e, e.val)], incs=[_inc(e)])
        self.cp("vector", MN, s0, waits=[(e, e.val)], incs=[_inc(e)])
        for i in range(1, nslots):
            si = self.ar2b(base + slotmap[i] * 256, 256)
            self.tt("vector", MT, MT, si, ALU.max,
                    waits=[(e, e.val)], incs=[_inc(e)])
            self.tt("vector", MN, MN, si, ALU.min,
                    waits=[(e, e.val)], incs=[_inc(e)])
        self.ts("vector", MN, MN, -1.0, None, ALU.mult,
                waits=[(e, e.val)], incs=[_inc(e)])
        self.tt("vector", MT, MT, MN, ALU.max,
                waits=[(e, e.val)], incs=[_inc(e)])
        if c1_from_rms:
            pb4, prd4, pfr4 = self.bank(4)
            TMPB = self.ar2b(TMPF_O, 256)
            sq = self.c(tag + "_sq")
            sqm = self.c(tag + "_sqm")
            for i in range(nslots):
                si = self.ar2b(base + slotmap[i] * 256, 256)
                self.tt("vector", TMPB, si, si, ALU.mult,
                        waits=[ready, (pfr4, pfr4.val), (sqm, sqm.val)],
                        incs=[_inc(sq)])
                self.mm(pb4[0:1, 0:256], self.ONESK[:, 0:1], TMPB, i == 0,
                        i == nslots - 1,
                        waits=[(sq, sq.val), self.ones, (pfr4, pfr4.val)],
                        incs=[_inc(sqm)] + ([_inc(prd4)] if i == nslots - 1 else []))
            self.ts("vector", self.AROW[0:1, 768:1024], pb4[0:1, 0:256],
                    1.0 / DI, EPS_RMS, ALU.mult, ALU.add,
                    waits=[(prd4, prd4.val)], incs=[_inc(e), _inc(pfr4)])
            qfs = self.c("f1_ss")
            self.P.raw("scalar", lambda sc, o=self.AROW[0:1, 768:1024]:
                       sc.activation(o, o, AF.Sqrt, bias=0.0, scale=1.0),
                       waits=[(e, e.val)], incs=[_inc(qfs)])
            self.recip(self.AROW[0:1, 768:1024], self.AROW[0:1, 768:1024],
                       waits=[(qfs, qfs.val)], incs=[_inc(e)])
            self.ts("vector", self.AROW[0:1, 768:1024], self.AROW[0:1, 768:1024],
                    ISQ8192, None, ALU.mult, waits=[(e, e.val)], incs=[_inc(e)])
        for half in range(2):
            pt, prd, pfr = self.pt[0], self.pt_rd[0], self.pt_fr[0]
            self.tr(pt[:, 0:128], MT[:, half * 128:(half + 1) * 128],
                    waits=[(e, e.val), (pfr, pfr.val)], incs=[_inc(prd)])
            self.red(self.STAT[:, 10:11], pt[:, 0:128], ALU.max,
                     waits=[(prd, prd.val)], incs=[_inc(e), _inc(pfr)])
            pb4, prd4, pfr4 = self.bank(4)
            self.mm(pb4[0:1, half * 128:(half + 1) * 128], self.STAT[:, 10:11],
                    self.IDENTF[:, :], True, True,
                    waits=[(e, e.val), (pfr4, pfr4.val), self.ld],
                    incs=[_inc(prd4)])
            self.cp("vector", self.AROW[0:1, half * 128:(half + 1) * 128],
                    pb4[0:1, half * 128:(half + 1) * 128],
                    waits=[(prd4, prd4.val)], incs=[_inc(e), _inc(pfr4)])
        if c1_from_rms:
            self.tt("vector", self.AROW[0:1, 0:256], self.AROW[0:1, 0:256],
                    self.AROW[0:1, 768:1024], ALU.mult, waits=[(e, e.val)],
                    incs=[_inc(e)])
            self.ts("vector", self.AROW[0:1, 0:256], self.AROW[0:1, 0:256],
                    EPS_Q, None, ALU.max, waits=[(e, e.val)], incs=[_inc(e)])
        else:
            self.ts("vector", self.AROW[0:1, 0:256], self.AROW[0:1, 0:256],
                    c1_const, EPS_Q, ALU.mult, ALU.max, waits=[(e, e.val)],
                    incs=[_inc(e)])
        self.P.raw("vector", lambda v, o=self.AROW[0:1, 512:768],
                   i=self.AROW[0:1, 0:256], s=sw_cell:
                   v.tensor_scalar(o, i, s, 1.0 / 127.0, ALU.mult, ALU.mult),
                   waits=[(e, e.val), (self.wp1_scales, self.wp1_scales.val)],
                   incs=[_inc(e)])
        self.recip(self.AROW[0:1, 1280:1536], self.AROW[0:1, 0:256],
                   waits=[(e, e.val)], incs=[_inc(e)])
        if c1_from_rms:
            self.tt("vector", self.AROW[0:1, 1280:1536], self.AROW[0:1, 1280:1536],
                    self.AROW[0:1, 768:1024], ALU.mult, waits=[(e, e.val)],
                    incs=[_inc(e)])
            self.ts("vector", self.AROW[0:1, 1280:1536], self.AROW[0:1, 1280:1536],
                    127.0, None, ALU.mult, waits=[(e, e.val)], incs=[_inc(e)])
        else:
            self.ts("vector", self.AROW[0:1, 1280:1536], self.AROW[0:1, 1280:1536],
                    127.0 * c1_const, None, ALU.mult, waits=[(e, e.val)],
                    incs=[_inc(e)])
        pb4, prd4, pfr4 = self.bank(4)
        self.mm(pb4[:, 0:256], self.ONES1F[0:1, 0:128], self.AROW[0:1, 1280:1536],
                True, True, waits=[(e, e.val), (pfr4, pfr4.val), self.ones],
                incs=[_inc(prd4)])
        TMPF = self.ar2f(TMPF_O, 256)
        for i in range(nslots):
            si = self.ar2b(base + slotmap[i] * 256, 256)
            self.tt("vector", TMPF, si, pb4[:, 0:256], ALU.mult,
                    waits=[(prd4, prd4.val), (e, e.val), (qd, qd.val)],
                    incs=[_inc(e)])
            self.ts("vector", si, TMPF, MAGIC, MAGIC, ALU.add, ALU.subtract,
                    waits=[(e, e.val)], incs=[_inc(qd)])
        self.P.raw("vector", lambda e_: e_.engine_nop(), waits=[(qd, qd.val)],
                   incs=[_inc(pfr4)])
        return qd

    def emit_l0ffn(self):
        x1t = self.stream_transpose(self.xT, "x1T",
                                    gate_per_tile=(self.stream_ready,
                                                   self.stream_base))
        wdma = [self.c("ws_d0"), self.c("ws_d1")]
        wfree = [self.c("wb_f0"), self.c("wb_f1")]
        ns2 = self.c("stream")
        for tt in range(4):
            hmm = self.c("f0_h")
            for dc in range(16):
                bi = dc % 2
                pbk, prd, pfr = self.bank(bi)
                self.mm(pbk[:, 0:256], self.CONST[:, 128:256],
                        self.xT[:, dc * TOK + tt * 256:dc * TOK + tt * 256 + 256],
                        True, True,
                        waits=[(x1t, x1t.val), self.ld, (pfr, pfr.val)],
                        incs=[_inc(prd)])
                self.cp("vector", self.ar2b(FW_O + dc * 256, 256), pbk[:, 0:256],
                        waits=[(prd, prd.val)], incs=[_inc(hmm), _inc(pfr)])
            bd = self.c("f0_bf")
            smap = self.fwht_stages(16, FW_O, 256, [(hmm, hmm.val)], bd)
            qd = self.quant_fm(smap, FW_O, 256, 16, "qf",
                               self.G1S[0:1, 8:9], (bd, bd.val),
                               c1_const=ISQ2048)
            self.P.raw("vector", lambda v, o=self.AROW[0:1, 1536:1792],
                       i=self.AROW[0:1, 0:256], s=self.G1S[0:1, 9:10]:
                       v.tensor_scalar(o, i, s, 1.0 / 127.0, ALU.mult, ALU.mult),
                       waits=[(qd, qd.val),
                              (self.wp1_scales, self.wp1_scales.val)],
                       incs=[_inc(qd)])
            if DBG0 and tt == 0:
                self.smap_dbg = list(smap)
                self.dma("sync", self.dbg_fw[:, :], self.ar2b(FW_O, 4352),
                         self.c("dbgd"), waits=[(qd, qd.val)])
                self.dma("sync", self.dbg_arow[0:1, 0:256],
                         self.AROW[0:1, 0:256], self.c("dbgd"),
                         waits=[(qd, qd.val)])
                self.dma("sync", self.dbg_arow[0:1, 512:768],
                         self.AROW[0:1, 512:768], self.c("dbgd"),
                         waits=[(qd, qd.val)])
                self.dma("sync", self.dbg_arow[0:1, 1280:1536],
                         self.AROW[0:1, 1280:1536], self.c("dbgd"),
                         waits=[(qd, qd.val)])
            agb = self.c("f0_agb")
            pb4, prd4, pfr4 = self.bank(4)
            self.mm(pb4[:, 0:256], self.ONES1F[0:1, 0:128], self.AROW[0:1, 512:768],
                    True, True, waits=[(qd, qd.val), (pfr4, pfr4.val)],
                    incs=[_inc(prd4)])
            AGB = self.ar2f(AGB_O, 256)
            self.cp("vector", AGB, pb4[:, 0:256], waits=[(prd4, prd4.val)],
                    incs=[_inc(agb), _inc(pfr4)])
            zready = self.c("f0_z")
            GTt = self.ar2b(GT_O, 256)
            SGt = self.ar2b(SG_O, 256)
            for og in range(32):
                wv = [0, 0]
                for mat in range(2):
                    gsrc = self.g2go if mat == 0 else self.g2uo
                    dstw = A_(self.WB[mat], 0, (4096, 128), (256, 16), (128, 2),
                              (1, 128))
                    srcw = A_(gsrc, og * 256, (DI, 128), (128 * DI, 16), (128, 2),
                              (1, 128))
                    self.dma("sync", dstw, srcw, wdma[mat],
                             waits=[(wfree[mat], wfree[mat].val),
                                    (self.g23, mat + 1)])
                    wv[mat] = wdma[mat].val
                for oc in range(2):
                    och = og * 2 + oc
                    gb, ub = (0, 1) if och % 2 == 0 else (2, 3)
                    for mat, bi in ((0, gb), (1, ub)):
                        pbk, prd, pfr = self.bank(bi)
                        for k in range(16):
                            self.mm(pbk[:, 0:256],
                                    self.WB[mat][:, (k * 2 + oc) * 128:
                                                 (k * 2 + oc + 1) * 128],
                                    self.ar2b(FW_O + smap[k] * 256, 256),
                                    k == 0, k == 15,
                                    waits=[(wdma[mat], wv[mat]), (qd, qd.val),
                                           (pfr, pfr.val)],
                                    incs=([_inc(prd)] if k == 15 else []))
                    gbk, gprd, gpfr = self.bank(gb)
                    ubk, uprd, upfr = self.bank(ub)
                    zs = self.c("f0_zs")
                    self.tt("vector", GTt, gbk[:, 0:256], AGB, ALU.mult,
                            waits=[(gprd, gprd.val), (agb, agb.val),
                                   (zs, zs.val)],
                            incs=[_inc(zready), _inc(gpfr)])
                    self.act(SGt, GTt, AF.Silu, waits=[(zready, zready.val)],
                             incs=[_inc(zs)])
                    self.tt("vector", self.ar2b(Z_O + och * 256, 256), SGt,
                            ubk[:, 0:256], ALU.mult,
                            waits=[(zs, zs.val), (uprd, uprd.val)]
                            + getattr(self, "_dbgz_gate", []),
                            incs=[_inc(zready), _inc(upfr)])
                for mat in range(2):
                    self.P.raw("vector", lambda e: e.nop(),
                               waits=[(zready, zready.val)],
                               incs=[_inc(wfree[mat])])
            dbg_gate = []
            if DBG0 and tt == 0:
                dd = self.c("dbgd")
                self.dma("sync", self.dbg_z[:, :], self.ar2b(Z_O, 16640),
                         dd, waits=[(zready, zready.val)])
                dbg_gate = [(dd, dd.val)]
                self._dbgz_gate = dbg_gate
            h2 = self.c("f0_h2")
            for sl in range(64):
                bi = sl % 2
                pbk, prd, pfr = self.bank(bi)
                zt = self.ar2b(Z_O + sl * 256, 256)
                self.mm(pbk[:, 0:256], self.CONST[:, 128:256], zt, True, True,
                        waits=[(zready, zready.val), (pfr, pfr.val)] + dbg_gate,
                        incs=[_inc(prd)])
                self.cp("vector", zt, pbk[:, 0:256], waits=[(prd, prd.val)],
                        incs=[_inc(h2), _inc(pfr)])
            bd2 = self.c("f0_bf2")
            smap2 = self.fwht_stages(64, Z_O, 256, [(h2, h2.val)], bd2)
            qd2 = self.quant_fm(smap2, Z_O, 256, 64, "qf",
                                self.G1S[0:1, 10:11], (bd2, bd2.val),
                                c1_const=ISQ8192)
            if DBG0 and tt == 0:
                self.smap2_dbg = list(smap2)
                dd = self.c("dbgd")
                self.dma("sync", self.dbg_z2[:, :], self.ar2b(Z_O, 16640),
                         dd, waits=[(qd2, qd2.val)])
                self._dbgz_gate = [(dd, dd.val)]
            self.tt("vector", self.AROW[0:1, 512:768], self.AROW[0:1, 512:768],
                    self.AROW[0:1, 1536:1792], ALU.mult,
                    waits=[(qd2, qd2.val)], incs=[_inc(qd2)])
            dacc = self.c("f0_dacc")
            bcol = self.c("f0_bcol")
            for mh in range(2):
                pb4, prd4, pfr4 = self.bank(4)
                self.mm(pb4[:, mh:mh + 1],
                        self.AROW[0:1, 512 + mh * 128:512 + (mh + 1) * 128],
                        self.ONES1F[0:1, 0:1], True, True,
                        waits=[(qd2, qd2.val), (pfr4, pfr4.val), self.ones],
                        incs=[_inc(prd4)])
                self.cp("vector", self.STAT[:, 12 + mh:13 + mh], pb4[:, mh:mh + 1],
                        waits=[(prd4, prd4.val)], incs=[_inc(bcol), _inc(pfr4)])
            for og in range(4):
                for kl in range(32):
                    slot = kl % 2
                    dstw = A_(self.WB[slot], 0, (4096, 128), (512, 2), (1, 512))
                    srcw = A_(self.g2do, kl * 256 * D + og * 512,
                              (D, 128), (128 * D, 2), (1, 512))
                    self.dma("sync", dstw, srcw, wdma[slot],
                             waits=[(wfree[slot], wfree[slot].val),
                                    (self.g23, 3)])
                    wv_ = wdma[slot].val
                    for kc2 in range(2):
                        kc = kl * 2 + kc2
                        for mh in range(2):
                            pbk, prd, pfr = self.bank(mh)
                            last_of_slot = (kc2 == 1 and mh == 1)
                            ii = ([_inc(prd)] if kc == 63 else [])
                            if last_of_slot:
                                ii = ii + [_inc(wfree[slot])]
                            self.mm(pbk[:, :],
                                    self.ar2b(Z_O + smap2[kc] * 256 + mh * 128, 128),
                                    self.WB[slot][:, kc2 * 512:kc2 * 512 + 512],
                                    kc == 0, kc == 63,
                                    waits=[(wdma[slot], wv_), (qd2, qd2.val),
                                           (pfr, pfr.val)],
                                    incs=ii)
                for mh in range(2):
                    j = tt * 2 + mh
                    pbk, prd, pfr = self.bank(mh)
                    xs = self.X[:, j * D + og * 512:j * D + og * 512 + 512]
                    self.stt("vector", xs, pbk[:, :], self.STAT[:, 12 + mh:13 + mh],
                             xs, ALU.mult, ALU.add,
                             waits=[(prd, prd.val), (bcol, bcol.val)]
                             + self.tap_gate(),
                             incs=[_inc(dacc), _inc(pfr)])
            if tt == 0:
                self.stream_base = ns2.val
            self.stream_ready = ns2
            for mh in range(2):
                self.ln(tt * 2 + mh, dacc, f"b2_{tt}_{mh}")
        self.stream_ready = ns2

    def emit_l1attn(self):
        x2t = self.stream_transpose(self.xT, "x2T",
                                    gate_per_tile=(self.stream_ready,
                                                   self.stream_base))
        wdma = [self.c("wb_d0"), self.c("wb_d1")]
        wfree = [self.c("wb_f0"), self.c("wb_f1")]
        spill = self.c("a1_spill")
        aev = self.c("qkv_ev")
        browc = self.c("ld_init_g")
        for mat, wsrc, ddst, brow in ((0, self.i_wu1T, self.dU, 2),
                                      (1, self.i_wv1T, self.dV, 3)):
            self.dma("gpsimd", self.BROW[:, :], self.i_bias_rows[brow:brow + 1, :],
                     browc, waits=[(aev, aev.val),
                                   (self.c("wo_acc"), self.c("wo_acc").val)])
            for oh in range(8):
                slot = oh % 2
                dstw = A_(self.WB[slot], 0, (4096, 128), (256, 16), (1, 256))
                srcw = A_(wsrc, oh * 256, (D, 128), (128 * D, 16), (1, 256))
                self.dma("gpsimd", dstw, srcw, wdma[slot],
                         waits=[(wfree[slot], wfree[slot].val)])
                for j in range(NT):
                    bi = j % 2
                    pbk, prd, pfr = self.bank(bi)
                    for k in range(16):
                        self.mm(pbk[:, 0:256],
                                self.xT[:, k * TOK + j * 128:k * TOK + (j + 1) * 128],
                                self.WB[slot][:, k * 256:(k + 1) * 256],
                                k == 0, False,
                                waits=[(wdma[slot], wdma[slot].val), (x2t, x2t.val),
                                       (pfr, pfr.val)])
                    self.mm(pbk[:, 0:256], self.ONES1[0:1, 0:128],
                            self.BROW[0:1, oh * 256:oh * 256 + 256],
                            False, True, waits=[self.ones, (browc, browc.val)],
                            incs=[_inc(prd)])
                    ust = self.ar2b(UJ_O + (j % 2) * 2048, 256)
                    self.cp("vector", ust, pbk[:, 0:256],
                            waits=[(prd, prd.val), (spill, spill.val)],
                            incs=[_inc(aev), _inc(pfr)])
                    self.dma("sync",
                             ddst[j * 128:(j + 1) * 128, oh * 256:oh * 256 + 256],
                             ust, spill, waits=[(aev, aev.val)])
                self.P.raw("sync", lambda e: e.nop(), waits=[(spill, spill.val)],
                           incs=[_inc(wfree[slot])])
        hred = self.c("a1_h")
        self.dma("gpsimd", self.BROW[:, :], self.i_bias_rows[6:7, :], browc,
                 waits=[(aev, aev.val)])
        for half in range(2):
            dstw = A_(self.WB[half], 0, (4096, 128), (256, 16), (1, 256))
            srcw = A_(self.i_wg1T, half * 256, (512, 128), (128 * 512, 16), (1, 256))
            self.dma("gpsimd", dstw, srcw, wdma[half],
                     waits=[(wfree[half], wfree[half].val)])
        HF = self.ar2b(HFULL_O, 4096)
        for j in range(NT):
            bi = j % 2
            pbk, prd, pfr = self.bank(bi)
            for half in range(2):
                for k in range(16):
                    self.mm(pbk[:, half * 256:half * 256 + 256],
                            self.xT[:, k * TOK + j * 128:k * TOK + (j + 1) * 128],
                            self.WB[half][:, k * 256:(k + 1) * 256],
                            k == 0 and half == 0, False,
                            waits=[(wdma[half], wdma[half].val), (x2t, x2t.val),
                                   (pfr, pfr.val)])
            self.mm(pbk[:, :], self.ONES1[0:1, 0:128], self.BROW[0:1, 0:512],
                    False, True, waits=[self.ones, (browc, browc.val)],
                    incs=[_inc(prd)])
            self.ts("vector", HF[:, j * 512:(j + 1) * 512], pbk[:, :], 0.0, None,
                    ALU.max, waits=[(prd, prd.val)], incs=[_inc(hred), _inc(pfr)])
        for half in range(2):
            self.P.raw("vector", lambda e: e.nop(), waits=[(hred, hred.val)],
                       incs=[_inc(wfree[half])])
        ht = self.c("a1_ht")
        for j in range(NT):
            pt, prd, pfr = self.pt[j % 2], self.pt_rd[j % 2], self.pt_fr[j % 2]
            for hc in range(4):
                self.tr(pt[:, hc * 128:(hc + 1) * 128],
                        HF[:, j * 512 + hc * 128:j * 512 + (hc + 1) * 128],
                        waits=[(hred, hred.val), (pfr, pfr.val)],
                        incs=([_inc(prd)] if hc == 3 else []))
            dv = A_(self.AR2, HT_O + j * 128, (24576, 128), (TOK, 4), (1, 128))
            sv = A_(pt, 0, (1024, 128), (128, 4), (1, 128))
            self.cp("vector", dv, sv, waits=[(prd, prd.val)],
                    incs=[_inc(ht), _inc(pfr)])
        for half in range(2):
            dstw = A_(self.WB[half], 0, (4096, 128), (2048, 2), (1, 2048))
            srcw = A_(self.i_wg2T, half * 2 * 128 * D, (D, 128), (128 * D, 2),
                      (1, 2048))
            self.dma("gpsimd", dstw, srcw, wdma[half],
                     waits=[(wfree[half], wfree[half].val)])
        uvld = self.c("a1_uvld")
        self.dma("gpsimd", self.BROW[:, :], self.i_bias_rows[4:5, :], browc,
                 waits=[(hred, hred.val)])
        gdone = self.c("a1_g")
        ugd = self.c("a1_ug")
        tdma = self.c("g5_stage")
        totd = self.c("a1_ug")
        GA = self.ar2b(GATES_O, 2048)
        UJ = self.ar2b(UJ_O, 2048)
        VJ = self.ar2b(VJ_O, 2048)
        gd0 = gdone.val
        for j in range(NT):
            for og in range(4):
                pbk, prd, pfr = self.bank(og)
                for kc in range(4):
                    ws = kc // 2
                    lhs = A_(self.AR2, HT_O + kc * 1024 + j * 128, (24576, 128),
                             (1, 128))
                    w = [(wdma[ws], wdma[ws].val), (ht, ht.val),
                         (pfr, pfr.val)]
                    if j > 0 and kc == 0:
                        w.append((gdone, gd0 + 4 * (j - 1) + og + 1))
                    self.mm(pbk[:, :], lhs,
                            self.WB[ws][:, (kc % 2) * 2048 + og * 512:
                                        (kc % 2) * 2048 + og * 512 + 512],
                            kc == 0, False, waits=w)
                self.mm(pbk[:, :], self.ONES1[0:1, 0:128],
                        self.BROW[0:1, og * 512:og * 512 + 512], False, True,
                        waits=[self.ones, (browc, browc.val)], incs=[_inc(prd)])
                self.act(GA[:, og * 512:og * 512 + 512], pbk[:, :], AF.Sigmoid,
                         waits=[(prd, prd.val), (ugd, ugd.val)],
                         incs=[_inc(gdone)])
            self.dma("sync", UJ, self.dU[j * 128:(j + 1) * 128, :], uvld,
                     waits=[(spill, spill.val), (ugd, ugd.val),
                            (totd, totd.val)])
            self.dma("sync", VJ, self.dV[j * 128:(j + 1) * 128, :], uvld,
                     waits=[(spill, spill.val), (ugd, ugd.val),
                            (totd, totd.val)])
            self.tt("vector", UJ, UJ, GA, ALU.mult,
                    waits=[(gdone, gdone.val), (uvld, uvld.val)], incs=[_inc(ugd)])
            self.tt("vector", VJ, VJ, GA, ALU.mult, incs=[_inc(ugd)])
            self.dma("sync", self.dUG[j * 128:(j + 1) * 128, :], UJ, spill,
                     waits=[(ugd, ugd.val)])
            self.dma("sync", self.dVG[j * 128:(j + 1) * 128, :], VJ, spill,
                     waits=[(ugd, ugd.val)])
            for og in range(4):
                pb4, prd4, pfr4 = self.bank(4)
                self.mm(pb4[0:1, :], self.ONESK[:, :],
                        UJ[:, og * 512:og * 512 + 512], True, True,
                        waits=[(ugd, ugd.val), self.ones, (pfr4, pfr4.val)],
                        incs=[_inc(prd4)])
                self.cp("vector", self.AROW[0:1, 1280:1792], pb4[0:1, :],
                        waits=[(prd4, prd4.val), (tdma, tdma.val)],
                        incs=[_inc(totd), _inc(pfr4)])
                self.dma("scalar", self.g5i[j:j + 1, og * 512:og * 512 + 512],
                         self.AROW[0:1, 1280:1792], tdma, waits=[(totd, totd.val)])
        for j in range(NT):
            self.dma("scalar", self.g5i[8 + 2 * j:9 + 2 * j, :],
                     self.X[0:1, j * D:(j + 1) * D], tdma,
                     waits=[(self.stream_ready, self.stream_ready.val)])
            self.dma("scalar", self.g5i[9 + 2 * j:10 + 2 * j, :],
                     self.X[127:128, j * D:(j + 1) * D], tdma,
                     waits=[(self.stream_ready, self.stream_ready.val)])
        g5done = self.c("g5_done")
        self.coll(self.g5i[:, :], self.g5o[:, :], [[0, 1, 2, 3], [4, 5, 6, 7]],
                  waits=[(tdma, tdma.val)], incs=[_inc(g5done)])
        g5ld = self.c("ld_init_g")
        G5Sb = self.AR2[0:96, G5S_O:G5S_O + 2048]
        self.dma("gpsimd", G5Sb, self.g5o[:, :], g5ld,
                 waits=[(g5done, 1), (spill, spill.val), (ugd, ugd.val)])
        seld = g5ld
        cbd = self.c("a1_cb")
        for r in range(4):
            self.dma("gpsimd", self.BROW[:, :], self.i_conv_rows[r:r + 1, :],
                     browc, waits=[(cbd, cbd.val)])
            for og in range(4):
                pb4, prd4, pfr4 = self.bank(4)
                self.mm(pb4[:, :], self.ONES1[0:1, 0:128],
                        self.BROW[0:1, og * 512:og * 512 + 512], True, True,
                        waits=[(browc, browc.val), self.ones, (pfr4, pfr4.val)],
                        incs=[_inc(prd4)])
                self.cp("vector", self.ar2b(CB_O + r * 2048 + og * 512, 512),
                        pb4[:, :], waits=[(prd4, prd4.val)],
                        incs=[_inc(cbd), _inc(pfr4)])
        p2 = self.c("a1_p2")
        gotd = self.c("a1_got")
        ld2 = self.c("a1_uvld")
        UG = self.ar2b(UGJ_O, 2048)
        VG = self.ar2b(VGJ_O, 2048)
        GO = self.ar2b(GO_O, 2048)
        GOT = self.ar2b(GOTJ_O, 2048)
        TOUT = self.ar2f(TOUT_O, 2048)
        CVS = self.ar2f(CVS_O, 512)
        for j in range(NT):
            self.dma("sync", UG, self.dUG[j * 128:(j + 1) * 128, :], ld2,
                     waits=[(spill, spill.val), (p2, p2.val)])
            self.dma("sync", VG, self.dVG[j * 128:(j + 1) * 128, :], ld2,
                     waits=[(spill, spill.val), (p2, p2.val)])
            for og in range(4):
                pbk, prd, pfr = self.bank(og)
                self.mm(pbk[:, :], self.CONST[:, 256:384],
                        UG[:, og * 512:og * 512 + 512], True, False,
                        waits=[(ld2, ld2.val), self.ld, (pfr, pfr.val)])
                self.mm(pbk[:, :], self.WSELR[:, j * 128:(j + 1) * 128],
                        G5Sb[:, og * 512:og * 512 + 512], False, True,
                        waits=[(seld, seld.val), self.ld], incs=[_inc(prd)])
                self.tt("vector", GO[:, og * 512:og * 512 + 512], pbk[:, :],
                        VG[:, og * 512:og * 512 + 512], ALU.mult,
                        waits=[(prd, prd.val), (ld2, ld2.val)],
                        incs=[_inc(p2), _inc(pfr)])
            for half in range(2):
                pt, prd, pfr = self.pt[half], self.pt_rd[half], self.pt_fr[half]
                for ci in range(8):
                    kc = half * 8 + ci
                    self.tr(pt[:, ci * 128:(ci + 1) * 128],
                            GO[:, kc * 128:(kc + 1) * 128],
                            waits=[(p2, p2.val), (pfr, pfr.val)],
                            incs=([_inc(prd)] if ci == 7 else []))
                self.cp("vector", GOT[:, half * 1024:(half + 1) * 1024], pt[:, :],
                        waits=[(prd, prd.val), (gotd, gotd.val)],
                        incs=[_inc(p2), _inc(pfr)])
            self.dma("sync", A_(self.dGOT, j * 128, (TOK, 128), (128 * TOK, 16),
                                (1, 128)), GOT, gotd, waits=[(p2, p2.val)])
            xb = self.ar2b(UGJ_O, 2048)
            self.cp("vector", xb, self.X[:, j * D:(j + 1) * D],
                    waits=[(self.stream_ready, self.stream_ready.val),
                           (p2, p2.val)], incs=[_inc(p2)])
            for og in range(4):
                pb4, prd4, pfr4 = self.bank(4)
                psk, prd5, pfr5 = self.bank(5)
                self.mm(pb4[:, :], self.CONST[:, 896:1024],
                        xb[:, og * 512:og * 512 + 512], True, False,
                        waits=[(p2, p2.val), (pfr4, pfr4.val)])
                self.mm(pb4[:, :], self.HSELP[:, j * 128:(j + 1) * 128],
                        G5Sb[:, og * 512:og * 512 + 512],
                        False, True, waits=[(seld, seld.val)], incs=[_inc(prd4)])
                self.mm(psk[:, :], self.CONST[:, 1024:1152],
                        xb[:, og * 512:og * 512 + 512], True, False,
                        waits=[(p2, p2.val), (pfr5, pfr5.val)])
                self.mm(psk[:, :], self.HSELN[:, j * 128:(j + 1) * 128],
                        G5Sb[:, og * 512:og * 512 + 512],
                        False, True, waits=[(seld, seld.val)], incs=[_inc(prd5)])
                to = TOUT[:, og * 512:og * 512 + 512]
                self.tt("vector", to, pb4[:, :],
                        self.ar2b(CB_O + og * 512, 512), ALU.mult,
                        waits=[(prd4, prd4.val), (cbd, cbd.val)],
                        incs=[_inc(p2), _inc(pfr4)])
                self.tt("vector", CVS, self.X[:, j * D + og * 512:
                                              j * D + og * 512 + 512],
                        self.ar2b(CB_O + 2048 + og * 512, 512), ALU.mult,
                        waits=[(p2, p2.val)], incs=[_inc(p2)])
                self.tt("vector", to, to, CVS, ALU.add,
                        waits=[(p2, p2.val)], incs=[_inc(p2)])
                self.tt("vector", CVS, psk[:, :],
                        self.ar2b(CB_O + 4096 + og * 512, 512), ALU.mult,
                        waits=[(prd5, prd5.val), (p2, p2.val)],
                        incs=[_inc(p2), _inc(pfr5)])
                self.tt("vector", to, to, CVS, ALU.add,
                        waits=[(p2, p2.val)], incs=[_inc(p2)])
                self.tt("vector", to, to, self.ar2b(CB_O + 6144 + og * 512, 512),
                        ALU.add, waits=[(cbd, cbd.val), (p2, p2.val)],
                        incs=[_inc(p2)])
                xs = self.X[:, j * D + og * 512:j * D + og * 512 + 512]
                self.tt("vector", xs, xs, to, ALU.add,
                        waits=self.tap_gate() + [(p2, p2.val)], incs=[_inc(p2)])
        uvacc = self.c("a1_uv")
        g2ld2 = [self.c("wp1_ld0"), self.c("wp1_ld1")]
        for oh in range(8):
            slot = oh % 2
            dstw = A_(self.WB[slot], 0, (4096, 128), (256, 16), (1, 256))
            srcw = A_(self.i_wu1T, oh * 256, (D, 128), (128 * D, 16), (1, 256))
            self.dma("gpsimd", dstw, srcw, wdma[slot],
                     waits=[(wfree[slot], wfree[slot].val)])
            for j in range(NT):
                gsl = self.ar2b(GOT2_O[j % 2], 2048)
                g2ld = g2ld2[j % 2]
                self.dma("sync", gsl,
                         A_(self.dGOT, j * 128, (TOK, 128), (128 * TOK, 16),
                            (1, 128)), g2ld,
                         waits=[(gotd, gotd.val), (p2, p2.val),
                                (uvacc, max(0, uvacc.val - 1))])
                bi = j % 2
                pbk, prd, pfr = self.bank(bi)
                for k in range(16):
                    self.mm(pbk[:, 0:256], gsl[:, k * 128:(k + 1) * 128],
                            self.WB[slot][:, k * 256:(k + 1) * 256],
                            k == 0, k == 15,
                            waits=[(wdma[slot], wdma[slot].val),
                                   (g2ld, g2ld.val), (pfr, pfr.val)],
                            incs=([_inc(prd)] if k == 15 else []))
                xs = self.X[:, j * D + oh * 256:j * D + oh * 256 + 256]
                self.stt("vector", xs, pbk[:, 0:256], 1.0, xs, ALU.mult, ALU.add,
                         waits=[(prd, prd.val), (p2, p2.val)],
                         incs=[_inc(uvacc), _inc(pfr)])
            self.P.raw("vector", lambda e: e.nop(),
                       waits=[(uvacc, uvacc.val)], incs=[_inc(wfree[slot])])
        self.stream_base = self.c("stream").val
        self.stream_ready = self.c("stream")
        for j in range(NT):
            self.ln(j, uvacc, f"b3_{j}")

    def emit_l1ffn(self):
        wdma = [self.c("ws_d0"), self.c("ws_d1")]
        wfree = [self.c("wb_f0"), self.c("wb_f1")]
        browc = self.c("ld_init_g")
        self.dma("gpsimd", self.BROW[:, :], self.i_bias_rows[5:6, :], browc,
                 waits=[(self.c("a1_g"), self.c("a1_g").val)])
        qdone = self.c("f1_qT")
        XQ = self.ar2b(XQ_O, 2048)
        RND = self.ar2f(TOUT_O, 2048)
        for j in range(NT):
            e = self.c("f1_s")
            xs = self.X[:, j * D:(j + 1) * D]
            dumpb = self.WPQ[:, 0:1024]
            self.tt("vector", dumpb, xs[:, 0:1024], xs[:, 0:1024], ALU.mult,
                    waits=[(self.stream_ready, self.stream_ready.val),
                           (e, e.val)],
                    incs=[_inc(e)], accum=self.STAT[:, 16:17])
            self.tt("vector", dumpb, xs[:, 1024:2048], xs[:, 1024:2048], ALU.mult,
                    waits=[(e, e.val)], incs=[_inc(e)],
                    accum=self.STAT[:, 17:18])
            self.red(self.STAT[:, 18:19], xs, ALU.max, absval=True,
                     waits=[(e, e.val)], incs=[_inc(e)])
            self.tt("vector", self.STAT[:, 19:20], self.STAT[:, 16:17],
                    self.STAT[:, 17:18], ALU.add, waits=[(e, e.val)],
                    incs=[_inc(e)])
            self.ts("vector", self.STAT[:, 19:20], self.STAT[:, 19:20], 1.0 / D,
                    EPS_RMS, ALU.mult, ALU.add, waits=[(e, e.val)], incs=[_inc(e)])
            es = self.c("f1_ss")
            self.act(self.STAT[:, 20:21], self.STAT[:, 19:20], AF.Sqrt,
                     waits=[(e, e.val)], incs=[_inc(es)])
            self.recip(self.STAT[:, 21:22], self.STAT[:, 20:21],
                       waits=[(es, es.val)], incs=[_inc(e)])
            self.ts("vector", self.STAT[:, 22:23], self.STAT[:, 21:22], ISQ2048,
                    None, ALU.mult, waits=[(e, e.val)], incs=[_inc(e)])
            self.tt("vector", self.STAT[:, 23:24], self.STAT[:, 18:19],
                    self.STAT[:, 22:23], ALU.mult, waits=[(e, e.val)],
                    incs=[_inc(e)])
            self.ts("vector", self.STAT[:, 23:24], self.STAT[:, 23:24], EPS_Q,
                    None, ALU.max, waits=[(e, e.val)], incs=[_inc(e)])
            self.recip(self.STAT[:, 24:25], self.STAT[:, 23:24],
                       waits=[(e, e.val)], incs=[_inc(e)])
            self.ts("vector", self.STAT[:, 25:26], self.STAT[:, 24:25], 127.0,
                    None, ALU.mult, waits=[(e, e.val)], incs=[_inc(e)])
            self.tt("vector", self.STAT[:, 26:27], self.STAT[:, 25:26],
                    self.STAT[:, 22:23], ALU.mult, waits=[(e, e.val)],
                    incs=[_inc(e)])
            self.ts("vector", self.STAT[:, 27:28], self.STAT[:, 23:24],
                    1.0 / 127.0, None, ALU.mult, waits=[(e, e.val)],
                    incs=[_inc(e)])
            self.tt("vector", self.STAT[:, 27:28], self.STAT[:, 27:28],
                    self.CONSTF[:, 102:103], ALU.mult,
                    waits=[(e, e.val),
                           (self.wp1_scales, self.wp1_scales.val)],
                    incs=[_inc(e)])
            pb4, prd4, pfr4 = self.bank(4)
            self.mm(pb4[0:1, 0:128], self.STAT[:, 27:28], self.IDENTF[:, :],
                    True, True, waits=[(e, e.val), (pfr4, pfr4.val), self.ld],
                    incs=[_inc(prd4)])
            self.cp("vector", self.AROW[0:1, 768 + j * 128:768 + (j + 1) * 128],
                    pb4[0:1, 0:128], waits=[(prd4, prd4.val)],
                    incs=[_inc(e), _inc(pfr4)])
            self.ts("vector", RND, xs, self.STAT[:, 26:27], MAGIC, ALU.mult,
                    ALU.add, waits=[(e, e.val)], incs=[_inc(e)])
            self.ts("vector", XQ, RND, MAGIC, None, ALU.subtract,
                    waits=[(e, e.val)], incs=[_inc(e)])
            self.transpose_tile(self.xT, XQ, j, qdone, waits=[(e, e.val)])
        ns4 = self.c("stream")
        for tt in range(4):
            a1b = self.c("f1_a1b")
            pb4, prd4, pfr4 = self.bank(4)
            self.mm(pb4[:, 0:256], self.ONES1F[0:1, 0:128],
                    self.AROW[0:1, 768 + tt * 256:768 + tt * 256 + 256], True, True,
                    waits=[(qdone, qdone.val), (pfr4, pfr4.val), self.ones],
                    incs=[_inc(prd4)])
            A1B = self.ar2f(AGB_O, 256)
            self.cp("vector", A1B, pb4[:, 0:256], waits=[(prd4, prd4.val)],
                    incs=[_inc(a1b), _inc(pfr4)])
            hdone = self.c("f1_h")
            TMPF = self.ar2f(TMPF_O, 256)
            for og in range(32):
                slot = og % 2
                dstw = A_(self.WB[slot], 0, (4096, 128), (256, 16), (128, 2),
                          (1, 128))
                srcw = A_(self.g3ao, og * 256, (DI, 128), (128 * DI, 16), (128, 2),
                          (1, 128))
                self.dma("sync", dstw, srcw, wdma[slot],
                         waits=[(wfree[slot], wfree[slot].val),
                                (self.g23, 4)])
                wv_ = wdma[slot].val
                for oc in range(2):
                    och = og * 2 + oc
                    bi = och % 2
                    pbk, prd, pfr = self.bank(bi)
                    for k in range(16):
                        self.mm(pbk[:, 0:256],
                                self.WB[slot][:, (k * 2 + oc) * 128:
                                              (k * 2 + oc + 1) * 128],
                                self.xT[:, k * TOK + tt * 256:
                                        k * TOK + tt * 256 + 256],
                                k == 0, k == 15,
                                waits=[(wdma[slot], wv_), (qdone, qdone.val),
                                       (pfr, pfr.val)],
                                incs=([_inc(prd)] if k == 15 else []))
                    hv = self.c("f1_hv")
                    self.tt("vector", TMPF, pbk[:, 0:256], A1B, ALU.mult,
                            waits=[(prd, prd.val), (a1b, a1b.val),
                                   (hdone, hdone.val)],
                            incs=[_inc(hv), _inc(pfr)])
                    self.act(self.ar2b(Z_O + och * 256, 256), TMPF, AF.Gelu,
                             bias=self.CONSTF[:, 32 + och:33 + och],
                             waits=[(hv, hv.val), self.ld],
                             incs=[_inc(hdone)])
                self.P.raw("scalar", lambda e: e.nop(),
                           waits=[(hdone, hdone.val)], incs=[_inc(wfree[slot])])
            qd2 = self.quant_fm(list(range(64)), Z_O, 256, 64, "qf",
                                self.G1S[0:1, 12:13], (hdone, hdone.val),
                                c1_from_rms=True)
            dacc = self.c("f1_dacc")
            bcol = self.c("f1_bcol")
            for mh in range(2):
                pb4, prd4, pfr4 = self.bank(4)
                self.mm(pb4[:, mh:mh + 1],
                        self.AROW[0:1, 512 + mh * 128:512 + (mh + 1) * 128],
                        self.ONES1F[0:1, 0:1], True, True,
                        waits=[(qd2, qd2.val), (pfr4, pfr4.val), self.ones],
                        incs=[_inc(prd4)])
                self.cp("vector", self.STAT[:, 14 + mh:15 + mh], pb4[:, mh:mh + 1],
                        waits=[(prd4, prd4.val)], incs=[_inc(bcol), _inc(pfr4)])
            for og in range(4):
                for kl in range(32):
                    slot = kl % 2
                    dstw = A_(self.WB[slot], 0, (4096, 128), (512, 2), (1, 512))
                    srcw = A_(self.g3bo, kl * 256 * D + og * 512,
                              (D, 128), (128 * D, 2), (1, 512))
                    self.dma("sync", dstw, srcw, wdma[slot],
                             waits=[(wfree[slot], wfree[slot].val),
                                    (self.g23, 5)])
                    wv_ = wdma[slot].val
                    for kc2 in range(2):
                        kc = kl * 2 + kc2
                        for mh in range(2):
                            pbk, prd, pfr = self.bank(mh)
                            ii = ([_inc(wfree[slot])]
                                  if (kc2 == 1 and mh == 1) else [])
                            self.mm(pbk[:, :],
                                    self.ar2b(Z_O + kc * 256 + mh * 128, 128),
                                    self.WB[slot][:, kc2 * 512:kc2 * 512 + 512],
                                    kc == 0, False,
                                    waits=[(wdma[slot], wv_), (qd2, qd2.val),
                                           (pfr, pfr.val)],
                                    incs=ii)
                for mh in range(2):
                    j = tt * 2 + mh
                    pbk, prd, pfr = self.bank(mh)
                    self.mm(pbk[:, :], self.ONES1[0:1, 0:128],
                            self.BROW[0:1, og * 512:og * 512 + 512], False, True,
                            waits=[self.ones, (browc, browc.val)],
                            incs=[_inc(prd)])
                    xs = self.X[:, j * D + og * 512:j * D + og * 512 + 512]
                    self.stt("vector", xs, pbk[:, :], self.STAT[:, 14 + mh:15 + mh],
                             xs, ALU.mult, ALU.add,
                             waits=[(prd, prd.val), (bcol, bcol.val)]
                             + self.tap_gate(),
                             incs=[_inc(dacc), _inc(pfr)])
            if tt == 0:
                self.stream_base = ns4.val
            self.stream_ready = ns4
            for mh in range(2):
                self.ln(tt * 2 + mh, dacc, f"b4_{tt}_{mh}")
        self.stream_ready = ns4

    def emit_store(self):
        st = self.c("outdma")
        for j in range(NT):
            self.dma("sync", self.o_out[j * 128:(j + 1) * 128, :],
                     self.X[:, j * D:(j + 1) * D], st,
                     waits=[(self.stream_ready, self.stream_ready.val)])
        self.P.wait("sync", st)

    def build(self, upto=7):
        self.declare()
        self.phase_load()
        self.stream_ready = self.ld
        self.stream_base = 0
        if upto >= 2:
            self.phase_wprep1()
            self.emit_qkv()
        if upto >= 3:
            self.emit_g4()
        if upto >= 5:
            self.phase_wprep2()
        if upto >= 5:
            self.emit_g2g3_triggers()
        if upto >= 3:
            self.emit_attention()
        if upto >= 4:
            self.emit_wo()
            if TAPS:
                self.emit_tap(1)
        if upto >= 5:
            self.emit_l0ffn()
            if TAPS:
                self.emit_tap(2)
        if upto >= 6:
            self.emit_l1attn()
            if TAPS:
                self.emit_tap(3)
        if upto >= 7:
            self.emit_l1ffn()
        self.emit_store()
        self.P.run()


def build_program(upto=7):
    nc = bass.Bass("TRN2", target_bir_lowering=False, debug=False,
                   dynamic_dma_scratch_size=20480)
    mk = MK(nc)
    mk.build(upto=upto)
    return nc


def kernel(**inputs):
    in_maps, out_index = _host_prep(inputs)
    nc = build_program()
    res = run_bass_kernel_spmd(nc, in_maps, core_ids=list(range(8)))
    out = np.zeros((2, 4096, D), dtype=np.float32)
    for core in range(8):
        b, tok = out_index[core]
        out[b][tok] = res.results[core]["out"]
    return out


# version 12
# speedup vs baseline: 1.0044x; 1.0024x over previous
import numpy as np
import ml_dtypes
import concourse.bass as bass
import concourse.mybir as mybir
from concourse.bass_utils import run_bass_kernel_spmd

F32 = mybir.dt.float32
BF16 = mybir.dt.bfloat16
AF = mybir.ActivationFunctionType
ALU = mybir.AluOpType
AXX = mybir.AxisListType.X
MAGIC = float(np.float32(1.5 * 2.0 ** 23))
SQ128 = float(1.0 / np.sqrt(128.0))
NH = 16
D = 2048
DI = 8192
NT = 8
TOK = NT * 128
EPS_LN = 1e-5
EPS_RMS = 1e-6
EPS_Q = 1e-5
ISQ2048 = float(1.0 / np.sqrt(2048.0))
ISQ8192 = float(1.0 / np.sqrt(8192.0))
TAPS = False
DBG0 = False

QPRE_O, RTMP_O, RTMP2_O = 0, 2048, 4096
QS_O, VST0_O, VST1_O = 6144, 7168, 8192
XB_O = 16384
KH_O = (0, 8192)
VH_O = (4096, 12288)
QH_O = (20480, 20992)
OST_O = (21504, 22016)
FW_O = 0
Z_O = 4352
TMPF_O = 20992
AGB_O = 21504
MAXT_O = 22016
GT_O = 22272
SG_O = 22528
HFULL_O, HT_O, CB_O = 0, 4096, 8192
GATES_O, UJ_O, VJ_O = 16384, 18432, 20480
UGJ_O, VGJ_O, GO_O, GOTJ_O = 0, 2048, 4096, 6144
CVS_O = 2048
TOUT_O = 16384
G5S_O = 20480
XQ_O = 0
GOT2_O = (2048, 4096)


def _rope_perm():
    p = np.zeros(D, dtype=np.int64)
    for h in range(NH):
        base = h * 128
        for i in range(64):
            p[base + i] = base + 2 * i
            p[base + 64 + i] = base + 2 * i + 1
    return p


def _hadamard(n):
    h = np.array([[1.0]], dtype=np.float32)
    while h.shape[0] < n:
        h = np.block([[h, h], [h, -h]])
    return h.astype(np.float32)


def _host_prep(inputs):
    x = np.asarray(inputs["L0_attn_x"])
    perm = _rope_perm()
    wq = np.ascontiguousarray(np.asarray(inputs["L0_attn_Wq"])[:, perm])
    wk = np.ascontiguousarray(np.asarray(inputs["L0_attn_Wk"])[:, perm])
    bq = np.ascontiguousarray(np.asarray(inputs["L0_attn_bq"])[perm])
    bk = np.ascontiguousarray(np.asarray(inputs["L0_attn_bk"])[perm])
    bq_cols = np.ascontiguousarray(bq.reshape(16, 128).T)
    bk_cols = np.ascontiguousarray(bk.reshape(16, 128).T)
    wgT = np.ascontiguousarray(np.asarray(inputs["L0_ffn_w_gate"]).T)
    wupT = np.ascontiguousarray(np.asarray(inputs["L0_ffn_w_up"]).T)
    wdT = np.ascontiguousarray(np.asarray(inputs["L0_ffn_w_down"]).T)
    wu1T = np.ascontiguousarray(np.asarray(inputs["L1_attn_Wu"]).T)
    wv1T = np.ascontiguousarray(np.asarray(inputs["L1_attn_Wv"]).T)
    wg1T = np.ascontiguousarray(np.asarray(inputs["L1_attn_Wg1"]).T)
    wg2T = np.ascontiguousarray(np.asarray(inputs["L1_attn_Wg2"]).T)
    convw = np.asarray(inputs["L1_attn_conv_w"]).reshape(D, 3)
    conv_rows = np.ascontiguousarray(np.stack(
        [convw[:, 0], convw[:, 1], convw[:, 2],
         np.asarray(inputs["L1_attn_conv_b"])], 0)).astype(np.float32)
    w1T = np.ascontiguousarray(np.asarray(inputs["L1_ffn_w1"]).T)
    w2T = np.ascontiguousarray(np.asarray(inputs["L1_ffn_w2"]).T)
    b1_cols = np.ascontiguousarray(
        np.asarray(inputs["L1_ffn_b1"]).reshape(64, 128).T)
    bias_rows = np.zeros((8, D), dtype=np.float32)
    bias_rows[0] = np.asarray(inputs["L0_attn_bv"])
    bias_rows[1] = np.asarray(inputs["L0_attn_bo"])
    bias_rows[2] = np.asarray(inputs["L1_attn_bu"])
    bias_rows[3] = np.asarray(inputs["L1_attn_bv"])
    bias_rows[4] = np.asarray(inputs["L1_attn_bg2"])
    bias_rows[5] = np.asarray(inputs["L1_ffn_b2"])
    bias_rows[6, :512] = np.asarray(inputs["L1_attn_bg1"])
    bf = ml_dtypes.bfloat16
    h128 = _hadamard(128).astype(bf)
    ident = np.eye(128, dtype=np.float32).astype(bf)
    identf = np.eye(128, dtype=np.float32)
    tri = (np.arange(128)[:, None] <= np.arange(128)[None, :]).astype(np.float32)
    shd = np.eye(128, k=1, dtype=np.float32).astype(bf)
    shu = np.eye(128, k=-1, dtype=np.float32).astype(bf)
    swp = np.block([[np.zeros((64, 64)), np.eye(64)],
                    [np.eye(64), np.zeros((64, 64))]]).astype(np.float32).astype(bf)
    inv_freq = 1.0 / (10000.0 ** (np.arange(0, 128, 2, dtype=np.float64) / 128))

    in_maps, out_index = [], []
    for core in range(8):
        b, g = core // 4, core % 4
        tok = np.concatenate([np.arange(128) + (4 * j + g) * 128 for j in range(NT)])
        out_index.append((b, tok))
        ang = tok.astype(np.float64)[None, :] * inv_freq[:, None]
        ropec = np.concatenate([np.cos(ang)] * 2, 0).astype(np.float32)
        ropes = np.concatenate([-np.sin(ang), np.sin(ang)], 0).astype(np.float32)
        am = np.zeros((128, 512), dtype=np.float32)
        for m in range(4):
            if m < g:
                am[:, m * 128:(m + 1) * 128] = 1.0
            elif m == g:
                am[:, m * 128:(m + 1) * 128] = tri
        wsel = np.zeros((96, 8), dtype=np.float32)
        hselp = np.zeros((96, 1024), dtype=np.float32)
        hseln = np.zeros((96, 1024), dtype=np.float32)
        for r_ in range(4):
            for j_ in range(8):
                gt_row = 4 * j_ + r_
                for jl in range(8):
                    if gt_row < 4 * jl + g:
                        wsel[r_ * 24 + j_, jl] = 1.0
        wselr = np.repeat(wsel, 128, axis=1)
        def gr(gt, which):
            return (gt % 4) * 24 + 8 + (gt // 4) * 2 + which
        for jl in range(8):
            gt = 4 * jl + g
            if gt > 0:
                hselp[gr(gt - 1, 1), jl * 128 + 0] = 1.0
            if gt < 31:
                hseln[gr(gt + 1, 0), jl * 128 + 127] = 1.0
        sh = lambda w: np.ascontiguousarray(
            w[core * (w.shape[0] // 8):(core + 1) * (w.shape[0] // 8)])
        in_maps.append(dict(
            x=np.ascontiguousarray(x[b][tok]), wq=wq, wk=wk,
            wv=np.ascontiguousarray(np.asarray(inputs["L0_attn_Wv"])),
            wo=np.ascontiguousarray(np.asarray(inputs["L0_attn_Wo"])),
            bq_cols=bq_cols, bk_cols=bk_cols, b1_cols=b1_cols, bias_rows=bias_rows,
            wg_sh=sh(wgT), wup_sh=sh(wupT), wd_sh=sh(wdT),
            w1_sh=sh(w1T), w2_sh=sh(w2T),
            wu1T=wu1T, wv1T=wv1T, wg1T=wg1T, wg2T=wg2T, conv_rows=conv_rows,
            ropec=ropec, ropes=ropes, amask=am.astype(bf),
            wselr=wselr.astype(bf), hselp=hselp.astype(bf), hseln=hseln.astype(bf),
            h128=h128, ident=ident, identf=identf, tri=tri.astype(bf),
            shd=shd, shu=shu, swp=swp,
        ))
    return in_maps, out_index



class Ctr:
    __slots__ = ("sem", "val", "name")

    def __init__(self, nc, name):
        self.sem = nc.alloc_semaphore(name)
        self.val = 0
        self.name = name


class Plan:
    ENGINES = ("tensor", "vector", "scalar", "gpsimd", "sync")

    def __init__(self, nc):
        self.nc = nc
        self.ops = {e: [] for e in self.ENGINES}
        self._ctrs = {}
        self._wm = {e: {} for e in self.ENGINES}
        self.n = 0

    def ctr(self, name):
        if name not in self._ctrs:
            self._ctrs[name] = Ctr(self.nc, name)
        return self._ctrs[name]

    def _wl(self, eng, waits):
        wl = []
        for w in waits:
            c, v = (w if isinstance(w, tuple) else (w, None))
            v = c.val if v is None else v
            assert v <= c.val, f"future wait {c.name}: {v} > {c.val}"
            if v <= 0 or self._wm[eng].get(c.name, 0) >= v:
                continue
            self._wm[eng][c.name] = v
            wl.append((c.sem, v))
        return wl

    def raw(self, eng, emit, waits=(), incs=()):
        wl = self._wl(eng, waits)
        il = []
        first_val = None
        for c, n in incs:
            c.val += n
            if first_val is None:
                first_val = c.val
            il.append((c.sem, n))
        self.n += 1 + len(wl) + 2 * max(0, len(il) - 1)

        def fn(e, emit=emit, wl=wl, il=il, fv=first_val):
            for sem, v in wl:
                e.wait_ge(sem, v)
            inst = emit(e)
            if il:
                inst.then_inc(*il[0])
                if len(il) > 1:
                    e.wait_ge(il[0][0], fv)
                    for sem, n in il[1:]:
                        nop = (e.engine_nop() if hasattr(e, "engine_nop")
                               else e.nop())
                        nop.then_inc(sem, n)
        self.ops[eng].append(fn)

    def wait(self, eng, *waits):
        wl = self._wl(eng, waits)
        if wl:
            self.n += len(wl)

            def fn(e, wl=wl):
                for sem, v in wl:
                    e.wait_ge(sem, v)
            self.ops[eng].append(fn)

    def run(self):
        with self.nc.Block() as block:
            @block.tensor
            def _(t):
                for fn in self.ops["tensor"]:
                    fn(t)

            @block.vector
            def _(v):
                for fn in self.ops["vector"]:
                    fn(v)

            @block.scalar
            def _(sc):
                for fn in self.ops["scalar"]:
                    fn(sc)

            @block.gpsimd
            def _(g):
                for fn in self.ops["gpsimd"]:
                    fn(g)

            @block.sync
            def _(s):
                for fn in self.ops["sync"]:
                    fn(s)


def A_(handle, off, *dims):
    return bass.AP(handle, off, [list(d) for d in dims])


def _inc(ctr, n=1):
    return (ctr, n)


class MK:
    def __init__(self, nc):
        self.nc = nc
        self.P = Plan(nc)

    def c(self, name):
        return self.P.ctr(name)

    def dma(self, eng, dst, src, inc=None, waits=(), extra=()):
        incs = ([(inc, 16)] if inc is not None else []) + list(extra)
        self.P.raw(eng, lambda e, d=dst, s=src: e.dma_start(out=d, in_=s),
                   waits=waits, incs=incs)

    def mm(self, out, lhsT, rhs, start, stop, waits=(), incs=()):
        self.P.raw("tensor",
                   lambda t, o=out, l=lhsT, r=rhs, a=start, b=stop:
                   t.matmul(o, l, r, start=a, stop=b), waits=waits, incs=incs)

    def tr(self, out, in_, waits=(), incs=()):
        ident = self.CONST[:, 0:128]
        self.P.raw("tensor", lambda t, o=out, i=in_, d=ident:
                   t.transpose(o, i, d), waits=waits, incs=incs)

    def tt(self, eng, out, a, b, op, waits=(), incs=(), accum=None):
        if accum is None:
            self.P.raw(eng, lambda e, o=out, x=a, y=b, p=op:
                       e.tensor_tensor(o, x, y, p), waits=waits, incs=incs)
        else:
            self.P.raw(eng, lambda e, o=out, x=a, y=b, p=op, ac=accum:
                       e.scalar_tensor_tensor(o, x, 1.0, y, ALU.mult, p,
                                              accum_out=ac),
                       waits=waits, incs=incs)

    def ts(self, eng, out, a, s1, s2, op0, op1=None, waits=(), incs=()):
        if s2 is None and op1 is None:
            self.P.raw(eng, lambda e, o=out, x=a, u=s1, p0=op0:
                       e.tensor_single_scalar(o, x, u, p0), waits=waits, incs=incs)
        else:
            self.P.raw(eng, lambda e, o=out, x=a, u=s1, w=s2, p0=op0, p1=op1:
                       e.tensor_scalar(o, x, u, w, p0, p1), waits=waits, incs=incs)

    def stt(self, eng, out, a, scal, b, op0, op1, waits=(), incs=()):
        self.P.raw(eng, lambda e, o=out, x=a, u=scal, y=b, p0=op0, p1=op1:
                   e.scalar_tensor_tensor(o, x, u, y, p0, p1),
                   waits=waits, incs=incs)

    def cp(self, eng, out, in_, waits=(), incs=()):
        if eng == "scalar":
            self.P.raw(eng, lambda e, o=out, i=in_: e.copy(o, i),
                       waits=waits, incs=incs)
        else:
            self.P.raw(eng, lambda e, o=out, i=in_: e.tensor_copy(o, i),
                       waits=waits, incs=incs)

    def act(self, out, in_, f, bias=0.0, scale=1.0, accum=None, waits=(), incs=()):
        self.P.raw("scalar", lambda e, o=out, i=in_, ff=f, b=bias, s=scale,
                   ac=accum: e.activation(o, i, ff, bias=b, scale=s,
                                          accum_out=ac), waits=waits, incs=incs)

    def red(self, out, in_, op, absval=False, waits=(), incs=()):
        self.P.raw("vector", lambda e, o=out, i=in_, p=op, av=absval:
                   e.tensor_reduce(o, i, AXX, p, apply_absolute_value=av),
                   waits=waits, incs=incs)

    def recip(self, out, in_, waits=(), incs=()):
        self.P.raw("vector", lambda e, o=out, i=in_: e.reciprocal(o, i),
                   waits=waits, incs=incs)

    def memset(self, t, v, waits=(), incs=()):
        self.P.raw("vector", lambda e, o=t, x=v: e.memset(o, x),
                   waits=waits, incs=incs)

    def coll(self, ins, outs, groups, waits=(), incs=()):
        self.P.raw("gpsimd", lambda e, i=ins, o=outs, g=groups:
                   e.collective_compute("AllGather", ALU.bypass,
                                        replica_groups=g, ins=[i], outs=[o]),
                   waits=waits, incs=incs)

    def declare(self):
        nc = self.nc
        DT = nc.dram_tensor

        def EI(name, shape, dt=F32):
            return DT(name, shape, dt, kind="ExternalInput")
        self.i_x = EI("x", [TOK, D])
        self.i_wq = EI("wq", [D, D]); self.i_wk = EI("wk", [D, D])
        self.i_wv = EI("wv", [D, D]); self.i_wo = EI("wo", [D, D])
        self.i_bq_cols = EI("bq_cols", [128, 16])
        self.i_bk_cols = EI("bk_cols", [128, 16])
        self.i_b1_cols = EI("b1_cols", [128, 64])
        self.i_bias_rows = EI("bias_rows", [8, D])
        self.i_wg_sh = EI("wg_sh", [256, DI]); self.i_wup_sh = EI("wup_sh", [256, DI])
        self.i_wd_sh = EI("wd_sh", [1024, D])
        self.i_w1_sh = EI("w1_sh", [256, DI]); self.i_w2_sh = EI("w2_sh", [1024, D])
        self.i_wu1T = EI("wu1T", [D, D]); self.i_wv1T = EI("wv1T", [D, D])
        self.i_wg1T = EI("wg1T", [D, 512]); self.i_wg2T = EI("wg2T", [512, D])
        self.i_conv_rows = EI("conv_rows", [4, D])
        self.i_ropec = EI("ropec", [128, TOK]); self.i_ropes = EI("ropes", [128, TOK])
        self.i_amask = EI("amask", [128, 512], BF16)
        self.i_wselr = EI("wselr", [96, 1024], BF16)
        self.i_hselp = EI("hselp", [96, 1024], BF16)
        self.i_hseln = EI("hseln", [96, 1024], BF16)
        self.i_h128 = EI("h128", [128, 128], BF16)
        self.i_ident = EI("ident", [128, 128], BF16)
        self.i_identf = EI("identf", [128, 128])
        self.i_tri = EI("tri", [128, 128], BF16)
        self.i_shd = EI("shd", [128, 128], BF16)
        self.i_shu = EI("shu", [128, 128], BF16)
        self.i_swp = EI("swp", [128, 128], BF16)
        self.o_out = DT("out", [TOK, D], F32, kind="ExternalOutput")
        if TAPS:
            self.o_taps = {i: DT(f"tap{i}", [TOK, D], F32, kind="ExternalOutput")
                           for i in (1, 2, 3)}
        self.dQ = DT("dQ", [D, TOK], BF16)
        self.dO = DT("dO", [D, TOK], BF16)
        self.dU = DT("dU", [TOK, D], BF16)
        self.dV = DT("dV", [TOK, D], BF16)
        self.dUG = DT("dUG", [TOK, D], BF16)
        self.dVG = DT("dVG", [TOK, D], BF16)
        self.dGOT = DT("dGOT", [D, TOK], BF16)
        self.g1i = DT("g1i", [1, 5], F32); self.g1o = DT("g1o", [8, 5], F32)
        self.g2gi = DT("g2gi", [256, DI], BF16)
        self.g2go = DT("g2go", [D, DI], BF16, addr_space="Shared")
        self.g2ui = DT("g2ui", [256, DI], BF16)
        self.g2uo = DT("g2uo", [D, DI], BF16, addr_space="Shared")
        self.g2di = DT("g2di", [1024, D], BF16)
        self.g2do = DT("g2do", [DI, D], BF16, addr_space="Shared")
        self.g3ai = DT("g3ai", [256, DI], BF16)
        self.g3ao = DT("g3ao", [D, DI], BF16, addr_space="Shared")
        self.g3bi = DT("g3bi", [1024, D], BF16)
        self.g3bo = DT("g3bo", [DI, D], BF16, addr_space="Shared")
        self.g4i = DT("g4i", [2, D * TOK], BF16)
        self.g4oK = DT("g4oK", [4, D * TOK], BF16)
        self.g4oV = DT("g4oV", [4, D * TOK], BF16)
        self.g5i = DT("g5i", [24, D], F32); self.g5o = DT("g5o", [96, D], F32)
        if DBG0:
            self.dbg_fw = DT("dbg_fw", [128, 4352], BF16)
            self.dbg_z = DT("dbg_z", [128, 16640], BF16)
            self.dbg_z2 = DT("dbg_z2", [128, 16640], BF16)
            self.dbg_arow = DT("dbg_arow", [1, 1792], F32)

        A = nc.alloc_sbuf_tensor
        self.X = A("X", [128, NT * D], F32)
        self.xT = A("xT", [128, 16 * TOK], BF16)
        self.AR1 = self.xT
        self.AR2 = A("AR2", [128, 24576], BF16)
        self.WB = [A(f"WB{i}", [128, 4096], BF16) for i in range(2)]
        self.WPST = A("WPST", [128, 4096], F32)
        self.WPQ = A("WPQ", [128, 2048], BF16)
        self.CONST = A("CONST", [128, 1280], BF16)
        self.CONSTF = A("CONSTF", [128, 104], F32)
        self.IDENTF = A("IDENTF", [128, 128], F32)
        self.BROW = A("BROW", [1, D], BF16)
        self.WSELR = A("WSELR", [96, 1024], BF16)
        self.HSELP = A("HSELP", [96, 1024], BF16)
        self.HSELN = A("HSELN", [96, 1024], BF16)
        self.ONES1 = A("ONES1", [1, 128], BF16)
        self.ONESK = A("ONESK", [128, 1], BF16)
        self.ONES1F = A("ONES1F", [1, 128], F32)
        self.ONESKF = A("ONESKF", [128, 1], F32)
        self.STAT = A("STAT", [128, 64], F32)
        self.AROW = A("AROW", [1, 1792], F32)
        self.G1S = A("G1S", [8, 24], F32)
        self.CONSTC = A("CONSTC", [128, 2], F32)
        self.pb = [nc.alloc_psum_tensor(f"pb{i}", [128, 512], F32) for i in range(5)]
        self.psb = nc.alloc_psum_tensor("ps", [128, 512], F32)
        self.pt = [nc.alloc_psum_tensor(f"pt{i}", [128, 1024], BF16)
                   for i in range(2)]
        assert nc.sbuf_bytes_remaining >= 0, nc.sbuf_bytes_remaining
        self.bk_rd = [self.c(f"bk{i}_rd") for i in range(6)]
        self.bk_fr = [self.c(f"bk{i}_fr") for i in range(6)]
        self.pt_rd = [self.c(f"pt{i}_rd") for i in range(2)]
        self.pt_fr = [self.c(f"pt{i}_fr") for i in range(2)]

    def bank(self, i):
        return (self.psb if i == 5 else self.pb[i]), self.bk_rd[i], self.bk_fr[i]

    def ar2b(self, off, n):
        return self.AR2[:, off:off + n]

    def ar2f(self, off, n_f32):
        return self.AR2[:, off:off + 2 * n_f32].bitcast(F32)

    def phase_load(self):
        ld = self.c("ld_init")
        for j in range(NT):
            self.dma("sync", self.X[:, j * D:(j + 1) * D],
                     self.i_x[j * 128:(j + 1) * 128, :], ld)
        self.dma("sync", self.CONST[:, 0:128], self.i_ident[:, :], ld)
        self.dma("sync", self.CONST[:, 128:256], self.i_h128[:, :], ld)
        self.dma("sync", self.CONST[:, 256:384], self.i_tri[:, :], ld)
        self.dma("sync", self.CONST[:, 384:896], self.i_amask[:, :], ld)
        self.dma("sync", self.CONST[:, 896:1024], self.i_shd[:, :], ld)
        self.dma("sync", self.CONST[:, 1024:1152], self.i_shu[:, :], ld)
        self.dma("sync", self.CONST[:, 1152:1280], self.i_swp[:, :], ld)
        self.dma("sync", self.IDENTF[:, :], self.i_identf[:, :], ld)
        ldg = self.c("ld_init_g")
        self.dma("gpsimd", self.ar2b(9216, 1024), self.i_ropec[:, :], ldg)
        self.dma("gpsimd", self.ar2b(10240, 1024), self.i_ropes[:, :], ldg)
        self.dma("sync", self.CONSTF[:, 0:16], self.i_bq_cols[:, :], ld)
        self.dma("sync", self.CONSTF[:, 16:32], self.i_bk_cols[:, :], ld)
        self.dma("sync", self.CONSTF[:, 32:96], self.i_b1_cols[:, :], ld)
        self.dma("sync", self.WSELR[:, :], self.i_wselr[:, :], ld)
        self.dma("sync", self.HSELP[:, :], self.i_hselp[:, :], ld)
        self.dma("sync", self.HSELN[:, :], self.i_hseln[:, :], ld)
        on = self.c("ones_ready")
        self.memset(self.ONES1[:, :], 1.0, incs=[_inc(on)])
        self.memset(self.ONESK[:, :], 1.0, incs=[_inc(on)])
        self.memset(self.ONES1F[:, :], 1.0, incs=[_inc(on)])
        self.memset(self.ONESKF[:, :], 1.0, incs=[_inc(on)])
        self.memset(self.CONSTC[:, 0:1], MAGIC, incs=[_inc(on)])
        self.memset(self.CONSTC[:, 1:2], SQ128, incs=[_inc(on)])
        self.ld, self.ones = ld, on
        self.ldg = ldg

    def transpose_tile(self, dst, src_tile, j, done, waits=()):
        for half in range(2):
            pt, prd, pfr = self.pt[half], self.pt_rd[half], self.pt_fr[half]
            for dci in range(8):
                dc = half * 8 + dci
                self.tr(pt[:, dci * 128:(dci + 1) * 128],
                        src_tile[:, dc * 128:(dc + 1) * 128],
                        waits=list(waits) + [(pfr, pfr.val), self.ld],
                        incs=([_inc(prd)] if dci == 7 else []))
            dv = A_(dst, (half * 8) * TOK + j * 128,
                    (16 * TOK, 128), (TOK, 8), (1, 128))
            sv = A_(pt, 0, (1024, 128), (128, 8), (1, 128))
            self.cp("vector", dv, sv, waits=[(prd, prd.val)],
                    incs=[_inc(done), _inc(pfr)])

    def stream_transpose(self, dst, tag, gate_per_tile=None):
        cast = self.c("st_cast")
        done = self.c("st_tev")
        for j in range(NT):
            w = [self.ld, self.ones]
            if gate_per_tile is not None:
                w.append((gate_per_tile[0], gate_per_tile[1] + j + 1))
            if j >= 2:
                w.append((done, (j - 1) * 2))
            xb = self.ar2b(XB_O + (j % 2) * 2048, 2048)
            self.cp("vector", xb, self.X[:, j * D:(j + 1) * D], waits=w,
                    incs=[_inc(cast)])
            self.transpose_tile(dst, xb, j, done, waits=[(cast, cast.val)])
        return done

    def phase_wprep1(self):
        self.shards = [
            ("g", self.i_wg_sh, self.g2gi, 2, 4),
            ("u", self.i_wup_sh, self.g2ui, 2, 4),
            ("d", self.i_wd_sh, self.g2di, 8, 1),
            ("1", self.i_w1_sh, self.g3ai, 2, 4),
            ("2", self.i_w2_sh, self.g3bi, 8, 1),
        ]
        ldc2 = [self.c("wp1_ld0"), self.c("wp1_ld1")]
        acc = self.c("wp1_acc")
        tsum = self.c("wp1_tsum")
        self.memset(self.STAT[:, 56:61], 0.0, incs=[_inc(tsum)])
        gci = 0
        for mi, (nm, src, _d, nrb, ncb) in enumerate(self.shards):
            cw = src.shape[1] // ncb
            for rb in range(nrb):
                for cb in range(ncb):
                    half = gci % 2
                    st = self.WPST[:, half * 2048:(half + 1) * 2048]
                    w = [(acc, acc.val - 1)] if gci >= 2 else []
                    gci += 1
                    self.dma("scalar", st,
                             src[rb * 128:(rb + 1) * 128, cb * cw:(cb + 1) * cw],
                             ldc2[half], waits=w)
                    self.act(st, st,
                             AF.Abs, accum=self.STAT[:, 32 + half:33 + half],
                             waits=[(ldc2[half], ldc2[half].val),
                                    (tsum, max(1, gci - 1))],
                             incs=[_inc(acc)])
                    self.tt("vector", self.STAT[:, 56 + mi:57 + mi],
                            self.STAT[:, 56 + mi:57 + mi],
                            self.STAT[:, 32 + half:33 + half], ALU.add,
                            waits=[(acc, acc.val), (tsum, tsum.val)],
                            incs=[_inc(tsum)])
        self.wp1_tsum = tsum

    def phase_wprep1_tail(self):
        red = self.c("wp1_red")
        g1dma = self.c("g1in_dma")
        tsum = self.wp1_tsum
        base = 1
        for mi, (nm, src, _d, nrb, ncb) in enumerate(self.shards):
            base += nrb * ncb
            pbk, prd, pfr = self.bank(5)
            self.mm(pbk[0:1, 0:128], self.STAT[:, 56 + mi:57 + mi],
                    self.IDENTF[:, :],
                    True, True, waits=[(tsum, base), self.ld, (pfr, pfr.val)],
                    incs=[_inc(prd)])
            self.red(self.G1S[0:1, mi:mi + 1], pbk[0:1, 0:128], ALU.add,
                     waits=[(prd, prd.val)], incs=[_inc(red), _inc(pfr)])
            self.dma("scalar", self.g1i[0:1, mi:mi + 1],
                     self.G1S[0:1, mi:mi + 1], g1dma, waits=[(red, red.val)])
        g1done = self.c("g1_done")
        self.coll(self.g1i[:, :], self.g1o[:, :], [list(range(8))],
                  waits=[g1dma], incs=[_inc(g1done)])
        g1ld = self.c("ld_init")
        self.dma("sync", self.G1S[0:8, 0:5], self.g1o[:, :], g1ld, waits=[g1done])
        sc = self.c("wp1_scales")
        pbk, prd, pfr = self.bank(5)
        self.mm(pbk[0:1, 0:5], self.ONESKF[0:8, 0:1], self.G1S[0:8, 0:5],
                True, True, waits=[(g1ld, g1ld.val), self.ones, (pfr, pfr.val)],
                incs=[_inc(prd)])
        numel = [D * DI, D * DI, DI * D, D * DI, DI * D]
        for mi in range(5):
            self.ts("vector", self.G1S[0:1, 8 + mi:9 + mi], pbk[0:1, mi:mi + 1],
                    1.0 / numel[mi], EPS_Q, ALU.mult, ALU.max,
                    waits=[(prd, prd.val)], incs=[_inc(sc)])
        self.recip(self.G1S[0:1, 16:21], self.G1S[0:1, 8:13],
                   waits=[(sc, sc.val)], incs=[_inc(sc), _inc(pfr)])
        for mi in range(3):
            pb4, prd4, pfr4 = self.bank(4)
            self.mm(pb4[:, 0:1], self.ONES1F[0:1, 0:128],
                    self.G1S[0:1, 16 + mi:17 + mi], True, True,
                    waits=[(sc, sc.val), (pfr4, pfr4.val)], incs=[_inc(prd4)])
            self.cp("vector", self.CONSTF[:, 96 + mi:97 + mi], pb4[:, 0:1],
                    waits=[(prd4, prd4.val)], incs=[_inc(sc), _inc(pfr4)])
        for mi in range(5):
            pb4, prd4, pfr4 = self.bank(4)
            self.mm(pb4[:, 0:1], self.ONES1F[0:1, 0:128],
                    self.G1S[0:1, 8 + mi:9 + mi], True, True,
                    waits=[(sc, sc.val), (pfr4, pfr4.val)], incs=[_inc(prd4)])
            self.cp("vector", self.CONSTF[:, 99 + mi:100 + mi], pb4[:, 0:1],
                    waits=[(prd4, prd4.val)], incs=[_inc(sc), _inc(pfr4)])
        self.wp1_scales = sc

    def phase_wprep2(self):
        ld2 = self.c("wp2_ld")
        up = self.c("wp2_u")
        qd = self.c("wp2_q")
        outd = self.c("wp2_out")
        self.wp2_trig = []
        gci = 0
        for mi, (nm, src, dstb, nrb, ncb) in enumerate(self.shards):
            tern = mi < 3
            cw = src.shape[1] // ncb
            ci = 0
            for rb in range(nrb):
                for cb in range(ncb):
                    half = gci % 2
                    st = self.WPST[:, half * 2048:(half + 1) * 2048]
                    w = [(outd, max(0, outd.val - 16))] if gci >= 2 else []
                    gci += 1
                    self.dma("gpsimd", st,
                             src[rb * 128:(rb + 1) * 128, cb * cw:(cb + 1) * cw],
                             ld2, waits=w)
                    qtile = self.WPQ[:, 0:2048]
                    w2 = [(outd, outd.val)]
                    if tern:
                        ups = self.c("wp2_us")
                        self.ts("vector", st, st,
                                self.CONSTF[:, 96 + mi:97 + mi], MAGIC,
                                ALU.mult, ALU.add,
                                waits=[(ld2, ld2.val),
                                       (self.wp1_scales, self.wp1_scales.val)],
                                incs=[_inc(ups)])
                        self.ts("vector", st, st, MAGIC, 1.0, ALU.subtract,
                                ALU.min, waits=[(ups, ups.val)], incs=[_inc(up)])
                        self.ts("vector", qtile, st, -1.0, None, ALU.max,
                                waits=[(up, up.val)] + w2, incs=[_inc(qd)])
                        qthis = qd
                    else:
                        qs = self.c("wp2_us")
                        self.ts("vector", st, st, 1e38, 1.0, ALU.mult,
                                ALU.min,
                                waits=[(ld2, ld2.val)],
                                incs=[_inc(qs)])
                        self.ts("vector", qtile, st, -1.0, None, ALU.max,
                                waits=[(qs, qs.val)] + w2, incs=[_inc(qs)])
                        qthis = qs
                    self.dma("gpsimd",
                             dstb[rb * 128:(rb + 1) * 128, cb * cw:(cb + 1) * cw],
                             qtile, outd, waits=[(qthis, qthis.val)])
                    ci += 1
            self.wp2_trig.append(outd.val)
        self.wp2_outd = outd

    def emit_g2g3_triggers(self):
        pairs = [(self.g2gi, self.g2go), (self.g2ui, self.g2uo),
                 (self.g2di, self.g2do), (self.g3ai, self.g3ao),
                 (self.g3bi, self.g3bo)]
        g23 = self.c("g23")
        for mi, (i_, o_) in enumerate(pairs):
            self.coll(i_[:, :], o_[:, :], [list(range(8))],
                      waits=[(self.wp2_outd, self.wp2_trig[mi])], incs=[_inc(g23)])
        self.g23 = g23

    def emit_qkv(self):
        self.x0t = self.stream_transpose(self.xT, "x0T")
        x0t = self.x0t
        wdma = [self.c("wb_d0"), self.c("wb_d1")]
        wfree = [self.c("wb_f0"), self.c("wb_f1")]
        stg = self.c("g4_stage")
        qspill = self.c("q_spill")
        ev = self.c("qkv_ev")
        QPRE = self.ar2f(QPRE_O, 1024)
        QB = self.ar2b(QS_O, 1024)
        RT = self.ar2f(RTMP_O, 1024)
        RT2 = self.ar2f(RTMP2_O, 1024)

        for proj, wsrc, bc0, isq in (("q", self.i_wq, 0, True),
                                     ("k", self.i_wk, 16, False)):
            for ogh in range(8):
                slot = ogh % 2
                dstw = A_(self.WB[slot], 0, (4096, 128), (256, 16), (128, 2), (1, 128))
                srcw = A_(wsrc, ogh * 256, (D, 128), (128 * D, 16), (128, 2), (1, 128))
                self.dma("gpsimd", dstw, srcw, wdma[slot],
                         waits=[(wfree[slot], wfree[slot].val)])
                for oc in range(2):
                    och = ogh * 2 + oc
                    pair = (0, 1) if och % 2 == 0 else (2, 3)
                    for hh in range(2):
                        bi = pair[hh]
                        pbk, prd, pfr = self.bank(bi)
                        for k in range(16):
                            self.mm(pbk[:, :],
                                    self.WB[slot][:, (k * 2 + oc) * 128:
                                                  (k * 2 + oc + 1) * 128],
                                    self.xT[:, k * TOK + hh * 512:
                                            k * TOK + hh * 512 + 512],
                                    k == 0, k == 15,
                                    waits=[(wdma[slot], wdma[slot].val),
                                           (x0t, x0t.val), (pfr, pfr.val)],
                                    incs=([_inc(prd)] if k == 15 else []))
                    for hh in range(2):
                        bi = pair[hh]
                        pbk, prd, pfr = self.bank(bi)
                        self.ts("vector", QPRE[:, hh * 512:hh * 512 + 512],
                                pbk[:, :], self.CONSTF[:, bc0 + och:bc0 + och + 1],
                                None, ALU.add,
                                waits=[(prd, prd.val), self.ld],
                                incs=[_inc(ev), _inc(pfr)])
                    self.cp("vector", QB, QPRE,
                            waits=[(ev, ev.val), (qspill, qspill.val),
                                   (stg, stg.val)], incs=[_inc(ev)])
                    for hh, bi in ((0, 4), (1, 5)):
                        pbk, prd, pfr = self.bank(bi)
                        self.mm(pbk[:, :], self.CONST[:, 1152:1280],
                                QB[:, hh * 512:hh * 512 + 512], True, True,
                                waits=[(ev, ev.val), (pfr, pfr.val)],
                                incs=[_inc(prd)])
                    self.tt("vector", RT, QPRE, self.ar2b(9216, 1024), ALU.mult,
                            waits=[(ev, ev.val), (self.ldg, self.ldg.val)],
                            incs=[_inc(ev)])
                    for hh, bi in ((0, 4), (1, 5)):
                        pbk, prd, pfr = self.bank(bi)
                        self.tt("vector", RT2[:, hh * 512:hh * 512 + 512],
                                pbk[:, :],
                                self.ar2b(10240 + hh * 512, 512), ALU.mult,
                                waits=[(prd, prd.val)], incs=[_inc(ev), _inc(pfr)])
                    self.tt("vector", QB, RT, RT2, ALU.add,
                            waits=[(ev, ev.val)], incs=[_inc(ev)])
                    if isq:
                        self.dma("sync", self.dQ[och * 128:(och + 1) * 128, :],
                                 QB, qspill, waits=[(ev, ev.val)])
                    else:
                        dstap = A_(self.g4i, och * 128 * TOK, (TOK, 128), (1, TOK))
                        self.dma("sync", dstap, QB, stg, waits=[(ev, ev.val)])
                self.P.wait("sync", (ev, ev.val))
                self.P.raw("sync", lambda e: e.nop(), incs=[_inc(wfree[slot])])
            if proj == "q":
                self.phase_wprep1_tail()
            if proj == "k":
                self.kstg = stg.val
        vev = self.c("qkv_ev")
        browc = self.c("ld_init_g")
        self.dma("gpsimd", self.BROW[:, :], self.i_bias_rows[0:1, :], browc)
        for oh in range(8):
            slot = oh % 2
            dstw = A_(self.WB[slot], 0, (4096, 128), (256, 16), (1, 256))
            srcw = A_(self.i_wv, oh * 256, (D, 128), (128 * D, 16), (1, 256))
            self.dma("gpsimd", dstw, srcw, wdma[slot],
                     waits=[(wfree[slot], wfree[slot].val)])
            for j in range(NT):
                vst = self.ar2b((VST0_O if j % 2 == 0 else VST1_O), 256)
                bi = j % 2
                pbk, prd, pfr = self.bank(bi)
                for k in range(16):
                    self.mm(pbk[:, 0:256],
                            self.xT[:, k * TOK + j * 128:k * TOK + (j + 1) * 128],
                            self.WB[slot][:, k * 256:(k + 1) * 256],
                            k == 0, False,
                            waits=[(wdma[slot], wdma[slot].val),
                                   (x0t, x0t.val), (pfr, pfr.val)])
                self.mm(pbk[:, 0:256], self.ONES1[0:1, 0:128],
                        self.BROW[0:1, oh * 256:oh * 256 + 256], False, True,
                        waits=[self.ones, (browc, browc.val)], incs=[_inc(prd)])
                self.cp("vector", vst, pbk[:, 0:256],
                        waits=[(prd, prd.val), (stg, stg.val)],
                        incs=[_inc(vev), _inc(pfr)])
                dstap = A_(self.g4i, D * TOK + j * 128 * D + oh * 256,
                           (D, 128), (1, 256))
                self.dma("sync", dstap, vst, stg, waits=[(vev, vev.val)])
            self.P.raw("sync", lambda e: e.nop(), waits=[(stg, stg.val)],
                       incs=[_inc(wfree[slot])])
        self.g4stage = stg
        self.qspill = qspill

    def emit_g4(self):
        g4k = self.c("g4k_done")
        g4v = self.c("g4v_done")
        self.coll(self.g4i[0:1, :], self.g4oK[:, :],
                  [[0, 1, 2, 3], [4, 5, 6, 7]],
                  waits=[(self.g4stage, self.kstg)], incs=[_inc(g4k)])
        self.coll(self.g4i[1:2, :], self.g4oV[:, :],
                  [[0, 1, 2, 3], [4, 5, 6, 7]],
                  waits=[(self.g4stage, self.g4stage.val)], incs=[_inc(g4v)])
        self.g4kdone, self.g4vdone = g4k, g4v

    def emit_attention(self):
        kld = [self.c("k_ld0"), self.c("k_ld1")]
        vld = [self.c("v_ld0"), self.c("v_ld1")]
        kvfr = [self.c("kv_fr0"), self.c("kv_fr1")]
        qld = [self.c("q_ld0"), self.c("q_ld1")]
        qfr = [self.c("q_fr0"), self.c("q_fr1")]
        aready = self.c("at_exp")
        atfree = self.c("at_free")
        amsk = self.c("at_msk")
        dnm = self.c("at_dnm")
        oev = self.c("at_oev")
        odma = self.c("at_odma")
        nquad = 0
        for h in range(NH):
            hb = h % 2
            for r in range(4):
                src = A_(self.g4oK, r * D * TOK + h * 128 * TOK,
                         (TOK, 128), (1, TOK))
                self.dma("sync", self.ar2b(KH_O[hb] + r * 1024, 1024), src,
                         kld[hb],
                         waits=[(self.g4kdone, 1), (kvfr[hb], kvfr[hb].val)])
            for r in range(4):
                src = A_(self.g4oV, r * D * TOK + h * 128,
                         (D, 128), (128 * D, 8), (1, 128))
                dst = A_(self.AR2, VH_O[hb] + r * 128, (24576, 128), (512, 8),
                         (1, 128))
                self.dma("sync", dst, src, vld[hb],
                         waits=[(self.g4vdone, 1), (kvfr[hb], kvfr[hb].val)])
            KH = self.ar2b(KH_O[hb], 4096)
            VH = self.ar2b(VH_O[hb], 4096)
            for quad in range(2):
                j0 = 4 * quad
                qb = nquad % 2
                QH = self.ar2b(QH_O[qb], 512)
                self.dma("sync", QH,
                         self.dQ[h * 128:(h + 1) * 128, quad * 512:quad * 512 + 512],
                         qld[qb], waits=[(self.qspill, self.qspill.val),
                                         (qfr[qb], qfr[qb].val)])
                a0, d0, m0 = aready.val, dnm.val, amsk.val
                for t in range(32):
                    koff = (t % 4) * 1024 + (t // 4) * 128
                    bi = t % 4
                    pbk, prd, pfr = self.bank(bi)
                    gidx = 32 * nquad + t
                    w = [(kld[hb], kld[hb].val), (qld[qb], qld[qb].val)]
                    if gidx >= 4:
                        w.append((aready, gidx - 3))
                    self.mm(pbk[:, :], KH[:, koff:koff + 128], QH, True, True,
                            waits=w, incs=[_inc(prd)])
                    we = [(prd, prd.val)]
                    if nquad > 0:
                        dp, avb, navp = self.at_prev
                        we.append((dnm, dp + t + 1))
                        if t < navp:
                            we.append((atfree, avb + t + 1))
                    self.act(self.AR1[:, t * 512:(t + 1) * 512], pbk[:, :], AF.Exp,
                             scale=self.CONSTC[:, 1:2],
                             waits=we, incs=[_inc(aready)])
                pb4, prd4, pfr4 = self.bank(4)
                for t in range(32):
                    self.mm(pb4[0:1, :], self.ONESK[:, :],
                            self.AR1[:, t * 512:(t + 1) * 512], t == 0, t == 31,
                            waits=[(aready, a0 + t + 1), self.ones,
                                   (pfr4, pfr4.val)],
                            incs=[_inc(dnm)] + ([_inc(prd4)] if t == 31 else []))
                for jj in range(4):
                    for m in range(4):
                        t = 4 * (j0 + jj) + m
                        sl = self.AR1[:, t * 512 + jj * 128:t * 512 + (jj + 1) * 128]
                        self.tt("vector", sl, sl,
                                self.CONST[:, 384 + m * 128:384 + (m + 1) * 128],
                                ALU.mult,
                                waits=[(dnm, d0 + t + 1), self.ld],
                                incs=[_inc(amsk)])
                psk, prd5, pfr5 = self.bank(5)
                nav = 4 * j0 + 16
                avb = atfree.val
                for t in range(nav):
                    c0 = 128 * max(0, min(4, t // 4 - j0))
                    w = [(aready, a0 + t + 1), (pfr5, pfr5.val),
                         (vld[hb], vld[hb].val)]
                    if j0 <= t // 4 <= j0 + 3:
                        w.append((amsk, m0 + (t // 4 - j0) * 4 + (t % 4) + 1))
                    self.mm(psk[:, c0:512], VH[:, t * 128:(t + 1) * 128],
                            self.AR1[:, t * 512 + c0:(t + 1) * 512],
                            t == 0, t == nav - 1, waits=w,
                            incs=[_inc(atfree)]
                            + ([_inc(prd5)] if t == nav - 1 else []))
                self.at_prev = (d0, avb, nav)
                self.recip(self.AROW[0:1, 0:512], pb4[0:1, :],
                           waits=[(prd4, prd4.val)], incs=[_inc(oev), _inc(pfr4)])
                self.mm(pb4[:, :], self.ONES1F[0:1, 0:128], self.AROW[0:1, 0:512],
                        True, True, waits=[(oev, oev.val), (pfr4, pfr4.val)],
                        incs=[_inc(prd4)])
                OST = self.ar2b(OST_O[qb], 512)
                RBF = self.ar2b(18432 + (qb % 2) * 1024, 1024).bitcast(F32)
                self.cp("vector", RBF, pb4[:, :],
                        waits=[(prd4, prd4.val), (oev, oev.val)],
                        incs=[_inc(oev), _inc(pfr4)])
                self.tt("vector", OST, psk[:, :], RBF, ALU.mult,
                        waits=[(oev, oev.val), (prd5, prd5.val),
                               (odma, odma.val)],
                        incs=[_inc(oev), _inc(pfr5)])
                self.dma("sync", self.dO[h * 128:(h + 1) * 128,
                                         quad * 512:quad * 512 + 512], OST,
                         odma, waits=[(oev, oev.val)],
                         extra=[_inc(qfr[qb])])
                nquad += 1
            self.P.raw("sync", lambda e: e.nop(), waits=[(odma, odma.val)],
                       incs=[_inc(kvfr[hb])])
        self.odma = odma
        self.atfree = atfree

    def ln(self, j, ready, tag):
        eS = self.c("ln_s")
        e = self.c("ln_v")
        src = self.X[:, j * D:(j + 1) * D]
        wq = self.c("wp2_out")
        for q in range(4):
            sl = src[:, q * 512:(q + 1) * 512]
            self.tt("vector", self.WPST[:, (q % 2) * 512:(q % 2) * 512 + 512],
                    sl, sl, ALU.mult,
                    waits=[(ready, ready.val), (wq, wq.val), (eS, eS.val),
                           (e, e.val)],
                    incs=[_inc(e)], accum=self.STAT[:, 2 + q:3 + q])
        self.act(src[:, 0:1024], src[:, 0:1024], AF.Identity,
                 accum=self.STAT[:, 0:1],
                 waits=[(ready, ready.val), (e, e.val)], incs=[_inc(eS)])
        self.act(src[:, 1024:2048], src[:, 1024:2048], AF.Identity,
                 accum=self.STAT[:, 1:2],
                 waits=[(ready, ready.val), (eS, eS.val)], incs=[_inc(eS)])
        self.tt("vector", self.STAT[:, 6:7], self.STAT[:, 0:1], self.STAT[:, 1:2],
                ALU.add, waits=[(e, e.val), (eS, eS.val)], incs=[_inc(e)])
        self.ts("vector", self.STAT[:, 6:7], self.STAT[:, 6:7], -1.0 / D, None,
                ALU.mult, waits=[(e, e.val)], incs=[_inc(e)])
        self.tt("vector", self.STAT[:, 7:8], self.STAT[:, 2:3], self.STAT[:, 3:4],
                ALU.add, waits=[(e, e.val)], incs=[_inc(e)])
        self.tt("vector", self.STAT[:, 8:9], self.STAT[:, 4:5], self.STAT[:, 5:6],
                ALU.add, waits=[(e, e.val)], incs=[_inc(e)])
        self.tt("vector", self.STAT[:, 7:8], self.STAT[:, 7:8], self.STAT[:, 8:9],
                ALU.add, waits=[(e, e.val)], incs=[_inc(e)])
        self.ts("vector", self.STAT[:, 7:8], self.STAT[:, 7:8], 1.0 / D, EPS_LN,
                ALU.mult, ALU.add, waits=[(e, e.val)], incs=[_inc(e)])
        self.tt("vector", self.STAT[:, 8:9], self.STAT[:, 6:7], self.STAT[:, 6:7],
                ALU.mult, waits=[(e, e.val)], incs=[_inc(e)])
        self.tt("vector", self.STAT[:, 7:8], self.STAT[:, 7:8], self.STAT[:, 8:9],
                ALU.subtract, waits=[(e, e.val)], incs=[_inc(e)])
        self.act(self.STAT[:, 9:10], self.STAT[:, 7:8], AF.Sqrt,
                 waits=[(e, e.val)], incs=[_inc(eS)])
        self.recip(self.STAT[:, 10:11], self.STAT[:, 9:10],
                   waits=[(eS, eS.val)], incs=[_inc(e)])
        self.ts("vector", src, src, self.STAT[:, 6:7], self.STAT[:, 10:11],
                ALU.add, ALU.mult, waits=[(e, e.val)],
                incs=[_inc(self.stream_ready)])

    def emit_wo(self):
        old = self.c("wo_old")
        src = A_(self.dO, 0, (TOK, 128), (128 * TOK, 16), (1, TOK))
        dst = A_(self.AR1, 0, (16 * TOK, 128), (TOK, 16), (1, TOK))
        self.dma("sync", dst, src, old,
                 waits=[(self.odma, self.odma.val), (self.atfree, self.atfree.val)])
        wdma = [self.c("wb_d0"), self.c("wb_d1")]
        wfree = [self.c("wb_f0"), self.c("wb_f1")]
        acc = self.c("wo_acc")
        browc = self.c("ld_init_g")
        self.dma("gpsimd", self.BROW[:, :], self.i_bias_rows[1:2, :], browc,
                 waits=[(self.g4stage, self.g4stage.val)])
        for og in range(4):
            for half in range(2):
                dstw = A_(self.WB[half], 0, (4096, 128), (256, 16), (1, 256))
                srcw = A_(self.i_wo, og * 512 + half * 256,
                          (D, 128), (128 * D, 16), (1, 256))
                self.dma("gpsimd", dstw, srcw, wdma[half],
                         waits=[(wfree[half], wfree[half].val)])
            for j in range(NT):
                bi = j % 2
                pbk, prd, pfr = self.bank(bi)
                for k in range(16):
                    lhs = self.AR1[:, k * TOK + j * 128:k * TOK + (j + 1) * 128]
                    for half in range(2):
                        self.mm(pbk[:, half * 256:half * 256 + 256], lhs,
                                self.WB[half][:, k * 256:(k + 1) * 256],
                                k == 0 and half == 0, False,
                                waits=[(old, old.val),
                                       (wdma[half], wdma[half].val),
                                       (pfr, pfr.val)])
                self.mm(pbk[:, :], self.ONES1[0:1, 0:128],
                        self.BROW[0:1, og * 512:og * 512 + 512], False, True,
                        waits=[self.ones, (browc, browc.val)], incs=[_inc(prd)])
                xs = self.X[:, j * D + og * 512:j * D + og * 512 + 512]
                self.stt("vector", xs, pbk[:, :], 1.0, xs, ALU.mult, ALU.add,
                         waits=[(prd, prd.val)], incs=[_inc(acc), _inc(pfr)])
            for half in range(2):
                self.P.raw("vector", lambda e: e.nop(),
                           waits=[(acc, acc.val)], incs=[_inc(wfree[half])])
        self.stream_base = self.c("stream").val
        self.stream_ready = self.c("stream")
        for j in range(NT):
            self.ln(j, acc, f"b1_{j}")

    def emit_tap(self, i):
        tc = self.c("tapdma")
        for j in range(NT):
            self.dma("scalar", self.o_taps[i][j * 128:(j + 1) * 128, :],
                     self.X[:, j * D:(j + 1) * D], tc,
                     waits=[(self.stream_ready, self.stream_ready.val)])

    def tap_gate(self):
        if not TAPS:
            return []
        tc = self.c("tapdma")
        return [(tc, tc.val)]

    def fwht_stages(self, n, base, width, first_waits, done):
        phys = list(range(n))
        spare = n
        hc = 1
        first = True
        while hc < n:
            for g0 in range(0, n, 2 * hc):
                for i in range(hc):
                    a, b = g0 + i, g0 + hc + i
                    pa, pb_ = phys[a], phys[b]
                    ta = self.ar2b(base + pa * width, width)
                    tb = self.ar2b(base + pb_ * width, width)
                    ts_ = self.ar2b(base + spare * width, width)
                    w = (list(first_waits) if first else []) + [(done, done.val)]
                    self.tt("vector", ts_, ta, tb, ALU.add, waits=w,
                            incs=[_inc(done)])
                    self.tt("vector", ta, ta, tb, ALU.subtract,
                            waits=[(done, done.val)], incs=[_inc(done)])
                    phys[a], phys[b], spare = spare, pa, pb_
            first = False
            hc *= 2
        return phys

    def quant_fm(self, slotmap, base, width, nslots, tag, sw_cell,
                 ready, c1_const=None, c1_from_rms=False):
        assert width == 256
        e = self.c(tag + "_e")
        qd = self.c(tag + "_qd")
        MT = self.ar2b(MAXT_O, 256)
        MN = self.ar2b(GT_O, 256)
        s0 = self.ar2b(base + slotmap[0] * 256, 256)
        self.cp("vector", MT, s0, waits=[ready, (e, e.val)], incs=[_inc(e)])
        self.cp("vector", MN, s0, waits=[(e, e.val)], incs=[_inc(e)])
        for i in range(1, nslots):
            si = self.ar2b(base + slotmap[i] * 256, 256)
            self.tt("vector", MT, MT, si, ALU.max,
                    waits=[(e, e.val)], incs=[_inc(e)])
            self.tt("vector", MN, MN, si, ALU.min,
                    waits=[(e, e.val)], incs=[_inc(e)])
        self.ts("vector", MN, MN, -1.0, None, ALU.mult,
                waits=[(e, e.val)], incs=[_inc(e)])
        self.tt("vector", MT, MT, MN, ALU.max,
                waits=[(e, e.val)], incs=[_inc(e)])
        if c1_from_rms:
            pb4, prd4, pfr4 = self.bank(4)
            TMPB = self.ar2b(TMPF_O, 256)
            sq = self.c(tag + "_sq")
            sqm = self.c(tag + "_sqm")
            for i in range(nslots):
                si = self.ar2b(base + slotmap[i] * 256, 256)
                self.tt("vector", TMPB, si, si, ALU.mult,
                        waits=[ready, (pfr4, pfr4.val), (sqm, sqm.val)],
                        incs=[_inc(sq)])
                self.mm(pb4[0:1, 0:256], self.ONESK[:, 0:1], TMPB, i == 0,
                        i == nslots - 1,
                        waits=[(sq, sq.val), self.ones, (pfr4, pfr4.val)],
                        incs=[_inc(sqm)] + ([_inc(prd4)] if i == nslots - 1 else []))
            self.ts("vector", self.AROW[0:1, 768:1024], pb4[0:1, 0:256],
                    1.0 / DI, EPS_RMS, ALU.mult, ALU.add,
                    waits=[(prd4, prd4.val)], incs=[_inc(e), _inc(pfr4)])
            qfs = self.c("f1_ss")
            self.P.raw("scalar", lambda sc, o=self.AROW[0:1, 768:1024]:
                       sc.activation(o, o, AF.Sqrt, bias=0.0, scale=1.0),
                       waits=[(e, e.val)], incs=[_inc(qfs)])
            self.recip(self.AROW[0:1, 768:1024], self.AROW[0:1, 768:1024],
                       waits=[(qfs, qfs.val)], incs=[_inc(e)])
            self.ts("vector", self.AROW[0:1, 768:1024], self.AROW[0:1, 768:1024],
                    ISQ8192, None, ALU.mult, waits=[(e, e.val)], incs=[_inc(e)])
        for half in range(2):
            pt, prd, pfr = self.pt[0], self.pt_rd[0], self.pt_fr[0]
            self.tr(pt[:, 0:128], MT[:, half * 128:(half + 1) * 128],
                    waits=[(e, e.val), (pfr, pfr.val)], incs=[_inc(prd)])
            self.red(self.STAT[:, 10:11], pt[:, 0:128], ALU.max,
                     waits=[(prd, prd.val)], incs=[_inc(e), _inc(pfr)])
            pb4, prd4, pfr4 = self.bank(4)
            self.mm(pb4[0:1, half * 128:(half + 1) * 128], self.STAT[:, 10:11],
                    self.IDENTF[:, :], True, True,
                    waits=[(e, e.val), (pfr4, pfr4.val), self.ld],
                    incs=[_inc(prd4)])
            self.cp("vector", self.AROW[0:1, half * 128:(half + 1) * 128],
                    pb4[0:1, half * 128:(half + 1) * 128],
                    waits=[(prd4, prd4.val)], incs=[_inc(e), _inc(pfr4)])
        if c1_from_rms:
            self.tt("vector", self.AROW[0:1, 0:256], self.AROW[0:1, 0:256],
                    self.AROW[0:1, 768:1024], ALU.mult, waits=[(e, e.val)],
                    incs=[_inc(e)])
            self.ts("vector", self.AROW[0:1, 0:256], self.AROW[0:1, 0:256],
                    EPS_Q, None, ALU.max, waits=[(e, e.val)], incs=[_inc(e)])
        else:
            self.ts("vector", self.AROW[0:1, 0:256], self.AROW[0:1, 0:256],
                    c1_const, EPS_Q, ALU.mult, ALU.max, waits=[(e, e.val)],
                    incs=[_inc(e)])
        self.P.raw("vector", lambda v, o=self.AROW[0:1, 512:768],
                   i=self.AROW[0:1, 0:256], s=sw_cell:
                   v.tensor_scalar(o, i, s, 1.0 / 127.0, ALU.mult, ALU.mult),
                   waits=[(e, e.val), (self.wp1_scales, self.wp1_scales.val)],
                   incs=[_inc(e)])
        self.recip(self.AROW[0:1, 1280:1536], self.AROW[0:1, 0:256],
                   waits=[(e, e.val)], incs=[_inc(e)])
        if c1_from_rms:
            self.tt("vector", self.AROW[0:1, 1280:1536], self.AROW[0:1, 1280:1536],
                    self.AROW[0:1, 768:1024], ALU.mult, waits=[(e, e.val)],
                    incs=[_inc(e)])
            self.ts("vector", self.AROW[0:1, 1280:1536], self.AROW[0:1, 1280:1536],
                    127.0, None, ALU.mult, waits=[(e, e.val)], incs=[_inc(e)])
        else:
            self.ts("vector", self.AROW[0:1, 1280:1536], self.AROW[0:1, 1280:1536],
                    127.0 * c1_const, None, ALU.mult, waits=[(e, e.val)],
                    incs=[_inc(e)])
        pb4, prd4, pfr4 = self.bank(4)
        self.mm(pb4[:, 0:256], self.ONES1F[0:1, 0:128], self.AROW[0:1, 1280:1536],
                True, True, waits=[(e, e.val), (pfr4, pfr4.val), self.ones],
                incs=[_inc(prd4)])
        TMPF = self.ar2f(TMPF_O, 256)
        for i in range(nslots):
            si = self.ar2b(base + slotmap[i] * 256, 256)
            self.tt("vector", TMPF, si, pb4[:, 0:256], ALU.mult,
                    waits=[(prd4, prd4.val), (e, e.val), (qd, qd.val)],
                    incs=[_inc(e)])
            self.ts("vector", si, TMPF, MAGIC, MAGIC, ALU.add, ALU.subtract,
                    waits=[(e, e.val)], incs=[_inc(qd)])
        self.P.raw("vector", lambda e_: e_.engine_nop(), waits=[(qd, qd.val)],
                   incs=[_inc(pfr4)])
        return qd

    def emit_l0ffn(self):
        x1t = self.stream_transpose(self.xT, "x1T",
                                    gate_per_tile=(self.stream_ready,
                                                   self.stream_base))
        wdma = [self.c("ws_d0"), self.c("ws_d1")]
        wfree = [self.c("wb_f0"), self.c("wb_f1")]
        ns2 = self.c("stream")
        for tt in range(4):
            hmm = self.c("f0_h")
            for dc in range(16):
                bi = dc % 2
                pbk, prd, pfr = self.bank(bi)
                self.mm(pbk[:, 0:256], self.CONST[:, 128:256],
                        self.xT[:, dc * TOK + tt * 256:dc * TOK + tt * 256 + 256],
                        True, True,
                        waits=[(x1t, x1t.val), self.ld, (pfr, pfr.val)],
                        incs=[_inc(prd)])
                self.cp("vector", self.ar2b(FW_O + dc * 256, 256), pbk[:, 0:256],
                        waits=[(prd, prd.val)], incs=[_inc(hmm), _inc(pfr)])
            bd = self.c("f0_bf")
            smap = self.fwht_stages(16, FW_O, 256, [(hmm, hmm.val)], bd)
            qd = self.quant_fm(smap, FW_O, 256, 16, "qf",
                               self.G1S[0:1, 8:9], (bd, bd.val),
                               c1_const=ISQ2048)
            self.P.raw("vector", lambda v, o=self.AROW[0:1, 1536:1792],
                       i=self.AROW[0:1, 0:256], s=self.G1S[0:1, 9:10]:
                       v.tensor_scalar(o, i, s, 1.0 / 127.0, ALU.mult, ALU.mult),
                       waits=[(qd, qd.val),
                              (self.wp1_scales, self.wp1_scales.val)],
                       incs=[_inc(qd)])
            if DBG0 and tt == 0:
                self.smap_dbg = list(smap)
                self.dma("sync", self.dbg_fw[:, :], self.ar2b(FW_O, 4352),
                         self.c("dbgd"), waits=[(qd, qd.val)])
                self.dma("sync", self.dbg_arow[0:1, 0:256],
                         self.AROW[0:1, 0:256], self.c("dbgd"),
                         waits=[(qd, qd.val)])
                self.dma("sync", self.dbg_arow[0:1, 512:768],
                         self.AROW[0:1, 512:768], self.c("dbgd"),
                         waits=[(qd, qd.val)])
                self.dma("sync", self.dbg_arow[0:1, 1280:1536],
                         self.AROW[0:1, 1280:1536], self.c("dbgd"),
                         waits=[(qd, qd.val)])
            agb = self.c("f0_agb")
            pb4, prd4, pfr4 = self.bank(4)
            self.mm(pb4[:, 0:256], self.ONES1F[0:1, 0:128], self.AROW[0:1, 512:768],
                    True, True, waits=[(qd, qd.val), (pfr4, pfr4.val)],
                    incs=[_inc(prd4)])
            AGB = self.ar2f(AGB_O, 256)
            self.cp("vector", AGB, pb4[:, 0:256], waits=[(prd4, prd4.val)],
                    incs=[_inc(agb), _inc(pfr4)])
            zready = self.c("f0_z")
            GTt = self.ar2b(GT_O, 256)
            SGt = self.ar2b(SG_O, 256)
            for og in range(32):
                wv = [0, 0]
                for mat in range(2):
                    gsrc = self.g2go if mat == 0 else self.g2uo
                    dstw = A_(self.WB[mat], 0, (4096, 128), (256, 16), (128, 2),
                              (1, 128))
                    srcw = A_(gsrc, og * 256, (DI, 128), (128 * DI, 16), (128, 2),
                              (1, 128))
                    self.dma("sync", dstw, srcw, wdma[mat],
                             waits=[(wfree[mat], wfree[mat].val),
                                    (self.g23, mat + 1)])
                    wv[mat] = wdma[mat].val
                for oc in range(2):
                    och = og * 2 + oc
                    gb, ub = (0, 1) if och % 2 == 0 else (2, 3)
                    for mat, bi in ((0, gb), (1, ub)):
                        pbk, prd, pfr = self.bank(bi)
                        for k in range(16):
                            self.mm(pbk[:, 0:256],
                                    self.WB[mat][:, (k * 2 + oc) * 128:
                                                 (k * 2 + oc + 1) * 128],
                                    self.ar2b(FW_O + smap[k] * 256, 256),
                                    k == 0, k == 15,
                                    waits=[(wdma[mat], wv[mat]), (qd, qd.val),
                                           (pfr, pfr.val)],
                                    incs=([_inc(prd)] if k == 15 else []))
                    gbk, gprd, gpfr = self.bank(gb)
                    ubk, uprd, upfr = self.bank(ub)
                    zs = self.c("f0_zs")
                    self.tt("vector", GTt, gbk[:, 0:256], AGB, ALU.mult,
                            waits=[(gprd, gprd.val), (agb, agb.val),
                                   (zs, zs.val)],
                            incs=[_inc(zready), _inc(gpfr)])
                    self.act(SGt, GTt, AF.Silu, waits=[(zready, zready.val)],
                             incs=[_inc(zs)])
                    self.tt("vector", self.ar2b(Z_O + och * 256, 256), SGt,
                            ubk[:, 0:256], ALU.mult,
                            waits=[(zs, zs.val), (uprd, uprd.val)]
                            + getattr(self, "_dbgz_gate", []),
                            incs=[_inc(zready), _inc(upfr)])
                for mat in range(2):
                    self.P.raw("vector", lambda e: e.nop(),
                               waits=[(zready, zready.val)],
                               incs=[_inc(wfree[mat])])
            dbg_gate = []
            if DBG0 and tt == 0:
                dd = self.c("dbgd")
                self.dma("sync", self.dbg_z[:, :], self.ar2b(Z_O, 16640),
                         dd, waits=[(zready, zready.val)])
                dbg_gate = [(dd, dd.val)]
                self._dbgz_gate = dbg_gate
            h2 = self.c("f0_h2")
            for sl in range(64):
                bi = sl % 2
                pbk, prd, pfr = self.bank(bi)
                zt = self.ar2b(Z_O + sl * 256, 256)
                self.mm(pbk[:, 0:256], self.CONST[:, 128:256], zt, True, True,
                        waits=[(zready, zready.val), (pfr, pfr.val)] + dbg_gate,
                        incs=[_inc(prd)])
                self.cp("vector", zt, pbk[:, 0:256], waits=[(prd, prd.val)],
                        incs=[_inc(h2), _inc(pfr)])
            bd2 = self.c("f0_bf2")
            smap2 = self.fwht_stages(64, Z_O, 256, [(h2, h2.val)], bd2)
            qd2 = self.quant_fm(smap2, Z_O, 256, 64, "qf",
                                self.G1S[0:1, 10:11], (bd2, bd2.val),
                                c1_const=ISQ8192)
            if DBG0 and tt == 0:
                self.smap2_dbg = list(smap2)
                dd = self.c("dbgd")
                self.dma("sync", self.dbg_z2[:, :], self.ar2b(Z_O, 16640),
                         dd, waits=[(qd2, qd2.val)])
                self._dbgz_gate = [(dd, dd.val)]
            self.tt("vector", self.AROW[0:1, 512:768], self.AROW[0:1, 512:768],
                    self.AROW[0:1, 1536:1792], ALU.mult,
                    waits=[(qd2, qd2.val)], incs=[_inc(qd2)])
            dacc = self.c("f0_dacc")
            bcol = self.c("f0_bcol")
            for mh in range(2):
                pb4, prd4, pfr4 = self.bank(4)
                self.mm(pb4[:, mh:mh + 1],
                        self.AROW[0:1, 512 + mh * 128:512 + (mh + 1) * 128],
                        self.ONES1F[0:1, 0:1], True, True,
                        waits=[(qd2, qd2.val), (pfr4, pfr4.val), self.ones],
                        incs=[_inc(prd4)])
                self.cp("vector", self.STAT[:, 12 + mh:13 + mh], pb4[:, mh:mh + 1],
                        waits=[(prd4, prd4.val)], incs=[_inc(bcol), _inc(pfr4)])
            for og in range(4):
                for kl in range(32):
                    slot = kl % 2
                    dstw = A_(self.WB[slot], 0, (4096, 128), (512, 2), (1, 512))
                    srcw = A_(self.g2do, kl * 256 * D + og * 512,
                              (D, 128), (128 * D, 2), (1, 512))
                    self.dma("sync", dstw, srcw, wdma[slot],
                             waits=[(wfree[slot], wfree[slot].val),
                                    (self.g23, 3)])
                    wv_ = wdma[slot].val
                    for kc2 in range(2):
                        kc = kl * 2 + kc2
                        for mh in range(2):
                            pbk, prd, pfr = self.bank(mh)
                            last_of_slot = (kc2 == 1 and mh == 1)
                            ii = ([_inc(prd)] if kc == 63 else [])
                            if last_of_slot:
                                ii = ii + [_inc(wfree[slot])]
                            self.mm(pbk[:, :],
                                    self.ar2b(Z_O + smap2[kc] * 256 + mh * 128, 128),
                                    self.WB[slot][:, kc2 * 512:kc2 * 512 + 512],
                                    kc == 0, kc == 63,
                                    waits=[(wdma[slot], wv_), (qd2, qd2.val),
                                           (pfr, pfr.val)],
                                    incs=ii)
                for mh in range(2):
                    j = tt * 2 + mh
                    pbk, prd, pfr = self.bank(mh)
                    xs = self.X[:, j * D + og * 512:j * D + og * 512 + 512]
                    self.stt("vector", xs, pbk[:, :], self.STAT[:, 12 + mh:13 + mh],
                             xs, ALU.mult, ALU.add,
                             waits=[(prd, prd.val), (bcol, bcol.val)]
                             + self.tap_gate(),
                             incs=[_inc(dacc), _inc(pfr)])
            if tt == 0:
                self.stream_base = ns2.val
            self.stream_ready = ns2
            for mh in range(2):
                self.ln(tt * 2 + mh, dacc, f"b2_{tt}_{mh}")
        self.stream_ready = ns2

    def emit_l1attn(self):
        x2t = self.stream_transpose(self.xT, "x2T",
                                    gate_per_tile=(self.stream_ready,
                                                   self.stream_base))
        wdma = [self.c("wb_d0"), self.c("wb_d1")]
        wfree = [self.c("wb_f0"), self.c("wb_f1")]
        spill = self.c("a1_spill")
        aev = self.c("qkv_ev")
        browc = self.c("ld_init_g")
        for mat, wsrc, ddst, brow in ((0, self.i_wu1T, self.dU, 2),
                                      (1, self.i_wv1T, self.dV, 3)):
            self.dma("gpsimd", self.BROW[:, :], self.i_bias_rows[brow:brow + 1, :],
                     browc, waits=[(aev, aev.val),
                                   (self.c("wo_acc"), self.c("wo_acc").val)])
            for oh in range(8):
                slot = oh % 2
                dstw = A_(self.WB[slot], 0, (4096, 128), (256, 16), (1, 256))
                srcw = A_(wsrc, oh * 256, (D, 128), (128 * D, 16), (1, 256))
                self.dma("gpsimd", dstw, srcw, wdma[slot],
                         waits=[(wfree[slot], wfree[slot].val)])
                for j in range(NT):
                    bi = j % 2
                    pbk, prd, pfr = self.bank(bi)
                    for k in range(16):
                        self.mm(pbk[:, 0:256],
                                self.xT[:, k * TOK + j * 128:k * TOK + (j + 1) * 128],
                                self.WB[slot][:, k * 256:(k + 1) * 256],
                                k == 0, False,
                                waits=[(wdma[slot], wdma[slot].val), (x2t, x2t.val),
                                       (pfr, pfr.val)])
                    self.mm(pbk[:, 0:256], self.ONES1[0:1, 0:128],
                            self.BROW[0:1, oh * 256:oh * 256 + 256],
                            False, True, waits=[self.ones, (browc, browc.val)],
                            incs=[_inc(prd)])
                    ust = self.ar2b(UJ_O + (j % 2) * 2048, 256)
                    self.cp("vector", ust, pbk[:, 0:256],
                            waits=[(prd, prd.val), (spill, spill.val)],
                            incs=[_inc(aev), _inc(pfr)])
                    self.dma("sync",
                             ddst[j * 128:(j + 1) * 128, oh * 256:oh * 256 + 256],
                             ust, spill, waits=[(aev, aev.val)])
                self.P.raw("sync", lambda e: e.nop(), waits=[(spill, spill.val)],
                           incs=[_inc(wfree[slot])])
        hred = self.c("a1_h")
        self.dma("gpsimd", self.BROW[:, :], self.i_bias_rows[6:7, :], browc,
                 waits=[(aev, aev.val)])
        for half in range(2):
            dstw = A_(self.WB[half], 0, (4096, 128), (256, 16), (1, 256))
            srcw = A_(self.i_wg1T, half * 256, (512, 128), (128 * 512, 16), (1, 256))
            self.dma("gpsimd", dstw, srcw, wdma[half],
                     waits=[(wfree[half], wfree[half].val)])
        HF = self.ar2b(HFULL_O, 4096)
        for j in range(NT):
            bi = j % 2
            pbk, prd, pfr = self.bank(bi)
            for half in range(2):
                for k in range(16):
                    self.mm(pbk[:, half * 256:half * 256 + 256],
                            self.xT[:, k * TOK + j * 128:k * TOK + (j + 1) * 128],
                            self.WB[half][:, k * 256:(k + 1) * 256],
                            k == 0 and half == 0, False,
                            waits=[(wdma[half], wdma[half].val), (x2t, x2t.val),
                                   (pfr, pfr.val)])
            self.mm(pbk[:, :], self.ONES1[0:1, 0:128], self.BROW[0:1, 0:512],
                    False, True, waits=[self.ones, (browc, browc.val)],
                    incs=[_inc(prd)])
            self.ts("vector", HF[:, j * 512:(j + 1) * 512], pbk[:, :], 0.0, None,
                    ALU.max, waits=[(prd, prd.val)], incs=[_inc(hred), _inc(pfr)])
        for half in range(2):
            self.P.raw("vector", lambda e: e.nop(), waits=[(hred, hred.val)],
                       incs=[_inc(wfree[half])])
        ht = self.c("a1_ht")
        for j in range(NT):
            pt, prd, pfr = self.pt[j % 2], self.pt_rd[j % 2], self.pt_fr[j % 2]
            for hc in range(4):
                self.tr(pt[:, hc * 128:(hc + 1) * 128],
                        HF[:, j * 512 + hc * 128:j * 512 + (hc + 1) * 128],
                        waits=[(hred, hred.val), (pfr, pfr.val)],
                        incs=([_inc(prd)] if hc == 3 else []))
            dv = A_(self.AR2, HT_O + j * 128, (24576, 128), (TOK, 4), (1, 128))
            sv = A_(pt, 0, (1024, 128), (128, 4), (1, 128))
            self.cp("vector", dv, sv, waits=[(prd, prd.val)],
                    incs=[_inc(ht), _inc(pfr)])
        for half in range(2):
            dstw = A_(self.WB[half], 0, (4096, 128), (2048, 2), (1, 2048))
            srcw = A_(self.i_wg2T, half * 2 * 128 * D, (D, 128), (128 * D, 2),
                      (1, 2048))
            self.dma("gpsimd", dstw, srcw, wdma[half],
                     waits=[(wfree[half], wfree[half].val)])
        uvld = self.c("a1_uvld")
        self.dma("gpsimd", self.BROW[:, :], self.i_bias_rows[4:5, :], browc,
                 waits=[(hred, hred.val)])
        gdone = self.c("a1_g")
        ugd = self.c("a1_ug")
        tdma = self.c("g5_stage")
        totd = self.c("a1_ug")
        GA = self.ar2b(GATES_O, 2048)
        UJ = self.ar2b(UJ_O, 2048)
        VJ = self.ar2b(VJ_O, 2048)
        gd0 = gdone.val
        for j in range(NT):
            for og in range(4):
                pbk, prd, pfr = self.bank(og)
                for kc in range(4):
                    ws = kc // 2
                    lhs = A_(self.AR2, HT_O + kc * 1024 + j * 128, (24576, 128),
                             (1, 128))
                    w = [(wdma[ws], wdma[ws].val), (ht, ht.val),
                         (pfr, pfr.val)]
                    if j > 0 and kc == 0:
                        w.append((gdone, gd0 + 4 * (j - 1) + og + 1))
                    self.mm(pbk[:, :], lhs,
                            self.WB[ws][:, (kc % 2) * 2048 + og * 512:
                                        (kc % 2) * 2048 + og * 512 + 512],
                            kc == 0, False, waits=w)
                self.mm(pbk[:, :], self.ONES1[0:1, 0:128],
                        self.BROW[0:1, og * 512:og * 512 + 512], False, True,
                        waits=[self.ones, (browc, browc.val)], incs=[_inc(prd)])
                self.act(GA[:, og * 512:og * 512 + 512], pbk[:, :], AF.Sigmoid,
                         waits=[(prd, prd.val), (ugd, ugd.val)],
                         incs=[_inc(gdone)])
            self.dma("sync", UJ, self.dU[j * 128:(j + 1) * 128, :], uvld,
                     waits=[(spill, spill.val), (ugd, ugd.val),
                            (totd, totd.val)])
            self.dma("sync", VJ, self.dV[j * 128:(j + 1) * 128, :], uvld,
                     waits=[(spill, spill.val), (ugd, ugd.val),
                            (totd, totd.val)])
            self.tt("vector", UJ, UJ, GA, ALU.mult,
                    waits=[(gdone, gdone.val), (uvld, uvld.val)], incs=[_inc(ugd)])
            self.tt("vector", VJ, VJ, GA, ALU.mult, incs=[_inc(ugd)])
            self.dma("sync", self.dUG[j * 128:(j + 1) * 128, :], UJ, spill,
                     waits=[(ugd, ugd.val)])
            self.dma("sync", self.dVG[j * 128:(j + 1) * 128, :], VJ, spill,
                     waits=[(ugd, ugd.val)])
            for og in range(4):
                pb4, prd4, pfr4 = self.bank(4)
                self.mm(pb4[0:1, :], self.ONESK[:, :],
                        UJ[:, og * 512:og * 512 + 512], True, True,
                        waits=[(ugd, ugd.val), self.ones, (pfr4, pfr4.val)],
                        incs=[_inc(prd4)])
                self.cp("vector", self.AROW[0:1, 1280:1792], pb4[0:1, :],
                        waits=[(prd4, prd4.val), (tdma, tdma.val)],
                        incs=[_inc(totd), _inc(pfr4)])
                self.dma("scalar", self.g5i[j:j + 1, og * 512:og * 512 + 512],
                         self.AROW[0:1, 1280:1792], tdma, waits=[(totd, totd.val)])
        for j in range(NT):
            self.dma("scalar", self.g5i[8 + 2 * j:9 + 2 * j, :],
                     self.X[0:1, j * D:(j + 1) * D], tdma,
                     waits=[(self.stream_ready, self.stream_ready.val)])
            self.dma("scalar", self.g5i[9 + 2 * j:10 + 2 * j, :],
                     self.X[127:128, j * D:(j + 1) * D], tdma,
                     waits=[(self.stream_ready, self.stream_ready.val)])
        g5done = self.c("g5_done")
        self.coll(self.g5i[:, :], self.g5o[:, :], [[0, 1, 2, 3], [4, 5, 6, 7]],
                  waits=[(tdma, tdma.val)], incs=[_inc(g5done)])
        g5ld = self.c("ld_init_g")
        G5Sb = self.AR2[0:96, G5S_O:G5S_O + 2048]
        self.dma("gpsimd", G5Sb, self.g5o[:, :], g5ld,
                 waits=[(g5done, 1), (spill, spill.val), (ugd, ugd.val)])
        seld = g5ld
        cbd = self.c("a1_cb")
        for r in range(4):
            self.dma("gpsimd", self.BROW[:, :], self.i_conv_rows[r:r + 1, :],
                     browc, waits=[(cbd, cbd.val)])
            for og in range(4):
                pb4, prd4, pfr4 = self.bank(4)
                self.mm(pb4[:, :], self.ONES1[0:1, 0:128],
                        self.BROW[0:1, og * 512:og * 512 + 512], True, True,
                        waits=[(browc, browc.val), self.ones, (pfr4, pfr4.val)],
                        incs=[_inc(prd4)])
                self.cp("vector", self.ar2b(CB_O + r * 2048 + og * 512, 512),
                        pb4[:, :], waits=[(prd4, prd4.val)],
                        incs=[_inc(cbd), _inc(pfr4)])
        p2 = self.c("a1_p2")
        gotd = self.c("a1_got")
        ld2 = self.c("a1_uvld")
        UG = self.ar2b(UGJ_O, 2048)
        VG = self.ar2b(VGJ_O, 2048)
        GO = self.ar2b(GO_O, 2048)
        GOT = self.ar2b(GOTJ_O, 2048)
        TOUT = self.ar2f(TOUT_O, 2048)
        CVS = self.ar2f(CVS_O, 512)
        for j in range(NT):
            self.dma("sync", UG, self.dUG[j * 128:(j + 1) * 128, :], ld2,
                     waits=[(spill, spill.val), (p2, p2.val)])
            self.dma("sync", VG, self.dVG[j * 128:(j + 1) * 128, :], ld2,
                     waits=[(spill, spill.val), (p2, p2.val)])
            for og in range(4):
                pbk, prd, pfr = self.bank(og)
                self.mm(pbk[:, :], self.CONST[:, 256:384],
                        UG[:, og * 512:og * 512 + 512], True, False,
                        waits=[(ld2, ld2.val), self.ld, (pfr, pfr.val)])
                self.mm(pbk[:, :], self.WSELR[:, j * 128:(j + 1) * 128],
                        G5Sb[:, og * 512:og * 512 + 512], False, True,
                        waits=[(seld, seld.val), self.ld], incs=[_inc(prd)])
                self.tt("vector", GO[:, og * 512:og * 512 + 512], pbk[:, :],
                        VG[:, og * 512:og * 512 + 512], ALU.mult,
                        waits=[(prd, prd.val), (ld2, ld2.val)],
                        incs=[_inc(p2), _inc(pfr)])
            for half in range(2):
                pt, prd, pfr = self.pt[half], self.pt_rd[half], self.pt_fr[half]
                for ci in range(8):
                    kc = half * 8 + ci
                    self.tr(pt[:, ci * 128:(ci + 1) * 128],
                            GO[:, kc * 128:(kc + 1) * 128],
                            waits=[(p2, p2.val), (pfr, pfr.val)],
                            incs=([_inc(prd)] if ci == 7 else []))
                self.cp("vector", GOT[:, half * 1024:(half + 1) * 1024], pt[:, :],
                        waits=[(prd, prd.val), (gotd, gotd.val)],
                        incs=[_inc(p2), _inc(pfr)])
            self.dma("sync", A_(self.dGOT, j * 128, (TOK, 128), (128 * TOK, 16),
                                (1, 128)), GOT, gotd, waits=[(p2, p2.val)])
            xb = self.ar2b(UGJ_O, 2048)
            self.cp("vector", xb, self.X[:, j * D:(j + 1) * D],
                    waits=[(self.stream_ready, self.stream_ready.val),
                           (p2, p2.val)], incs=[_inc(p2)])
            for og in range(4):
                pb4, prd4, pfr4 = self.bank(4)
                psk, prd5, pfr5 = self.bank(5)
                self.mm(pb4[:, :], self.CONST[:, 896:1024],
                        xb[:, og * 512:og * 512 + 512], True, False,
                        waits=[(p2, p2.val), (pfr4, pfr4.val)])
                self.mm(pb4[:, :], self.HSELP[:, j * 128:(j + 1) * 128],
                        G5Sb[:, og * 512:og * 512 + 512],
                        False, True, waits=[(seld, seld.val)], incs=[_inc(prd4)])
                self.mm(psk[:, :], self.CONST[:, 1024:1152],
                        xb[:, og * 512:og * 512 + 512], True, False,
                        waits=[(p2, p2.val), (pfr5, pfr5.val)])
                self.mm(psk[:, :], self.HSELN[:, j * 128:(j + 1) * 128],
                        G5Sb[:, og * 512:og * 512 + 512],
                        False, True, waits=[(seld, seld.val)], incs=[_inc(prd5)])
                to = TOUT[:, og * 512:og * 512 + 512]
                self.tt("vector", to, pb4[:, :],
                        self.ar2b(CB_O + og * 512, 512), ALU.mult,
                        waits=[(prd4, prd4.val), (cbd, cbd.val)],
                        incs=[_inc(p2), _inc(pfr4)])
                self.tt("vector", CVS, self.X[:, j * D + og * 512:
                                              j * D + og * 512 + 512],
                        self.ar2b(CB_O + 2048 + og * 512, 512), ALU.mult,
                        waits=[(p2, p2.val)], incs=[_inc(p2)])
                self.tt("vector", to, to, CVS, ALU.add,
                        waits=[(p2, p2.val)], incs=[_inc(p2)])
                self.tt("vector", CVS, psk[:, :],
                        self.ar2b(CB_O + 4096 + og * 512, 512), ALU.mult,
                        waits=[(prd5, prd5.val), (p2, p2.val)],
                        incs=[_inc(p2), _inc(pfr5)])
                self.tt("vector", to, to, CVS, ALU.add,
                        waits=[(p2, p2.val)], incs=[_inc(p2)])
                self.tt("vector", to, to, self.ar2b(CB_O + 6144 + og * 512, 512),
                        ALU.add, waits=[(cbd, cbd.val), (p2, p2.val)],
                        incs=[_inc(p2)])
                xs = self.X[:, j * D + og * 512:j * D + og * 512 + 512]
                self.tt("vector", xs, xs, to, ALU.add,
                        waits=self.tap_gate() + [(p2, p2.val)], incs=[_inc(p2)])
        uvacc = self.c("a1_uv")
        g2ld2 = [self.c("wp1_ld0"), self.c("wp1_ld1")]
        for oh in range(8):
            slot = oh % 2
            dstw = A_(self.WB[slot], 0, (4096, 128), (256, 16), (1, 256))
            srcw = A_(self.i_wu1T, oh * 256, (D, 128), (128 * D, 16), (1, 256))
            self.dma("gpsimd", dstw, srcw, wdma[slot],
                     waits=[(wfree[slot], wfree[slot].val)])
            for j in range(NT):
                gsl = self.ar2b(GOT2_O[j % 2], 2048)
                g2ld = g2ld2[j % 2]
                self.dma("sync", gsl,
                         A_(self.dGOT, j * 128, (TOK, 128), (128 * TOK, 16),
                            (1, 128)), g2ld,
                         waits=[(gotd, gotd.val), (p2, p2.val),
                                (uvacc, max(0, uvacc.val - 1))])
                bi = j % 2
                pbk, prd, pfr = self.bank(bi)
                for k in range(16):
                    self.mm(pbk[:, 0:256], gsl[:, k * 128:(k + 1) * 128],
                            self.WB[slot][:, k * 256:(k + 1) * 256],
                            k == 0, k == 15,
                            waits=[(wdma[slot], wdma[slot].val),
                                   (g2ld, g2ld.val), (pfr, pfr.val)],
                            incs=([_inc(prd)] if k == 15 else []))
                xs = self.X[:, j * D + oh * 256:j * D + oh * 256 + 256]
                self.stt("vector", xs, pbk[:, 0:256], 1.0, xs, ALU.mult, ALU.add,
                         waits=[(prd, prd.val), (p2, p2.val)],
                         incs=[_inc(uvacc), _inc(pfr)])
            self.P.raw("vector", lambda e: e.nop(),
                       waits=[(uvacc, uvacc.val)], incs=[_inc(wfree[slot])])
        self.stream_base = self.c("stream").val
        self.stream_ready = self.c("stream")
        for j in range(NT):
            self.ln(j, uvacc, f"b3_{j}")

    def emit_l1ffn(self):
        wdma = [self.c("ws_d0"), self.c("ws_d1")]
        wfree = [self.c("wb_f0"), self.c("wb_f1")]
        browc = self.c("ld_init_g")
        self.dma("gpsimd", self.BROW[:, :], self.i_bias_rows[5:6, :], browc,
                 waits=[(self.c("a1_g"), self.c("a1_g").val)])
        qdone = self.c("f1_qT")
        XQ = self.ar2b(XQ_O, 2048)
        RND = self.ar2f(TOUT_O, 2048)
        for j in range(NT):
            e = self.c("f1_s")
            xs = self.X[:, j * D:(j + 1) * D]
            dumpb = self.WPQ[:, 0:1024]
            self.tt("vector", dumpb, xs[:, 0:1024], xs[:, 0:1024], ALU.mult,
                    waits=[(self.stream_ready, self.stream_ready.val),
                           (e, e.val)],
                    incs=[_inc(e)], accum=self.STAT[:, 16:17])
            self.tt("vector", dumpb, xs[:, 1024:2048], xs[:, 1024:2048], ALU.mult,
                    waits=[(e, e.val)], incs=[_inc(e)],
                    accum=self.STAT[:, 17:18])
            self.red(self.STAT[:, 18:19], xs, ALU.max, absval=True,
                     waits=[(e, e.val)], incs=[_inc(e)])
            self.tt("vector", self.STAT[:, 19:20], self.STAT[:, 16:17],
                    self.STAT[:, 17:18], ALU.add, waits=[(e, e.val)],
                    incs=[_inc(e)])
            self.ts("vector", self.STAT[:, 19:20], self.STAT[:, 19:20], 1.0 / D,
                    EPS_RMS, ALU.mult, ALU.add, waits=[(e, e.val)], incs=[_inc(e)])
            es = self.c("f1_ss")
            self.act(self.STAT[:, 20:21], self.STAT[:, 19:20], AF.Sqrt,
                     waits=[(e, e.val)], incs=[_inc(es)])
            self.recip(self.STAT[:, 21:22], self.STAT[:, 20:21],
                       waits=[(es, es.val)], incs=[_inc(e)])
            self.ts("vector", self.STAT[:, 22:23], self.STAT[:, 21:22], ISQ2048,
                    None, ALU.mult, waits=[(e, e.val)], incs=[_inc(e)])
            self.tt("vector", self.STAT[:, 23:24], self.STAT[:, 18:19],
                    self.STAT[:, 22:23], ALU.mult, waits=[(e, e.val)],
                    incs=[_inc(e)])
            self.ts("vector", self.STAT[:, 23:24], self.STAT[:, 23:24], EPS_Q,
                    None, ALU.max, waits=[(e, e.val)], incs=[_inc(e)])
            self.recip(self.STAT[:, 24:25], self.STAT[:, 23:24],
                       waits=[(e, e.val)], incs=[_inc(e)])
            self.ts("vector", self.STAT[:, 25:26], self.STAT[:, 24:25], 127.0,
                    None, ALU.mult, waits=[(e, e.val)], incs=[_inc(e)])
            self.tt("vector", self.STAT[:, 26:27], self.STAT[:, 25:26],
                    self.STAT[:, 22:23], ALU.mult, waits=[(e, e.val)],
                    incs=[_inc(e)])
            self.ts("vector", self.STAT[:, 27:28], self.STAT[:, 23:24],
                    1.0 / 127.0, None, ALU.mult, waits=[(e, e.val)],
                    incs=[_inc(e)])
            self.tt("vector", self.STAT[:, 27:28], self.STAT[:, 27:28],
                    self.CONSTF[:, 102:103], ALU.mult,
                    waits=[(e, e.val),
                           (self.wp1_scales, self.wp1_scales.val)],
                    incs=[_inc(e)])
            pb4, prd4, pfr4 = self.bank(4)
            self.mm(pb4[0:1, 0:128], self.STAT[:, 27:28], self.IDENTF[:, :],
                    True, True, waits=[(e, e.val), (pfr4, pfr4.val), self.ld],
                    incs=[_inc(prd4)])
            self.cp("vector", self.AROW[0:1, 768 + j * 128:768 + (j + 1) * 128],
                    pb4[0:1, 0:128], waits=[(prd4, prd4.val)],
                    incs=[_inc(e), _inc(pfr4)])
            self.ts("vector", RND, xs, self.STAT[:, 26:27], MAGIC, ALU.mult,
                    ALU.add, waits=[(e, e.val)], incs=[_inc(e)])
            self.ts("vector", XQ, RND, MAGIC, None, ALU.subtract,
                    waits=[(e, e.val)], incs=[_inc(e)])
            self.transpose_tile(self.xT, XQ, j, qdone, waits=[(e, e.val)])
        ns4 = self.c("stream")
        for tt in range(4):
            a1b = self.c("f1_a1b")
            pb4, prd4, pfr4 = self.bank(4)
            self.mm(pb4[:, 0:256], self.ONES1F[0:1, 0:128],
                    self.AROW[0:1, 768 + tt * 256:768 + tt * 256 + 256], True, True,
                    waits=[(qdone, qdone.val), (pfr4, pfr4.val), self.ones],
                    incs=[_inc(prd4)])
            A1B = self.ar2f(AGB_O, 256)
            self.cp("vector", A1B, pb4[:, 0:256], waits=[(prd4, prd4.val)],
                    incs=[_inc(a1b), _inc(pfr4)])
            hdone = self.c("f1_h")
            TMPF = self.ar2f(TMPF_O, 256)
            for og in range(32):
                slot = og % 2
                dstw = A_(self.WB[slot], 0, (4096, 128), (256, 16), (128, 2),
                          (1, 128))
                srcw = A_(self.g3ao, og * 256, (DI, 128), (128 * DI, 16), (128, 2),
                          (1, 128))
                self.dma("sync", dstw, srcw, wdma[slot],
                         waits=[(wfree[slot], wfree[slot].val),
                                (self.g23, 4)])
                wv_ = wdma[slot].val
                for oc in range(2):
                    och = og * 2 + oc
                    bi = och % 2
                    pbk, prd, pfr = self.bank(bi)
                    for k in range(16):
                        self.mm(pbk[:, 0:256],
                                self.WB[slot][:, (k * 2 + oc) * 128:
                                              (k * 2 + oc + 1) * 128],
                                self.xT[:, k * TOK + tt * 256:
                                        k * TOK + tt * 256 + 256],
                                k == 0, k == 15,
                                waits=[(wdma[slot], wv_), (qdone, qdone.val),
                                       (pfr, pfr.val)],
                                incs=([_inc(prd)] if k == 15 else []))
                    hv = self.c("f1_hv")
                    self.tt("vector", TMPF, pbk[:, 0:256], A1B, ALU.mult,
                            waits=[(prd, prd.val), (a1b, a1b.val),
                                   (hdone, hdone.val)],
                            incs=[_inc(hv), _inc(pfr)])
                    self.act(self.ar2b(Z_O + och * 256, 256), TMPF, AF.Gelu,
                             bias=self.CONSTF[:, 32 + och:33 + och],
                             waits=[(hv, hv.val), self.ld],
                             incs=[_inc(hdone)])
                self.P.raw("scalar", lambda e: e.nop(),
                           waits=[(hdone, hdone.val)], incs=[_inc(wfree[slot])])
            qd2 = self.quant_fm(list(range(64)), Z_O, 256, 64, "qf",
                                self.G1S[0:1, 12:13], (hdone, hdone.val),
                                c1_from_rms=True)
            dacc = self.c("f1_dacc")
            bcol = self.c("f1_bcol")
            for mh in range(2):
                pb4, prd4, pfr4 = self.bank(4)
                self.mm(pb4[:, mh:mh + 1],
                        self.AROW[0:1, 512 + mh * 128:512 + (mh + 1) * 128],
                        self.ONES1F[0:1, 0:1], True, True,
                        waits=[(qd2, qd2.val), (pfr4, pfr4.val), self.ones],
                        incs=[_inc(prd4)])
                self.cp("vector", self.STAT[:, 14 + mh:15 + mh], pb4[:, mh:mh + 1],
                        waits=[(prd4, prd4.val)], incs=[_inc(bcol), _inc(pfr4)])
            for og in range(4):
                for kl in range(32):
                    slot = kl % 2
                    dstw = A_(self.WB[slot], 0, (4096, 128), (512, 2), (1, 512))
                    srcw = A_(self.g3bo, kl * 256 * D + og * 512,
                              (D, 128), (128 * D, 2), (1, 512))
                    self.dma("sync", dstw, srcw, wdma[slot],
                             waits=[(wfree[slot], wfree[slot].val),
                                    (self.g23, 5)])
                    wv_ = wdma[slot].val
                    for kc2 in range(2):
                        kc = kl * 2 + kc2
                        for mh in range(2):
                            pbk, prd, pfr = self.bank(mh)
                            ii = ([_inc(wfree[slot])]
                                  if (kc2 == 1 and mh == 1) else [])
                            self.mm(pbk[:, :],
                                    self.ar2b(Z_O + kc * 256 + mh * 128, 128),
                                    self.WB[slot][:, kc2 * 512:kc2 * 512 + 512],
                                    kc == 0, False,
                                    waits=[(wdma[slot], wv_), (qd2, qd2.val),
                                           (pfr, pfr.val)],
                                    incs=ii)
                for mh in range(2):
                    j = tt * 2 + mh
                    pbk, prd, pfr = self.bank(mh)
                    self.mm(pbk[:, :], self.ONES1[0:1, 0:128],
                            self.BROW[0:1, og * 512:og * 512 + 512], False, True,
                            waits=[self.ones, (browc, browc.val)],
                            incs=[_inc(prd)])
                    xs = self.X[:, j * D + og * 512:j * D + og * 512 + 512]
                    self.stt("vector", xs, pbk[:, :], self.STAT[:, 14 + mh:15 + mh],
                             xs, ALU.mult, ALU.add,
                             waits=[(prd, prd.val), (bcol, bcol.val)]
                             + self.tap_gate(),
                             incs=[_inc(dacc), _inc(pfr)])
            if tt == 0:
                self.stream_base = ns4.val
            self.stream_ready = ns4
            for mh in range(2):
                self.ln(tt * 2 + mh, dacc, f"b4_{tt}_{mh}")
        self.stream_ready = ns4

    def emit_store(self):
        st = self.c("outdma")
        for j in range(NT):
            self.dma("sync", self.o_out[j * 128:(j + 1) * 128, :],
                     self.X[:, j * D:(j + 1) * D], st,
                     waits=[(self.stream_ready, self.stream_ready.val)])
        self.P.wait("sync", st)

    def build(self, upto=7):
        self.declare()
        self.phase_load()
        self.stream_ready = self.ld
        self.stream_base = 0
        if upto >= 2:
            self.phase_wprep1()
            self.emit_qkv()
        if upto >= 3:
            self.emit_g4()
        if upto >= 5:
            self.phase_wprep2()
        if upto >= 5:
            self.emit_g2g3_triggers()
        if upto >= 3:
            self.emit_attention()
        if upto >= 4:
            self.emit_wo()
            if TAPS:
                self.emit_tap(1)
        if upto >= 5:
            self.emit_l0ffn()
            if TAPS:
                self.emit_tap(2)
        if upto >= 6:
            self.emit_l1attn()
            if TAPS:
                self.emit_tap(3)
        if upto >= 7:
            self.emit_l1ffn()
        self.emit_store()
        self.P.run()


def build_program(upto=7):
    nc = bass.Bass("TRN2", target_bir_lowering=False, debug=False,
                   dynamic_dma_scratch_size=20480)
    mk = MK(nc)
    mk.build(upto=upto)
    return nc


def kernel(**inputs):
    in_maps, out_index = _host_prep(inputs)
    nc = build_program()
    res = run_bass_kernel_spmd(nc, in_maps, core_ids=list(range(8)))
    out = np.zeros((2, 4096, D), dtype=np.float32)
    for core in range(8):
        b, tok = out_index[core]
        out[b][tok] = res.results[core]["out"]
    return out
